# version 12
# baseline (speedup 1.0000x reference)
import math
import numpy as np
from contextlib import ExitStack

import concourse.bass as bass
import concourse.mybir as mybir
from concourse.bass_utils import run_bass_kernel_spmd

F32 = mybir.dt.float32
BF16 = mybir.dt.bfloat16
I32 = mybir.dt.int32
AF = mybir.ActivationFunctionType
ALU = mybir.AluOpType
AX = mybir.AxisListType

NCORE = 8
S = 8192
DM = 4096
TOWN = 1024
NQT = 16
SCALE = 1.0 / math.sqrt(128.0)
CSHIFT = 12.0
IC = 11
GROUPS = [11, 11, 11, 11, 11, 11, 10, 10]
G0S = [0, 11, 22, 33, 44, 55, 66, 76]
EW = 8576
EL = 8704
NSB = 16
IQ_CHUNKS = [3, 3, 3, 2]
IQ_STARTS = [0, 3, 6, 9]
BIGNEG = -6.6e8


class Sem:
    __slots__ = ("h", "n")

    def __init__(self, ctx, nc, name):
        self.h = ctx.enter_context(nc.semaphore(name))
        self.n = 0

    def inc(self, ins, k=1):
        ins.then_inc(self.h, k)
        self.n += k
        return ins


def build_program(dbg=(), upto=6):
    nc = bass.Bass("TRN2", target_bir_lowering=False)

    x_own = nc.dram_tensor("x_own", [TOWN, DM], F32, kind="ExternalInput").ap()
    x_sc = nc.dram_tensor("x_sc", [TOWN, DM], F32, kind="ExternalInput").ap()
    wqkv = nc.dram_tensor("wqkv", [DM, 768], F32, kind="ExternalInput").ap()
    wo = nc.dram_tensor("wo", [512, DM], F32, kind="ExternalInput").ap()
    slopes = nc.dram_tensor("slopes", [1, 4], F32, kind="ExternalInput").ap()
    g_qw = nc.dram_tensor("g_qw", [DM, 176], I32, kind="ExternalInput").ap()
    g_qz = nc.dram_tensor("g_qz", [32, 176], I32, kind="ExternalInput").ap()
    g_sc = nc.dram_tensor("g_sc", [32, 1408], F32, kind="ExternalInput").ap()
    u_qw = nc.dram_tensor("u_qw", [DM, 176], I32, kind="ExternalInput").ap()
    u_qz = nc.dram_tensor("u_qz", [32, 176], I32, kind="ExternalInput").ap()
    u_sc = nc.dram_tensor("u_sc", [32, 1408], F32, kind="ExternalInput").ap()
    d_qw = nc.dram_tensor("d_qw", [1408, 512], I32, kind="ExternalInput").ap()
    d_qz = nc.dram_tensor("d_qz", [IC, 512], I32, kind="ExternalInput").ap()
    d_sc = nc.dram_tensor("d_sc", [IC, DM], F32, kind="ExternalInput").ap()

    out_own = nc.dram_tensor("out_own", [TOWN, DM], F32, kind="ExternalOutput").ap()

    cc_xn = [nc.dram_tensor(f"cc_xn{i}", [DM, 512], BF16, kind="Internal").ap()
             for i in range(2)]
    xn_g = [nc.dram_tensor(f"xn_g{i}", [NCORE * DM, 512], BF16,
                           kind="Internal").ap() for i in range(2)]
    q_scr = nc.dram_tensor("q_scr", [512, S], BF16, kind="Internal").ap()
    o_scr = nc.dram_tensor("o_scr", [512, S], BF16, kind="Internal").ap()
    wo_bf = nc.dram_tensor("wo_bf", [512, DM], BF16, kind="Internal").ap()
    e_lin = nc.dram_tensor("e_lin", [4, EL], BF16, kind="Internal").ap()
    zf_g = nc.dram_tensor("zf_g", [32, 1408], I32, kind="Internal").ap()
    zf_u = nc.dram_tensor("zf_u", [32, 1408], I32, kind="Internal").ap()
    zf_d = nc.dram_tensor("zf_d", [IC, DM], I32, kind="Internal").ap()
    wg_dq = nc.dram_tensor("wg_dq", [32, 128, 1408], BF16, kind="Internal").ap()
    wu_dq = nc.dram_tensor("wu_dq", [32, 128, 1408], BF16, kind="Internal").ap()
    wd_dq = nc.dram_tensor("wd_dq", [IC, 128, DM], BF16, kind="Internal").ap()
    s_part = [nc.dram_tensor(f"s_part{r}", [1024, DM], F32, kind="Internal").ap()
              for r in range(8)]
    f_part = [nc.dram_tensor(f"f_part{r}", [1024, DM], F32, kind="Internal").ap()
              for r in range(8)]
    s_ownr = [nc.dram_tensor(f"s_ownr{r}", [128, DM], F32,
                             kind="Internal").ap() for r in range(8)]
    f_ownr = [nc.dram_tensor(f"f_ownr{r}", [128, DM], F32,
                             kind="Internal").ap() for r in range(8)]

    dbg_outs = {}
    dbg_srcs = {"xn_g0": xn_g[0], "xn_g1": xn_g[1], "q_scr": q_scr, "o_scr": o_scr,
                "e_lin": e_lin, "wg_dq": wg_dq, "wd_dq": wd_dq,
                "s_part0": s_part[0], "f_part0": f_part[0],
                "s_ownr0": s_ownr[0], "f_ownr0": f_ownr[0]}
    for name in dbg:
        src = dbg_srcs[name]
        dbg_outs[name] = nc.dram_tensor(
            "dbg_" + name, list(src.shape), src.dtype, kind="ExternalOutput").ap()

    groups = [list(range(NCORE))]

    NKB = [4 * (qt + 1) for qt in range(NQT)]
    KB_CUM = [0]
    PB_CUM = [0]
    for h in range(4):
        for qt in range(NQT):
            KB_CUM.append(KB_CUM[-1] + NKB[qt])
            PB_CUM.append(PB_CUM[-1] + NKB[qt] // 2)

    with ExitStack() as top:
        def sem(name):
            return Sem(top, nc, name)

        sPH = sem("sPH")
        tSELF = sem("tSELF"); aSELF = sem("aSELF"); vSELF = sem("vSELF")
        gSELF = sem("gSELF")
        sLX = sem("sLX"); sLX2 = sem("sLX2")
        aSQ = sem("aSQ"); vST = sem("vST"); aST = sem("aST"); aXN = sem("aXN")
        tTR = sem("tTR"); vTS = sem("vTS")
        sWXNp = [sem("sWXNa"), sem("sWXNb")]
        sLQW = sem("sLQW")
        sLDQz = sem("sLDQz"); vDQz = sem("vDQz")
        sLDQp = [sem("sLDQa"), sem("sLDQb")]
        sWDQp = [sem("sWDQa"), sem("sWDQb")]
        vDQs = sem("vDQs"); vDQm = sem("vDQm"); tDQ = sem("tDQ")
        sSF = sem("sSF"); vDQb = sem("vDQb"); sZS = sem("sZS")
        gEB = sem("gEB"); vEB = sem("vEB"); aEB = sem("aEB"); sEB = sem("sEB")
        cAGs = [sem("cAG0"), sem("cAG1")]
        sLXN4 = [sem(f"sLXN{i}") for i in range(4)]
        tQK = sem("tQK"); vEVQ = sem("vEVQ")
        tTRV = sem("tTRV"); vEVV = sem("vEVV"); sWQS = sem("sWQS")
        sEB2 = sem("sEB2"); sEB3 = sem("sEB3")
        sLQp = [sem("sLQa"), sem("sLQb")]
        tSC = sem("tSC"); aEX = sem("aEX"); vMK = sem("vMK")
        tPV = sem("tPV"); aOE = sem("aOE"); vRC = sem("vRC"); tLB = sem("tLB")
        vON = sem("vON"); sWOSp = [sem("sWOSa"), sem("sWOSb")]
        sLWOp = [sem("sLWOa"), sem("sLWOb")]
        sLORp = [sem("sLORa"), sem("sLORb")]
        tS4 = sem("tS4"); aSE = sem("aSE")
        sWSPp = [sem(f"sWSP{i}") for i in range(3)]
        cS = [sem(f"cS{r}") for r in range(8)]
        sLXF = sem("sLXF")
        sLWGp = [sem("sLWGa"), sem("sLWGb")]
        sLWDp = [sem("sLWDa"), sem("sLWDb")]
        tGU = sem("tGU"); aGU = sem("aGU"); vH = sem("vH")
        tDN = sem("tDN"); vFE = sem("vFE")
        sWFPp = [sem("sWFPa"), sem("sWFPb")]
        cF = [sem(f"cF{r}") for r in range(8)]
        sLSFp = [sem("sLSFa"), sem("sLSFb")]
        aG6 = sem("aG6"); vOUT = sem("vOUT")
        sWOUTp = [sem("sWOUTa"), sem("sWOUTb")]

        def sbt(ctx, name, shape, dtype, side="left"):
            return ctx.enter_context(nc.sbuf_tensor(name, shape, dtype, side=side))

        def pst(ctx, name, shape, dtype):
            return ctx.enter_context(nc.psum_tensor(name, shape, dtype))

        ones_rowf = sbt(top, "ones_rowf", [1, 128], F32)
        ones_colb = sbt(top, "ones_colb", [128, 1], BF16)
        bias_c = sbt(top, "bias_c", [128, 1], F32)
        eps_b = sbt(top, "eps_b", [128, 1], F32)
        ident = sbt(top, "ident", [128, 128], BF16)
        slopes_ng = sbt(top, "slopes_ng", [128, 4], F32)
        slopes_rep = sbt(top, "slopes_rep", [128, 4], F32)

        PH = [0]

        def phase_start(eng):
            if PH[0] > 0:
                eng.wait_ge(sPH.h, 5 * PH[0])

        def phase_end_q(eng):
            sPH.n += 1
            eng.nop().then_inc(sPH.h, 1)

        def phase_end_c(eng, last_sem):
            eng.wait_ge(last_sem.h, last_sem.n)
            sPH.n += 1
            eng.nop().then_inc(sPH.h, 1)

        def bwait(eng, sems, g, U):
            eng.wait_ge(sems[g % 2].h, U * (g // 2 + 1))

        blk = top.enter_context(nc.Block())

        with ExitStack() as scWQ:
            wqkv_sb = sbt(scWQ, "wqkv_sb", [128, 32 * 768], BF16, side="left")

            with ExitStack() as p0:
                x_t = sbt(p0, "x_t", [128, DM], F32, side="right")
                sq_t = sbt(p0, "sq_t", [128, DM], F32, side="right")
                msum = sbt(p0, "msum", [128, 8], F32, side="right")
                qsum = sbt(p0, "qsum", [128, 8], F32, side="right")
                mean8 = sbt(p0, "mean8", [128, 8], F32, side="right")
                eq8 = sbt(p0, "eq8", [128, 8], F32, side="right")
                m28 = sbt(p0, "m28", [128, 8], F32, side="right")
                var8 = sbt(p0, "var8", [128, 8], F32, side="right")
                std8 = sbt(p0, "std8", [128, 8], F32, side="right")
                rinv8 = sbt(p0, "rinv8", [128, 8], F32, side="right")
                bneg8 = sbt(p0, "bneg8", [128, 8], F32, side="right")
                tmp8 = sbt(p0, "tmp8", [128, 8], F32, side="right")
                xn_tm = sbt(p0, "xn_tm", [128, 2 * DM], BF16, side="right")
                xn_stage = sbt(p0, "xn_stage", [128, 2 * 512], BF16, side="right")
                qw_t = sbt(p0, "qw_t", [128, 2 * 512], I32, side="right")
                nib_t = sbt(p0, "nib_t", [128, 2048], I32, side="right")
                d1_t = sbt(p0, "d1_t", [128, 2048], F32, side="right")
                w_t = sbt(p0, "w_t", [128, 2 * 2048], BF16, side="right")
                gz_sb = sbt(p0, "gz_sb", [32, 176], I32, side="right")
                uz_sb = sbt(p0, "uz_sb", [32, 176], I32, side="right")
                dz_sb = sbt(p0, "dz_sb", [IC, 512], I32, side="right")
                zfg_sb = sbt(p0, "zfg_sb", [32, 1408], I32, side="right")
                zfu_sb = sbt(p0, "zfu_sb", [32, 1408], I32, side="right")
                zfd_sb = sbt(p0, "zfd_sb", [IC, DM], I32, side="right")
                zfg_bf = sbt(p0, "zfg_bf", [32, 1408], BF16, side="right")
                zfu_bf = sbt(p0, "zfu_bf", [32, 1408], BF16, side="right")
                zfd_bf = sbt(p0, "zfd_bf", [IC, DM], BF16, side="right")
                sfg_bf = sbt(p0, "sfg_bf", [32, 1408], BF16, side="right")
                sfu_bf = sbt(p0, "sfu_bf", [32, 1408], BF16, side="right")
                sfd_bf = sbt(p0, "sfd_bf", [IC, DM], BF16, side="right")
                ones_rowb = sbt(p0, "ones_rowb", [1, 128], BF16, side="right")
                z_stg = sbt(p0, "z_stg", [1, 2 * 1536], BF16, side="right")
                s_stg = sbt(p0, "s_stg", [1, 2 * 1536], BF16, side="right")
                ramp = sbt(p0, "ramp", [128, 68], F32, side="right")
                rmax = sbt(p0, "rmax", [128, 68], F32, side="right")
                rneg = sbt(p0, "rneg", [128, 68], F32, side="right")
                rmix = sbt(p0, "rmix", [128, 68], F32, side="right")
                sq_row = sbt(p0, "sq_row", [128, 68], F32, side="right")
                e_rowf = sbt(p0, "e_rowf", [128, 68], F32, side="right")
                e_rowb = sbt(p0, "e_rowb", [128, 68], BF16, side="right")
                ps_tr = pst(p0, "ps_tr", [128, 2 * 1024], BF16)
                ps_zb = pst(p0, "ps_zb", [128, 1536], F32)
                ps_sb5 = pst(p0, "ps_sb5", [128, 1536], F32)

                dq_jobs = []
                for g in range(32):
                    dq_jobs.append(("g", g, 176, 1408, 0))
                for g in range(32):
                    dq_jobs.append(("u", g, 176, 1408, 0))
                for it in range(IC):
                    for oh in range(4):
                        dq_jobs.append(("d", it, 128, 1024, oh))

                @blk.gpsimd
                def _(gpsimd):
                    for g in range(32):
                        sLQW.inc(gpsimd.dma_start(
                            out=wqkv_sb[:, 768 * g:768 * (g + 1)],
                            in_=wqkv[128 * g:128 * (g + 1), :]), 16)
                    sLQW.inc(gpsimd.dma_start(out=wo_bf, in_=wo), 16)
                    sSF.inc(gpsimd.dma_start(out=sfg_bf[:, :], in_=g_sc), 16)
                    sSF.inc(gpsimd.dma_start(out=sfu_bf[:, :], in_=u_sc), 16)
                    sSF.inc(gpsimd.dma_start(out=sfd_bf[:, :], in_=d_sc), 16)
                    gpsimd.memset(ident[:, :], 1.0)
                    gSELF.inc(gpsimd.iota(
                        ramp[:, :], [[1, 68]], base=-512, channel_multiplier=68,
                        allow_small_or_imprecise_dtypes=True))
                    gpsimd.wait_ge(gSELF.h, gSELF.n)
                    gEB.inc(gpsimd.affine_select(
                        out=ident[:, :], in_=ident[:, :],
                        compare_op=ALU.is_equal, fill=0.0, base=0,
                        pattern=[[-1, 128]], channel_multiplier=1))
                    for tb in range(8):
                        for fg in range(8):
                            idx = tb * 8 + fg
                            gpsimd.wait_ge(vTS.h, idx + 1)
                            for j in range(4):
                                fb = 4 * fg + j
                                sWXNp[idx % 2].inc(gpsimd.dma_start(
                                    out=cc_xn[tb // 4][
                                        128 * fb:128 * (fb + 1),
                                        128 * (tb % 4):128 * (tb % 4 + 1)],
                                    in_=xn_stage[:, 512 * (idx % 2) + 128 * j:
                                                 512 * (idx % 2) + 128 * (j + 1)]),
                                    16)
                    gpsimd.wait_ge(sWXNp[0].h, 64 * 16)
                    gpsimd.wait_ge(sWXNp[1].h, 64 * 16)
                    cAGs[0].inc(gpsimd.collective_compute(
                        "AllGather", ALU.bypass, replica_groups=groups,
                        ins=[cc_xn[0].opt()], outs=[xn_g[0].opt()]))
                    gpsimd.wait_ge(sWXNp[0].h, 64 * 32)
                    gpsimd.wait_ge(sWXNp[1].h, 64 * 32)
                    cAGs[1].inc(gpsimd.collective_compute(
                        "AllGather", ALU.bypass, replica_groups=groups,
                        ins=[cc_xn[1].opt()], outs=[xn_g[1].opt()]))
                    gpsimd.wait_ge(sLQW.h, sLQW.n)
                    phase_end_q(gpsimd)

                @blk.sync
                def _(sync):
                    sLDQz.inc(sync.dma_start(out=slopes_rep[:, :],
                                             in_=slopes.broadcast_to([128, 4])), 16)
                    for tb in range(8):
                        if tb >= 1:
                            sync.wait_ge(vST.h, tb)
                            sync.wait_ge(aSQ.h, tb)
                        sLX.inc(sync.dma_start(
                            out=x_t[:, :],
                            in_=x_own[128 * tb:128 * (tb + 1), :]), 16)
                    for tb in range(8):
                        sync.wait_ge(vST.h, 8)
                        sync.wait_ge(aXN.h, tb)
                        sLX2.inc(sync.dma_start(
                            out=x_t[:, :],
                            in_=x_own[128 * tb:128 * (tb + 1), :]), 16)
                    sLDQz.inc(sync.dma_start(out=gz_sb[:, :], in_=g_qz), 16)
                    sLDQz.inc(sync.dma_start(out=uz_sb[:, :], in_=u_qz), 16)
                    sLDQz.inc(sync.dma_start(out=dz_sb[:, :], in_=d_qz), 16)

                    sync.wait_ge(vDQz.h, 3)
                    sync.wait_ge(sSF.h, 48)
                    for k, (tag, i0, nw, wout, oh) in enumerate(dq_jobs):
                        if k >= 2:
                            sync.wait_ge(vDQs.h, k - 1)
                        b = k % 2
                        if tag == "g":
                            qsrc = g_qw[128 * i0:128 * (i0 + 1), :]
                            zrow, srow, off = zfg_bf, sfg_bf, 0
                        elif tag == "u":
                            qsrc = u_qw[128 * i0:128 * (i0 + 1), :]
                            zrow, srow, off = zfu_bf, sfu_bf, 0
                        else:
                            qsrc = d_qw[128 * i0:128 * (i0 + 1),
                                        128 * oh:128 * (oh + 1)]
                            zrow, srow, off = zfd_bf, sfd_bf, 1024 * oh
                        if k >= 2:
                            sync.wait_ge(tDQ.h, k - 1)
                        sZS.inc(sync.dma_start(
                            out=z_stg[0:1, 1536 * b:1536 * b + wout],
                            in_=zrow[i0:i0 + 1, off:off + wout]), 16)
                        sZS.inc(sync.dma_start(
                            out=s_stg[0:1, 1536 * b:1536 * b + wout],
                            in_=srow[i0:i0 + 1, off:off + wout]), 16)
                        sLDQp[b].inc(sync.dma_start(
                            out=qw_t[:, 512 * b:512 * b + nw], in_=qsrc), 16)
                        sync.wait_ge(vDQm.h, k + 1)
                        if tag == "g":
                            dst = bass.AP(wg_dq.tensor, i0 * 128 * 1408,
                                          [[1408, 128], [1, 1408]])
                        elif tag == "u":
                            dst = bass.AP(wu_dq.tensor, i0 * 128 * 1408,
                                          [[1408, 128], [1, 1408]])
                        else:
                            dst = bass.AP(wd_dq.tensor,
                                          i0 * 128 * DM + 1024 * oh,
                                          [[DM, 128], [1, 1024]])
                        sWDQp[b].inc(sync.dma_start(
                            out=dst, in_=w_t[:, 2048 * b:2048 * b + wout]), 16)
                    for h in range(4):
                        sync.wait_ge(vEB.h, 2 + h + 1)
                        sEB.inc(sync.dma_start(
                            out=bass.AP(e_lin.tensor, h * EL,
                                        [[68, 128], [1, 68]]),
                            in_=e_rowb[:, :]), 16)
                    for s_ in (sLX, sLX2, sLDQz, sLDQp[0], sLDQp[1], sZS,
                               sWDQp[0], sWDQp[1], sEB):
                        sync.wait_ge(s_.h, s_.n)
                    phase_end_q(sync)

                @blk.tensor
                def _(tensor):
                    tensor.wait_ge(gEB.h, 1)
                    last = None
                    for tb in range(8):
                        tensor.wait_ge(aXN.h, tb + 1)
                        for fg in range(8):
                            idx = tb * 8 + fg
                            if idx >= 2:
                                tensor.wait_ge(vTS.h, idx - 1)
                            for j in range(4):
                                fb = 4 * fg + j
                                last = tensor.transpose(
                                    out=ps_tr[:, 1024 * (idx % 2) + 128 * j:
                                              1024 * (idx % 2) + 128 * (j + 1)],
                                    in_=xn_tm[:, DM * (tb % 2) + 128 * fb:
                                              DM * (tb % 2) + 128 * (fb + 1)],
                                    identity=ident[:, :])
                            tTR.inc(last)
                    for k, (tag, i0, nw, wout, oh) in enumerate(dq_jobs):
                        b = k % 2
                        tensor.wait_ge(sZS.h, 32 * (k + 1))
                        if k >= 1:
                            tensor.wait_ge(vDQb.h, k)
                            tensor.wait_ge(vDQm.h, k)
                        last = None
                        for c0 in range(0, wout, 512):
                            w = min(512, wout - c0)
                            tensor.matmul(
                                out=ps_zb[:, c0:c0 + w],
                                lhsT=ones_rowb[:, :],
                                rhs=z_stg[0:1, 1536 * b + c0:1536 * b + c0 + w],
                                start=True, stop=True, skip_group_check=True)
                            last = tensor.matmul(
                                out=ps_sb5[:, c0:c0 + w],
                                lhsT=ones_rowb[:, :],
                                rhs=s_stg[0:1, 1536 * b + c0:1536 * b + c0 + w],
                                start=True, stop=True, skip_group_check=True)
                        tDQ.inc(last)
                    phase_end_c(tensor, tDQ)

                @blk.scalar
                def _(scalar):
                    scalar.preload_act_table(AF.Exp)
                    for tb in range(8):
                        scalar.wait_ge(sLX.h, 16 * (tb + 1))
                        if tb >= 1:
                            scalar.wait_ge(vST.h, tb)
                        aSQ.inc(scalar.activation(out=sq_t[:, :], in_=x_t[:, :],
                                                  func=AF.Square))
                    scalar.wait_ge(vSELF.h, 4)
                    scalar.preload_act_table(AF.Sqrt)
                    aST.inc(scalar.activation(out=std8[:, :], in_=var8[:, :],
                                              func=AF.Sqrt, bias=eps_b[:, :]))
                    scalar.preload_act_table(AF.Exp)
                    for tb in range(8):
                        scalar.wait_ge(sLX2.h, 16 * (tb + 1))
                        scalar.wait_ge(vSELF.h, 7)
                        if tb >= 2:
                            scalar.wait_ge(tTR.h, 8 * (tb - 1))
                        aXN.inc(scalar.activation(
                            out=xn_tm[:, DM * (tb % 2):DM * (tb % 2 + 1)],
                            in_=x_t[:, :],
                            func=AF.Identity, scale=rinv8[:, tb:tb + 1],
                            bias=bneg8[:, tb:tb + 1]))
                    scalar.wait_ge(vEB.h, 1)
                    scalar.preload_act_table(AF.Sqrt)
                    aSELF.inc(scalar.activation(out=sq_row[:, :], in_=rmix[:, :],
                                                func=AF.Sqrt))
                    scalar.wait_ge(aSELF.h, aSELF.n)
                    scalar.preload_act_table(AF.Exp)
                    last = None
                    for h in range(4):
                        scalar.wait_ge(vEB.h, 2 + h)
                        last = scalar.activation(
                            out=e_rowf[:, :],
                            in_=sq_row[:, :], func=AF.Exp,
                            scale=slopes_ng[:, h:h + 1])
                        aEB.inc(last)
                    phase_end_c(scalar, aEB)

                @blk.vector
                def _(vector):
                    vector.memset(ones_rowf[:, :], 1.0)
                    vector.memset(ones_colb[:, :], 1.0)
                    vector.memset(bias_c[:, :], -CSHIFT)
                    vector.memset(eps_b[:, :], 1e-5)
                    for tb in range(8):
                        vector.wait_ge(sLX.h, 16 * (tb + 1))
                        vector.wait_ge(aSQ.h, tb + 1)
                        vector.reduce_sum(out=msum[:, tb:tb + 1], in_=x_t[:, :],
                                          axis=AX.X)
                        vST.inc(vector.reduce_sum(out=qsum[:, tb:tb + 1],
                                                  in_=sq_t[:, :], axis=AX.X))
                    vector.wait_ge(vST.h, 8)
                    vSELF.inc(vector.tensor_scalar_mul(
                        out=mean8[:, :], in0=msum[:, :], scalar1=1.0 / DM))
                    vSELF.inc(vector.tensor_scalar_mul(
                        out=eq8[:, :], in0=qsum[:, :], scalar1=1.0 / DM))
                    vector.wait_ge(vSELF.h, 2)
                    vSELF.inc(vector.tensor_mul(out=m28[:, :], in0=mean8[:, :],
                                                in1=mean8[:, :]))
                    vector.wait_ge(vSELF.h, 3)
                    vSELF.inc(vector.tensor_sub(out=var8[:, :], in0=eq8[:, :],
                                                in1=m28[:, :]))
                    vector.wait_ge(aST.h, 1)
                    vSELF.inc(vector.reciprocal(out=rinv8[:, :], in_=std8[:, :]))
                    vector.wait_ge(vSELF.h, 5)
                    vSELF.inc(vector.tensor_mul(out=tmp8[:, :], in0=mean8[:, :],
                                                in1=rinv8[:, :]))
                    vector.wait_ge(vSELF.h, 6)
                    vSELF.inc(vector.tensor_scalar_mul(
                        out=bneg8[:, :], in0=tmp8[:, :], scalar1=-1.0))
                    for tb in range(8):
                        for fg in range(8):
                            idx = tb * 8 + fg
                            vector.wait_ge(tTR.h, idx + 1)
                            if idx >= 2:
                                bwait(vector, sWXNp, idx - 2, 64)
                            vTS.inc(vector.tensor_copy(
                                out=xn_stage[:, 512 * (idx % 2):
                                             512 * (idx % 2 + 1)],
                                in_=ps_tr[:, 1024 * (idx % 2):
                                          1024 * (idx % 2) + 512]))
                    vector.memset(ones_rowb[:, :], 1.0)
                    vector.wait_ge(sLDQz.h, 16 * 4)
                    for zsrc, dstz, dstzb, wtot in (
                            (gz_sb, zfg_sb, zfg_bf, 1408),
                            (uz_sb, zfu_sb, zfu_bf, 1408),
                            (dz_sb, zfd_sb, zfd_bf, DM)):
                        last = None
                        for j in range(8):
                            last = vector.tensor_scalar(
                                out=dstz[:, j:wtot:8], in0=zsrc[:, :],
                                scalar1=4 * j, scalar2=15,
                                op0=ALU.logical_shift_right,
                                op1=ALU.bitwise_and)
                        vSELF.inc(last)
                        vector.wait_ge(vSELF.h, vSELF.n)
                        vDQz.inc(vector.tensor_copy(out=dstzb[:, :],
                                                    in_=dstz[:, :]))
                    for k, (tag, i0, nw, wout, oh) in enumerate(dq_jobs):
                        b = k % 2
                        bwait(vector, sLDQp, k, 16)
                        if k >= 1:
                            vector.wait_ge(vDQm.h, k)
                        last = None
                        for j in range(8):
                            last = vector.tensor_scalar(
                                out=nib_t[:, j:wout:8],
                                in0=qw_t[:, 512 * b:512 * b + nw],
                                scalar1=4 * j, scalar2=15,
                                op0=ALU.logical_shift_right,
                                op1=ALU.bitwise_and)
                        vDQs.inc(last)
                        vector.wait_ge(vDQs.h, k + 1)
                        vector.wait_ge(tDQ.h, k + 1)
                        vDQb.inc(vector.tensor_sub(
                            out=d1_t[:, 0:wout], in0=nib_t[:, 0:wout],
                            in1=ps_zb[:, 0:wout]))
                        vector.wait_ge(vDQb.h, vDQb.n)
                        if k >= 2:
                            bwait(vector, sWDQp, k - 2, 16)
                        vDQm.inc(vector.tensor_mul(
                            out=w_t[:, 2048 * b:2048 * b + wout],
                            in0=d1_t[:, 0:wout],
                            in1=ps_sb5[:, 0:wout]))
                    vector.wait_ge(gEB.h, 1)
                    vector.tensor_scalar_max(out=rmax[:, :], in0=ramp[:, :],
                                             scalar1=0.0)
                    vSELF.inc(vector.tensor_scalar_min(out=rneg[:, :],
                                                       in0=ramp[:, :],
                                                       scalar1=0.0))
                    vector.wait_ge(vSELF.h, vSELF.n)
                    vEB.inc(vector.scalar_tensor_tensor(
                        out=rmix[:, :], in0=rneg[:, :], scalar=BIGNEG,
                        in1=rmax[:, :], op0=ALU.mult, op1=ALU.add))
                    vector.wait_ge(sLDQz.h, 16 * 4)
                    vEB.inc(vector.tensor_scalar_mul(
                        out=slopes_ng[:, :], in0=slopes_rep[:, :],
                        scalar1=-1.0))
                    last = None
                    for h in range(4):
                        vector.wait_ge(aEB.h, h + 1)
                        if h >= 1:
                            vector.wait_ge(sEB.h, 16 * h)
                        last = vector.tensor_copy(
                            out=e_rowb[:, :], in_=e_rowf[:, :])
                        vEB.inc(last)
                    phase_end_c(vector, vEB)

            PH[0] += 1
            assert sPH.n == 5 * PH[0], sPH.n

            with ExitStack() as scKV:
                k_fm = sbt(scKV, "k_fm", [128, S], BF16, side="right")
                v_tm = sbt(scKV, "v_tm", [128, S], BF16, side="right")
                e_sb = sbt(scKV, "e_sb", [128, 4 * EW], BF16, side="right")

                with ExitStack() as p2:
                    xn_t2 = sbt(p2, "xn_t2", [128, 4 * 512], BF16, side="right")
                    q_stage = sbt(p2, "q_stage", [128, 4 * 512], BF16, side="right")
                    v_stage = sbt(p2, "v_stage", [128, 512], BF16, side="right")
                    ps_qkv = pst(p2, "ps_qkv", [128, 6 * 512], F32)
                    ps_vt = pst(p2, "ps_vt", [128, 512], BF16)

                    @blk.sync
                    def _(sync):
                        phase_start(sync)
                        for tt in range(NQT):
                            sync.wait_ge(cAGs[tt % 2].h, 1)
                            for g in range(32):
                                gi = 32 * tt + g
                                if gi % 2 != 0:
                                    continue
                                if gi >= 4:
                                    sync.wait_ge(tQK.h, gi - 3)
                                sLXN4[gi % 4].inc(sync.dma_start(
                                    out=xn_t2[:, 512 * (gi % 4):512 * (gi % 4 + 1)],
                                    in_=xn_g[tt % 2][
                                        DM * (tt // 2) + 128 * g:
                                        DM * (tt // 2) + 128 * (g + 1), :]), 16)
                                r = gi // 2
                                h_, i_ = divmod(r, 128)
                                sEB2.inc(sync.dma_start(
                                    out=e_sb[i_:i_ + 1, EW * h_:EW * (h_ + 1)],
                                    in_=bass.AP(e_lin.tensor, h_ * EL + 128 - i_,
                                                [[1, 1], [1, EW]])), 16)
                            sync.wait_ge(vEVQ.h, tt + 1)
                            for f in range(4):
                                sWQS.inc(sync.dma_start(
                                    out=q_scr[128 * f:128 * (f + 1),
                                              512 * tt:512 * (tt + 1)],
                                    in_=q_stage[:, 512 * f:512 * (f + 1)]), 16)
                        for s_ in (sLXN4[0], sLXN4[2], sWQS, sEB2):
                            sync.wait_ge(s_.h, s_.n)
                        phase_end_q(sync)

                    @blk.gpsimd
                    def _(gpsimd):
                        phase_start(gpsimd)
                        for tt in range(NQT):
                            gpsimd.wait_ge(cAGs[tt % 2].h, 1)
                            for g in range(32):
                                gi = 32 * tt + g
                                if gi % 2 != 1:
                                    continue
                                if gi >= 4:
                                    gpsimd.wait_ge(tQK.h, gi - 3)
                                sLXN4[gi % 4].inc(gpsimd.dma_start(
                                    out=xn_t2[:, 512 * (gi % 4):512 * (gi % 4 + 1)],
                                    in_=xn_g[tt % 2][
                                        DM * (tt // 2) + 128 * g:
                                        DM * (tt // 2) + 128 * (g + 1), :]), 16)
                                r = 256 + gi // 2
                                h_, i_ = divmod(r, 128)
                                sEB3.inc(gpsimd.dma_start(
                                    out=e_sb[i_:i_ + 1, EW * h_:EW * (h_ + 1)],
                                    in_=bass.AP(e_lin.tensor, h_ * EL + 128 - i_,
                                                [[1, 1], [1, EW]])), 16)
                        gpsimd.wait_ge(sEB3.h, sEB3.n)
                        gpsimd.wait_ge(sLXN4[1].h, sLXN4[1].n)
                        gpsimd.wait_ge(sLXN4[3].h, sLXN4[3].n)
                        phase_end_q(gpsimd)

                    @blk.tensor
                    def _(tensor):
                        phase_start(tensor)
                        last = None
                        for tt in range(NQT):
                            for g in range(32):
                                gi = 32 * tt + g
                                tensor.wait_ge(sLXN4[gi % 4].h, 16 * (gi // 4 + 1))
                                if g == 0 and tt >= 1:
                                    tensor.wait_ge(vEVQ.h, tt)
                                for f in range(6):
                                    last = tensor.matmul(
                                        out=ps_qkv[:, 512 * f:512 * (f + 1)],
                                        lhsT=wqkv_sb[:, 768 * g + 128 * f:
                                                     768 * g + 128 * (f + 1)],
                                        rhs=xn_t2[:, 512 * (gi % 4):
                                                  512 * (gi % 4 + 1)],
                                        start=(g == 0), stop=(g == 31),
                                        skip_group_check=True)
                                tQK.inc(last)
                            tensor.wait_ge(vEVQ.h, tt + 1)
                            if tt >= 1:
                                tensor.wait_ge(vEVV.h, tt)
                            for j in range(4):
                                last = tensor.transpose(
                                    out=ps_vt[:, 128 * j:128 * (j + 1)],
                                    in_=v_stage[:, 128 * j:128 * (j + 1)],
                                    identity=ident[:, :])
                            tTRV.inc(last)
                        phase_end_c(tensor, tTRV)

                    @blk.scalar
                    def _(scalar):
                        phase_start(scalar)
                        phase_end_q(scalar)

                    @blk.vector
                    def _(vector):
                        phase_start(vector)
                        last = None
                        for tt in range(NQT):
                            vector.wait_ge(tQK.h, 32 * (tt + 1))
                            if tt >= 1:
                                vector.wait_ge(sWQS.h, 16 * 4 * tt)
                            for f in range(4):
                                vector.tensor_copy(
                                    out=q_stage[:, 512 * f:512 * (f + 1)],
                                    in_=ps_qkv[:, 512 * f:512 * (f + 1)])
                            vector.tensor_copy(
                                out=k_fm[:, 512 * tt:512 * (tt + 1)],
                                in_=ps_qkv[:, 512 * 4:512 * 5])
                            vEVQ.inc(vector.tensor_copy(
                                out=v_stage[:, :], in_=ps_qkv[:, 512 * 5:512 * 6]))
                            vector.wait_ge(tTRV.h, tt + 1)
                            last = vector.tensor_copy(
                                out=v_tm[:, 512 * tt:512 * (tt + 1)],
                                in_=ps_vt[:, :])
                            vEVV.inc(last)
                        phase_end_c(vector, vEVV)

                PH[0] += 1
                assert sPH.n == 5 * PH[0], sPH.n

                if True:
                    with ExitStack() as p3:
                        q_t = sbt(p3, "q_t", [128, 2 * 512], BF16, side="right")
                        p_sb = sbt(p3, "p_sb", [128, 4 * 512], BF16, side="right")
                        pm_sb = sbt(p3, "pm_sb", [128, 4 * 512], BF16, side="right")
                        o_raw = sbt(p3, "o_raw", [128, 2 * 512], F32, side="right")
                        o_stage = sbt(p3, "o_stage", [128, 2 * 512], BF16,
                                      side="right")
                        linv = sbt(p3, "linv", [1, 512], F32, side="right")
                        ps_sc = pst(p3, "ps_sc", [128, 4 * 512], F32)
                        ps_po = pst(p3, "ps_po", [128, 2 * 512], F32)
                        ps_pl = pst(p3, "ps_pl", [1, 512], F32)
                        ps_plb = pst(p3, "ps_plb", [128, 512], F32)

                        @blk.sync
                        def _(sync):
                            phase_start(sync)
                            def o_store(sidx):
                                hh, qq = divmod(sidx, NQT)
                                sync.wait_ge(vON.h, sidx + 1)
                                sWOSp[sidx % 2].inc(sync.dma_start(
                                    out=o_scr[128 * hh:128 * (hh + 1),
                                              512 * qq:512 * (qq + 1)],
                                    in_=o_stage[:, 512 * (sidx % 2):
                                                512 * (sidx % 2 + 1)]), 16)

                            for idx in range(64):
                                if idx >= 2:
                                    sync.wait_ge(tSC.h, PB_CUM[idx - 1])
                                h, qt = divmod(idx, NQT)
                                sLQp[idx % 2].inc(sync.dma_start(
                                    out=q_t[:, 512 * (idx % 2):512 * (idx % 2 + 1)],
                                    in_=q_scr[128 * h:128 * (h + 1),
                                              512 * qt:512 * (qt + 1)]), 16)
                                if idx >= 2:
                                    o_store(idx - 2)
                            o_store(62)
                            o_store(63)
                            for s_ in (sLQp[0], sLQp[1], sWOSp[0], sWOSp[1]):
                                sync.wait_ge(s_.h, s_.n)
                            phase_end_q(sync)

                        @blk.gpsimd
                        def _(gpsimd):
                            phase_start(gpsimd)
                            phase_end_q(gpsimd)

                        @blk.tensor
                        def _(tensor):
                            phase_start(tensor)
                            last = None
                            for idx in range(64):
                                h, qt = divmod(idx, NQT)
                                n = NKB[qt]
                                K0 = KB_CUM[idx]
                                bwait(tensor, sLQp, idx, 16)
                                if idx >= 1:
                                    tensor.wait_ge(aOE.h, idx - 1)
                                    tensor.wait_ge(vRC.h, idx)
                                qs = 512 * (idx % 2)

                                np_ = n // 2
                                P0_ = PB_CUM[idx]

                                def score_pair(p):
                                    pp = P0_ + p
                                    sp = pp % 2
                                    if pp >= 2:
                                        tensor.wait_ge(aEX.h, pp - 1)
                                    for u in range(2):
                                        j = 2 * p + u
                                        ins = tensor.matmul(
                                            out=ps_sc[:, 1024 * sp + 512 * u:
                                                      1024 * sp + 512 * (u + 1)],
                                            lhsT=k_fm[:, 128 * j:128 * (j + 1)],
                                            rhs=q_t[:, qs:qs + 512],
                                            start=True, stop=True,
                                            skip_group_check=True)
                                    tSC.inc(ins)

                                score_pair(0)
                                if np_ > 1:
                                    score_pair(1)
                                for p in range(np_):
                                    pp = P0_ + p
                                    sp = pp % 2
                                    if p + 2 < np_:
                                        score_pair(p + 2)
                                    tensor.wait_ge(vMK.h, pp + 1)
                                    for u in range(2):
                                        j = 2 * p + u
                                        kk = K0 + j
                                        tensor.matmul(
                                            out=ps_po[:, 512 * (idx % 2):
                                                      512 * (idx % 2 + 1)],
                                            lhsT=v_tm[:, 128 * j:128 * (j + 1)],
                                            rhs=pm_sb[:, 1024 * sp + 512 * u:
                                                      1024 * sp + 512 * (u + 1)],
                                            start=(j == 0), stop=(j == n - 1),
                                            skip_group_check=True)
                                        last = tensor.matmul(
                                            out=ps_pl[0:1, :],
                                            lhsT=ones_colb[:, :],
                                            rhs=pm_sb[:, 1024 * sp + 512 * u:
                                                      1024 * sp + 512 * (u + 1)],
                                            start=(j == 0), stop=(j == n - 1),
                                            skip_group_check=True)
                                        tPV.inc(last)
                                tensor.wait_ge(vRC.h, idx + 1)
                                if idx >= 1:
                                    tensor.wait_ge(vON.h, idx)
                                last = tensor.matmul(
                                    out=ps_plb[:, :],
                                    lhsT=ones_rowf[:, :], rhs=linv[0:1, :],
                                    start=True, stop=True,
                                    skip_group_check=True)
                                tLB.inc(last)
                            phase_end_c(tensor, tLB)

                        @blk.scalar
                        def _(scalar):
                            phase_start(scalar)
                            last = None
                            for idx in range(64):
                                h, qt = divmod(idx, NQT)
                                n = NKB[qt]
                                K0 = KB_CUM[idx]
                                np_ = n // 2
                                P0_ = PB_CUM[idx]
                                for p in range(np_):
                                    pp = P0_ + p
                                    sp = pp % 2
                                    scalar.wait_ge(tSC.h, pp + 1)
                                    if pp >= 2:
                                        scalar.wait_ge(vMK.h, pp - 1)
                                    aEX.inc(scalar.activation(
                                        out=p_sb[:, 1024 * sp:1024 * (sp + 1)],
                                        in_=ps_sc[:, 1024 * sp:1024 * (sp + 1)],
                                        func=AF.Exp, scale=SCALE,
                                        bias=bias_c[:, :]))
                                scalar.wait_ge(tPV.h, K0 + n)
                                if idx >= 2:
                                    scalar.wait_ge(vON.h, idx - 1)
                                last = scalar.activation(
                                    out=o_raw[:, 512 * (idx % 2):
                                              512 * (idx % 2 + 1)],
                                    in_=ps_po[:, 512 * (idx % 2):
                                              512 * (idx % 2 + 1)],
                                    func=AF.Copy)
                                aOE.inc(last)
                            phase_end_c(scalar, aOE)

                        @blk.vector
                        def _(vector):
                            phase_start(vector)
                            last = None
                            for idx in range(64):
                                h, qt = divmod(idx, NQT)
                                n = NKB[qt]
                                K0 = KB_CUM[idx]
                                np_ = n // 2
                                P0_ = PB_CUM[idx]
                                for p in range(np_):
                                    pp = P0_ + p
                                    sp = pp % 2
                                    vector.wait_ge(aEX.h, pp + 1)
                                    if pp >= 2:
                                        vector.wait_ge(tPV.h, 2 * (pp - 1))
                                    ins = None
                                    for u in range(2):
                                        j = 2 * p + u
                                        u0 = 512 * qt - 128 * j + 384
                                        ins = vector.tensor_mul(
                                            out=pm_sb[:, 1024 * sp + 512 * u:
                                                      1024 * sp + 512 * (u + 1)],
                                            in0=p_sb[:, 1024 * sp + 512 * u:
                                                     1024 * sp + 512 * (u + 1)],
                                            in1=e_sb[:, EW * h + u0:
                                                     EW * h + u0 + 512])
                                    vMK.inc(ins)
                                vector.wait_ge(tPV.h, K0 + n)
                                vector.wait_ge(tLB.h, idx)
                                vRC.inc(vector.reciprocal(out=linv[0:1, :],
                                                          in_=ps_pl[0:1, :]))
                                vector.wait_ge(tLB.h, idx + 1)
                                vector.wait_ge(aOE.h, idx + 1)
                                if idx >= 2:
                                    bwait(vector, sWOSp, idx - 2, 16)
                                last = vector.tensor_mul(
                                    out=o_stage[:, 512 * (idx % 2):
                                                512 * (idx % 2 + 1)],
                                    in0=o_raw[:, 512 * (idx % 2):
                                              512 * (idx % 2 + 1)],
                                    in1=ps_plb[:, :])
                                vON.inc(last)
                            phase_end_c(vector, vON)

                    PH[0] += 1
                    assert sPH.n == 5 * PH[0], sPH.n

                    with ExitStack() as p4:
                        wo_sb = sbt(p4, "wo_sb", [128, 2 * 4 * 512], BF16,
                                    side="right")
                        o_rc = sbt(p4, "o_rc", [128, 2 * 4 * 1024], BF16,
                                   side="right")
                        s_stage = sbt(p4, "s_stage", [128, 3 * 512], F32,
                                      side="right")
                        ps_s = pst(p4, "ps_s", [128, 3 * 512], F32)

                        @blk.gpsimd
                        def _(gpsimd):
                            phase_start(gpsimd)
                            phase_end_q(gpsimd)

                        @blk.sync
                        def _(sync):
                            phase_start(sync)
                            for rc in range(8):
                                if rc >= 2:
                                    sync.wait_ge(tS4.h, 64 * (rc - 1))
                                for h in range(4):
                                    sLORp[rc % 2].inc(sync.dma_start(
                                        out=o_rc[:, 4096 * (rc % 2) + 1024 * h:
                                                 4096 * (rc % 2)
                                                 + 1024 * (h + 1)],
                                        in_=o_scr[128 * h:128 * (h + 1),
                                                  1024 * rc:1024 * (rc + 1)]), 16)
                                for r1 in range(64):
                                    gtb = 64 * rc + r1
                                    dm8, tb = divmod(r1, 8)
                                    sync.wait_ge(aSE.h, gtb + 1)
                                    if gtb >= 3:
                                        sync.wait_ge(sWSPp[gtb % 3].h,
                                                     16 * ((gtb - 3) // 3 + 1))
                                    sWSPp[gtb % 3].inc(sync.dma_start(
                                        out=s_part[rc][128 * tb:128 * (tb + 1),
                                                       512 * dm8:512 * (dm8 + 1)],
                                        in_=s_stage[:, 512 * (gtb % 3):
                                                    512 * (gtb % 3 + 1)]), 16)
                            for s_ in (sLORp[0], sLORp[1], sWSPp[0], sWSPp[1],
                                       sWSPp[2]):
                                sync.wait_ge(s_.h, s_.n)
                            phase_end_q(sync)

                        @blk.tensor
                        def _(tensor):
                            phase_start(tensor)
                            last = None
                            for rc in range(8):
                                bwait(tensor, sLORp, rc, 64)
                                for dm8 in range(8):
                                    gi = 8 * rc + dm8
                                    bwait(tensor, sLWOp, gi, 64)
                                    for tb in range(8):
                                        gtb = 8 * gi + tb
                                        if gtb >= 3:
                                            tensor.wait_ge(aSE.h, gtb - 2)
                                        for h in range(4):
                                            last = tensor.matmul(
                                                out=ps_s[:, 512 * (gtb % 3):
                                                         512 * (gtb % 3 + 1)],
                                                lhsT=o_rc[:, 4096 * (rc % 2)
                                                          + 1024 * h + 128 * tb:
                                                          4096 * (rc % 2)
                                                          + 1024 * h
                                                          + 128 * (tb + 1)],
                                                rhs=wo_sb[:, 2048 * (gi % 2)
                                                          + 512 * h:
                                                          2048 * (gi % 2)
                                                          + 512 * (h + 1)],
                                                start=(h == 0), stop=(h == 3),
                                                skip_group_check=True)
                                        tS4.inc(last)
                            phase_end_c(tensor, tS4)

                        @blk.scalar
                        def _(scalar):
                            phase_start(scalar)

                            def wo_load(gi):
                                dm8 = gi % 8
                                if gi >= 2:
                                    scalar.wait_ge(tS4.h, 8 * (gi - 1))
                                for h in range(4):
                                    sLWOp[gi % 2].inc(scalar.dma_start(
                                        out=wo_sb[:, 2048 * (gi % 2) + 512 * h:
                                                  2048 * (gi % 2) + 512 * (h + 1)],
                                        in_=wo_bf[128 * h:128 * (h + 1),
                                                  512 * dm8:512 * (dm8 + 1)]), 16)

                            wo_load(0)
                            wo_load(1)
                            last = None
                            for gtb in range(512):
                                scalar.wait_ge(tS4.h, gtb + 1)
                                if gtb >= 3:
                                    scalar.wait_ge(sWSPp[gtb % 3].h,
                                                   16 * ((gtb - 3) // 3 + 1))
                                last = scalar.activation(
                                    out=s_stage[:, 512 * (gtb % 3):
                                                512 * (gtb % 3 + 1)],
                                    in_=ps_s[:, 512 * (gtb % 3):
                                             512 * (gtb % 3 + 1)],
                                    func=AF.Copy)
                                aSE.inc(last)
                                if gtb % 8 == 7 and gtb // 8 + 2 < 64:
                                    wo_load(gtb // 8 + 2)
                            scalar.wait_ge(sLWOp[0].h, sLWOp[0].n)
                            scalar.wait_ge(sLWOp[1].h, sLWOp[1].n)
                            phase_end_c(scalar, aSE)

                        @blk.vector
                        def _(vector):
                            phase_start(vector)
                            phase_end_q(vector)

                    PH[0] += 1
                    assert sPH.n == 5 * PH[0], sPH.n

        with ExitStack() as p5:
            xn_f = sbt(p5, "xn_f", [128, 32 * 512], BF16, side="left")
            wgu_q = sbt(p5, "wgu_q", [128, 2 * 2 * 32 * 384], BF16, side="left")
            wd_sb = sbt(p5, "wd_sb", [128, 2 * IC * 1024], BF16, side="left")
            h_sb = sbt(p5, "h_sb", [128, IC * 512], BF16, side="left")
            sg_sb = sbt(p5, "sg_sb", [128, 2 * 512], BF16, side="left")
            su_sb = sbt(p5, "su_sb", [128, 2 * 512], BF16, side="left")
            f_stage = sbt(p5, "f_stage", [128, 2 * 1024], F32, side="left")
            ps_g = pst(p5, "ps_g", [128, 2 * 512], F32)
            ps_u = pst(p5, "ps_u", [128, 2 * 512], F32)
            ps_f = pst(p5, "ps_f", [128, 2 * 1024], F32)

            WQ = 384

            def gu_sl(buf, w, g, lo, width, nit):
                base = (2 * buf + w) * 32 * WQ + g * (128 * nit) + lo
                return wgu_q[:, base:base + width]

            @blk.sync
            def _(sync):
                phase_start(sync)

                def wg_half(giq):
                    if giq >= 4 * NSB:
                        return
                    iq = giq % 4
                    nit = IQ_CHUNKS[iq]
                    i0 = IQ_STARTS[iq]
                    if giq >= 2:
                        psb, piq = divmod(giq - 2, 4)
                        sync.wait_ge(tGU.h, IC * psb + IQ_STARTS[piq]
                                     + IQ_CHUNKS[piq])
                    sLWGp[giq % 2].inc(sync.dma_start(
                        out=gu_sl(giq % 2, 0, 0, 0, 32 * 128 * nit, nit),
                        in_=bass.AP(wg_dq.tensor, 128 * i0,
                                    [[1408, 128], [128 * 1408, 32],
                                     [1, 128 * nit]])), 16)

                def xn_load(sb):
                    if sb >= NSB:
                        return
                    sync.wait_ge(cAGs[sb % 2].h, 1)
                    if sb >= 1:
                        sync.wait_ge(tGU.h, IC * sb)
                    sLXF.inc(sync.dma_start(
                        out=xn_f[:, :],
                        in_=bass.AP(xn_g[sb % 2].tensor, DM * (sb // 2) * 512,
                                    [[512, 128], [128 * 512, 32],
                                     [1, 512]])), 16)

                xn_load(0)
                wg_half(0)
                wg_half(1)
                for sb_i in range(NSB):
                    def f_stores(q4):
                        for tb2 in range(4):
                            ftb = 16 * sb_i + 4 * q4 + tb2
                            sync.wait_ge(vFE.h, ftb + 1)
                            sWFPp[ftb % 2].inc(sync.dma_start(
                                out=f_part[sb_i // 2][
                                    512 * (sb_i % 2) + 128 * tb2:
                                    512 * (sb_i % 2) + 128 * (tb2 + 1),
                                    1024 * q4:1024 * (q4 + 1)],
                                in_=f_stage[:, 1024 * (ftb % 2):
                                            1024 * (ftb % 2 + 1)]), 16)

                    wg_half(4 * sb_i + 2)
                    wg_half(4 * sb_i + 3)
                    wg_half(4 * sb_i + 4)
                    wg_half(4 * sb_i + 5)
                    f_stores(0)
                    xn_load(sb_i + 1)
                    f_stores(1)
                    f_stores(2)
                    f_stores(3)
                for s_ in (sWFPp[0], sWFPp[1], sLWDp[0], sLWDp[1],
                           sLWGp[0], sLWGp[1], sLXF):
                    sync.wait_ge(s_.h, s_.n)
                phase_end_q(sync)

            @blk.gpsimd
            def _(gpsimd):
                phase_start(gpsimd)
                for sb_i in range(NSB):
                    if sb_i % 2 == 1:
                        j = (sb_i - 1) // 2
                        for _p, _sw in enumerate(sWSPp):
                            _cnt = sum(1 for _g in range(64 * (j + 1))
                                       if _g % 3 == _p)
                            gpsimd.wait_ge(_sw.h, 16 * _cnt)
                        cS[j].inc(gpsimd.collective_compute(
                            "ReduceScatter", ALU.add, replica_groups=groups,
                            ins=[s_part[j].opt()], outs=[s_ownr[j].opt()]))
                    if sb_i >= 2 and sb_i % 2 == 0:
                        r = (sb_i - 2) // 2
                        gpsimd.wait_ge(sWFPp[0].h, 16 * 8 * (2 * r + 2))
                        gpsimd.wait_ge(sWFPp[1].h, 16 * 8 * (2 * r + 2))
                        cF[r].inc(gpsimd.collective_compute(
                            "ReduceScatter", ALU.add, replica_groups=groups,
                            ins=[f_part[r].opt()], outs=[f_ownr[r].opt()]))
                for r in (7,):
                    gpsimd.wait_ge(sWFPp[0].h, 16 * 8 * (2 * r + 2))
                    gpsimd.wait_ge(sWFPp[1].h, 16 * 8 * (2 * r + 2))
                    cF[r].inc(gpsimd.collective_compute(
                        "ReduceScatter", ALU.add, replica_groups=groups,
                        ins=[f_part[r].opt()], outs=[f_ownr[r].opt()]))
                phase_end_q(gpsimd)

            @blk.tensor
            def _(tensor):
                phase_start(tensor)
                last = None
                for sb_i in range(NSB):
                    tensor.wait_ge(sLXF.h, 16 * (sb_i + 1))
                    for iq in range(4):
                        giq = 4 * sb_i + iq
                        nit = IQ_CHUNKS[iq]
                        i0 = IQ_STARTS[iq]
                        bwait(tensor, sLWGp, giq, 16 * 2)
                        for itl in range(nit):
                            it = i0 + itl
                            git = IC * sb_i + it
                            if git >= 2:
                                tensor.wait_ge(vH.h, git - 1)
                            for g in range(32):
                                tensor.matmul(
                                    out=ps_g[:, 512 * (git % 2):
                                             512 * (git % 2 + 1)],
                                    lhsT=gu_sl(giq % 2, 0, g, 128 * itl, 128, nit),
                                    rhs=xn_f[:, 512 * g:512 * (g + 1)],
                                    start=(g == 0), stop=(g == 31),
                                    skip_group_check=True)
                                last = tensor.matmul(
                                    out=ps_u[:, 512 * (git % 2):
                                             512 * (git % 2 + 1)],
                                    lhsT=gu_sl(giq % 2, 1, g, 128 * itl, 128, nit),
                                    rhs=xn_f[:, 512 * g:512 * (g + 1)],
                                    start=(g == 0), stop=(g == 31),
                                    skip_group_check=True)
                            tGU.inc(last)
                    tensor.wait_ge(vH.h, IC * (sb_i + 1))
                    for q4 in range(4):
                        gq = 4 * sb_i + q4
                        bwait(tensor, sLWDp, gq, 16)
                        for tb2 in range(4):
                            ftb = 16 * sb_i + 4 * q4 + tb2
                            if ftb >= 2:
                                tensor.wait_ge(vFE.h, ftb - 1)
                            for it in range(IC):
                                for a in range(2):
                                    last = tensor.matmul(
                                        out=ps_f[:, 1024 * (ftb % 2) + 512 * a:
                                                 1024 * (ftb % 2)
                                                 + 512 * (a + 1)],
                                        lhsT=h_sb[:, 512 * it + 128 * tb2:
                                                  512 * it + 128 * (tb2 + 1)],
                                        rhs=wd_sb[:, IC * 1024 * (gq % 2)
                                                  + 1024 * it + 512 * a:
                                                  IC * 1024 * (gq % 2)
                                                  + 1024 * it + 512 * (a + 1)],
                                        start=(it == 0), stop=(it == IC - 1),
                                        skip_group_check=True)
                            tDN.inc(last)
                phase_end_c(tensor, tDN)

            @blk.scalar
            def _(scalar):
                phase_start(scalar)
                scalar.preload_act_table(AF.Silu)

                def wu_half(giq):
                    if giq >= 4 * NSB:
                        return
                    iq = giq % 4
                    nit = IQ_CHUNKS[iq]
                    i0 = IQ_STARTS[iq]
                    if giq >= 2:
                        psb, piq = divmod(giq - 2, 4)
                        scalar.wait_ge(tGU.h, IC * psb + IQ_STARTS[piq]
                                       + IQ_CHUNKS[piq])
                    sLWGp[giq % 2].inc(scalar.dma_start(
                        out=gu_sl(giq % 2, 1, 0, 0, 32 * 128 * nit, nit),
                        in_=bass.AP(wu_dq.tensor, 128 * i0,
                                    [[1408, 128], [128 * 1408, 32],
                                     [1, 128 * nit]])), 16)

                def wd_load(gq):
                    if gq >= 2:
                        scalar.wait_ge(tDN.h, 4 * (gq - 1))
                    sLWDp[gq % 2].inc(scalar.dma_start(
                        out=wd_sb[:, IC * 1024 * (gq % 2):
                                  IC * 1024 * (gq % 2 + 1)],
                        in_=bass.AP(wd_dq.tensor, 1024 * (gq % 4),
                                    [[DM, 128], [128 * DM, IC],
                                     [1, 1024]])), 16)

                wu_half(0)
                wu_half(1)
                last = None
                for git in range(IC * NSB):
                    sb_i, it = divmod(git, IC)
                    scalar.wait_ge(tGU.h, git + 1)
                    if git >= 2:
                        scalar.wait_ge(vH.h, git - 1)
                    scalar.activation(
                        out=sg_sb[:, 512 * (git % 2):512 * (git % 2 + 1)],
                        in_=ps_g[:, 512 * (git % 2):512 * (git % 2 + 1)],
                        func=AF.Silu)
                    last = scalar.activation(
                        out=su_sb[:, 512 * (git % 2):512 * (git % 2 + 1)],
                        in_=ps_u[:, 512 * (git % 2):512 * (git % 2 + 1)],
                        func=AF.Copy)
                    aGU.inc(last)
                    if it == 0:
                        wd_load(4 * sb_i)
                        wd_load(4 * sb_i + 1)
                    elif it == 2:
                        wu_half(4 * sb_i + 2)
                    elif it == 5:
                        wu_half(4 * sb_i + 3)
                    elif it == 8:
                        wu_half(4 * sb_i + 4)
                    elif it == IC - 1:
                        wu_half(4 * sb_i + 5)
                        wd_load(4 * sb_i + 2)
                        wd_load(4 * sb_i + 3)
                phase_end_c(scalar, aGU)

            @blk.vector
            def _(vector):
                phase_start(vector)
                last = None
                for sb_i in range(NSB):
                    for it in range(IC):
                        git = IC * sb_i + it
                        vector.wait_ge(aGU.h, git + 1)
                        vH.inc(vector.tensor_mul(
                            out=h_sb[:, 512 * it:512 * (it + 1)],
                            in0=sg_sb[:, 512 * (git % 2):512 * (git % 2 + 1)],
                            in1=su_sb[:, 512 * (git % 2):512 * (git % 2 + 1)]))
                    for q4 in range(4):
                        for tb2 in range(4):
                            ftb = 16 * sb_i + 4 * q4 + tb2
                            vector.wait_ge(tDN.h, ftb + 1)
                            if ftb >= 2:
                                bwait(vector, sWFPp, ftb - 2, 16)
                            vector.tensor_copy(
                                out=f_stage[:, 1024 * (ftb % 2):
                                            1024 * (ftb % 2) + 512],
                                in_=ps_f[:, 1024 * (ftb % 2):
                                         1024 * (ftb % 2) + 512])
                            last = vector.tensor_copy(
                                out=f_stage[:, 1024 * (ftb % 2) + 512:
                                            1024 * (ftb % 2 + 1)],
                                in_=ps_f[:, 1024 * (ftb % 2) + 512:
                                         1024 * (ftb % 2 + 1)])
                            vFE.inc(last)
                phase_end_c(vector, vFE)

        PH[0] += 1
        assert sPH.n == 5 * PH[0], sPH.n

        with ExitStack() as p6:
            s_t6 = sbt(p6, "s_t6", [128, 2 * DM], F32, side="left")
            f_t6 = sbt(p6, "f_t6", [128, 2 * DM], F32, side="left")
            x_t6 = sbt(p6, "x_t6", [128, 2 * DM], F32, side="left")
            g_t6 = sbt(p6, "g_t6", [128, 2 * DM], F32, side="left")
            sg_t6 = sbt(p6, "sg_t6", [128, 2 * DM], F32, side="left")
            out_t6 = sbt(p6, "out_t6", [128, 2 * DM], F32, side="left")

            @blk.sync
            def _(sync):
                phase_start(sync)
                def st6(rc):
                    b = DM * (rc % 2)
                    sync.wait_ge(vOUT.h, rc + 1)
                    sWOUTp[rc % 2].inc(sync.dma_start(
                        out=out_own[128 * rc:128 * (rc + 1), :],
                        in_=out_t6[:, b:b + DM]), 16)

                for rc in range(8):
                    b = DM * (rc % 2)
                    sync.wait_ge(cS[rc].h, 1)
                    sync.wait_ge(cF[rc].h, 1)
                    if rc >= 2:
                        sync.wait_ge(vOUT.h, rc - 1)
                    sLSFp[rc % 2].inc(sync.dma_start(out=s_t6[:, b:b + DM],
                                                     in_=s_ownr[rc]), 16)
                    sLSFp[rc % 2].inc(sync.dma_start(out=f_t6[:, b:b + DM],
                                                     in_=f_ownr[rc]), 16)
                    sLSFp[rc % 2].inc(sync.dma_start(
                        out=x_t6[:, b:b + DM],
                        in_=x_sc[128 * rc:128 * (rc + 1), :]), 16)
                    if rc >= 2:
                        st6(rc - 2)
                st6(6)
                st6(7)
                for name, dst in dbg_outs.items():
                    sWOUTp[0].inc(sync.dma_start(out=dst, in_=dbg_srcs[name]), 16)
                sync.wait_ge(sWOUTp[0].h, sWOUTp[0].n)
                sync.wait_ge(sWOUTp[1].h, sWOUTp[1].n)

            @blk.gpsimd
            def _(gpsimd):
                phase_start(gpsimd)

            @blk.tensor
            def _(tensor):
                phase_start(tensor)

            @blk.scalar
            def _(scalar):
                phase_start(scalar)
                for rc in range(8):
                    b = DM * (rc % 2)
                    scalar.wait_ge(sLSFp[rc % 2].h, 48 * (rc // 2 + 1))
                    if rc >= 2:
                        scalar.wait_ge(vOUT.h, rc - 1)
                    aG6.inc(scalar.activation(out=g_t6[:, b:b + DM],
                                              in_=f_t6[:, b:b + DM],
                                              func=AF.Silu))

            @blk.vector
            def _(vector):
                phase_start(vector)
                for rc in range(8):
                    b = DM * (rc % 2)
                    vector.wait_ge(aG6.h, rc + 1)
                    vSELF.inc(vector.tensor_mul(out=sg_t6[:, b:b + DM],
                                                in0=s_t6[:, b:b + DM],
                                                in1=g_t6[:, b:b + DM]))
                    vector.wait_ge(vSELF.h, vSELF.n)
                    if rc >= 2:
                        vector.wait_ge(sWOUTp[rc % 2].h, 16 * (rc // 2))
                    vOUT.inc(vector.tensor_add(out=out_t6[:, b:b + DM],
                                               in0=sg_t6[:, b:b + DM],
                                               in1=x_t6[:, b:b + DM]))

    return nc


def _alibi_slopes(n):
    cp2 = 2 ** math.floor(math.log2(n))
    base = 2.0 ** (-8.0 / cp2)
    slopes = base ** np.arange(1, cp2 + 1, dtype=np.float32)
    if cp2 != n:
        extra_base = 2.0 ** (-4.0 / cp2)
        extra = extra_base ** np.arange(1, 1 + 2 * (n - cp2), 2, dtype=np.float32)
        slopes = np.concatenate([slopes, extra])
    return slopes.astype(np.float32)


def make_in_maps(x_full, w_qkv, w_o, gqw, gsc, gqz, uqw, usc, uqz, dqw, dsc, dqz):
    x = np.ascontiguousarray(np.asarray(x_full, dtype=np.float32).reshape(S, DM))
    sl = _alibi_slopes(32)
    x4 = x.reshape(8, 8, 128, DM)
    in_maps = []
    for c in range(NCORE):
        g0, gn = G0S[c], GROUPS[c]
        m = {}
        m["x_own"] = np.ascontiguousarray(x[1024 * c:1024 * (c + 1)])
        m["x_sc"] = np.ascontiguousarray(x4[:, c].reshape(1024, DM))
        m["wqkv"] = np.ascontiguousarray(np.concatenate(
            [w_qkv[:, 512 * c:512 * (c + 1)],
             w_qkv[:, 4096 + 128 * c:4096 + 128 * (c + 1)],
             w_qkv[:, 5120 + 128 * c:5120 + 128 * (c + 1)]], axis=1))
        m["wo"] = np.ascontiguousarray(w_o[512 * c:512 * (c + 1)])
        m["slopes"] = np.ascontiguousarray(sl[4 * c:4 * (c + 1)].reshape(1, 4))
        gw = np.zeros((DM, 176), np.int32)
        gw[:, :16 * gn] = gqw[:, 16 * g0:16 * (g0 + gn)]
        m["g_qw"] = gw
        gz = np.zeros((32, 176), np.int32)
        gz[:, :16 * gn] = gqz[:, 16 * g0:16 * (g0 + gn)]
        m["g_qz"] = gz
        gs = np.zeros((32, 1408), np.float32)
        gs[:, :128 * gn] = gsc[:, 128 * g0:128 * (g0 + gn)]
        m["g_sc"] = gs
        uw = np.zeros((DM, 176), np.int32)
        uw[:, :16 * gn] = uqw[:, 16 * g0:16 * (g0 + gn)]
        m["u_qw"] = uw
        uz = np.zeros((32, 176), np.int32)
        uz[:, :16 * gn] = uqz[:, 16 * g0:16 * (g0 + gn)]
        m["u_qz"] = uz
        us = np.zeros((32, 1408), np.float32)
        us[:, :128 * gn] = usc[:, 128 * g0:128 * (g0 + gn)]
        m["u_sc"] = us
        dw = np.zeros((1408, 512), np.int32)
        dw[:128 * gn] = dqw[128 * g0:128 * (g0 + gn)]
        m["d_qw"] = dw
        dz = np.zeros((IC, 512), np.int32)
        dz[:gn] = dqz[g0:g0 + gn]
        m["d_qz"] = dz
        ds = np.zeros((IC, DM), np.float32)
        ds[:gn] = dsc[g0:g0 + gn]
        m["d_sc"] = ds
        in_maps.append(m)
    return in_maps


def kernel(L0_attn_hidden_states, L0_attn_w_qkv, L0_attn_w_o,
           L0_ffn_gate_qweight, L0_ffn_gate_scales, L0_ffn_gate_qzeros,
           L0_ffn_up_qweight, L0_ffn_up_scales, L0_ffn_up_qzeros,
           L0_ffn_down_qweight, L0_ffn_down_scales, L0_ffn_down_qzeros):
    in_maps = make_in_maps(
        np.asarray(L0_attn_hidden_states, dtype=np.float32),
        np.asarray(L0_attn_w_qkv, dtype=np.float32),
        np.asarray(L0_attn_w_o, dtype=np.float32),
        np.asarray(L0_ffn_gate_qweight, dtype=np.int32),
        np.asarray(L0_ffn_gate_scales, dtype=np.float32),
        np.asarray(L0_ffn_gate_qzeros, dtype=np.int32),
        np.asarray(L0_ffn_up_qweight, dtype=np.int32),
        np.asarray(L0_ffn_up_scales, dtype=np.float32),
        np.asarray(L0_ffn_up_qzeros, dtype=np.int32),
        np.asarray(L0_ffn_down_qweight, dtype=np.int32),
        np.asarray(L0_ffn_down_scales, dtype=np.float32),
        np.asarray(L0_ffn_down_qzeros, dtype=np.int32))

    nc = build_program()
    res = run_bass_kernel_spmd(nc, in_maps, core_ids=list(range(NCORE)))

    out = np.empty((8, 8, 128, DM), dtype=np.float32)
    for c in range(NCORE):
        out[:, c] = res.results[c]["out_own"].reshape(8, 128, DM)
    return out.reshape(1, S, DM)


# version 13
# speedup vs baseline: 1.0298x; 1.0298x over previous
import math
import numpy as np
from contextlib import ExitStack

import concourse.bass as bass
import concourse.mybir as mybir
from concourse.bass_utils import run_bass_kernel_spmd

F32 = mybir.dt.float32
BF16 = mybir.dt.bfloat16
I32 = mybir.dt.int32
AF = mybir.ActivationFunctionType
ALU = mybir.AluOpType
AX = mybir.AxisListType

NCORE = 8
S = 8192
DM = 4096
TOWN = 1024
NQT = 16
SCALE = 1.0 / math.sqrt(128.0)
CSHIFT = 12.0
IC = 11
GROUPS = [11, 11, 11, 11, 11, 11, 10, 10]
G0S = [0, 11, 22, 33, 44, 55, 66, 76]
EW = 8576
EL = 8704
NSB = 16
IQ_CHUNKS = [3, 3, 3, 2]
IQ_STARTS = [0, 3, 6, 9]
BIGNEG = -6.6e8


class Sem:
    __slots__ = ("h", "n")

    def __init__(self, ctx, nc, name):
        self.h = ctx.enter_context(nc.semaphore(name))
        self.n = 0

    def inc(self, ins, k=1):
        ins.then_inc(self.h, k)
        self.n += k
        return ins


def build_program(dbg=(), upto=6):
    nc = bass.Bass("TRN2", target_bir_lowering=False)

    x_own = nc.dram_tensor("x_own", [TOWN, DM], F32, kind="ExternalInput").ap()
    x_sc = nc.dram_tensor("x_sc", [TOWN, DM], F32, kind="ExternalInput").ap()
    wqkv = nc.dram_tensor("wqkv", [DM, 768], F32, kind="ExternalInput").ap()
    wo = nc.dram_tensor("wo", [512, DM], F32, kind="ExternalInput").ap()
    slopes = nc.dram_tensor("slopes", [1, 4], F32, kind="ExternalInput").ap()
    g_qw = nc.dram_tensor("g_qw", [DM, 176], I32, kind="ExternalInput").ap()
    g_qz = nc.dram_tensor("g_qz", [32, 176], I32, kind="ExternalInput").ap()
    g_sc = nc.dram_tensor("g_sc", [32, 1408], F32, kind="ExternalInput").ap()
    u_qw = nc.dram_tensor("u_qw", [DM, 176], I32, kind="ExternalInput").ap()
    u_qz = nc.dram_tensor("u_qz", [32, 176], I32, kind="ExternalInput").ap()
    u_sc = nc.dram_tensor("u_sc", [32, 1408], F32, kind="ExternalInput").ap()
    d_qw = nc.dram_tensor("d_qw", [1408, 512], I32, kind="ExternalInput").ap()
    d_qz = nc.dram_tensor("d_qz", [IC, 512], I32, kind="ExternalInput").ap()
    d_sc = nc.dram_tensor("d_sc", [IC, DM], F32, kind="ExternalInput").ap()

    out_own = nc.dram_tensor("out_own", [TOWN, DM], F32, kind="ExternalOutput").ap()

    cc_xn = [nc.dram_tensor(f"cc_xn{i}", [DM, 512], BF16, kind="Internal").ap()
             for i in range(2)]
    xn_g = [nc.dram_tensor(f"xn_g{i}", [NCORE * DM, 512], BF16,
                           kind="Internal").ap() for i in range(2)]
    q_scr = nc.dram_tensor("q_scr", [512, S], BF16, kind="Internal").ap()
    o_scr = nc.dram_tensor("o_scr", [512, S], BF16, kind="Internal").ap()
    wo_bf = nc.dram_tensor("wo_bf", [512, DM], BF16, kind="Internal").ap()
    e_lin = nc.dram_tensor("e_lin", [4, EL], BF16, kind="Internal").ap()
    zf_g = nc.dram_tensor("zf_g", [32, 1408], I32, kind="Internal").ap()
    zf_u = nc.dram_tensor("zf_u", [32, 1408], I32, kind="Internal").ap()
    zf_d = nc.dram_tensor("zf_d", [IC, DM], I32, kind="Internal").ap()
    wg_dq = nc.dram_tensor("wg_dq", [32, 128, 1408], BF16, kind="Internal").ap()
    wu_dq = nc.dram_tensor("wu_dq", [32, 128, 1408], BF16, kind="Internal").ap()
    wd_dq = nc.dram_tensor("wd_dq", [IC, 128, DM], BF16, kind="Internal").ap()
    s_part = [nc.dram_tensor(f"s_part{r}", [1024, DM], F32, kind="Internal").ap()
              for r in range(8)]
    f_part = [nc.dram_tensor(f"f_part{r}", [1024, DM], F32, kind="Internal").ap()
              for r in range(8)]
    s_ownr = [nc.dram_tensor(f"s_ownr{r}", [128, DM], F32,
                             kind="Internal").ap() for r in range(8)]
    f_ownr = [nc.dram_tensor(f"f_ownr{r}", [128, DM], F32,
                             kind="Internal").ap() for r in range(8)]

    dbg_outs = {}
    dbg_srcs = {"xn_g0": xn_g[0], "xn_g1": xn_g[1], "q_scr": q_scr, "o_scr": o_scr,
                "e_lin": e_lin, "wg_dq": wg_dq, "wd_dq": wd_dq,
                "s_part0": s_part[0], "f_part0": f_part[0],
                "s_ownr0": s_ownr[0], "f_ownr0": f_ownr[0]}
    for name in dbg:
        src = dbg_srcs[name]
        dbg_outs[name] = nc.dram_tensor(
            "dbg_" + name, list(src.shape), src.dtype, kind="ExternalOutput").ap()

    groups = [list(range(NCORE))]

    NKB = [4 * (qt + 1) for qt in range(NQT)]
    KB_CUM = [0]
    PB_CUM = [0]
    for h in range(4):
        for qt in range(NQT):
            KB_CUM.append(KB_CUM[-1] + NKB[qt])
            PB_CUM.append(PB_CUM[-1] + NKB[qt] // 2)

    with ExitStack() as top:
        def sem(name):
            return Sem(top, nc, name)

        sPH = sem("sPH")
        tSELF = sem("tSELF"); aSELF = sem("aSELF"); vSELF = sem("vSELF")
        gSELF = sem("gSELF")
        sLX = sem("sLX"); sLX2 = sem("sLX2")
        aSQ = sem("aSQ"); vST = sem("vST"); aST = sem("aST"); aXN = sem("aXN")
        tTR = sem("tTR"); vTS = sem("vTS")
        sWXNp = [sem("sWXNa"), sem("sWXNb")]
        sLQW = sem("sLQW")
        sLDQz = sem("sLDQz"); vDQz = sem("vDQz")
        sLDQp = [sem("sLDQa"), sem("sLDQb")]
        sWDQp = [sem("sWDQa"), sem("sWDQb")]
        vDQs = sem("vDQs"); vDQm = sem("vDQm"); tDQ = sem("tDQ")
        sSF = sem("sSF"); vDQb = sem("vDQb"); sZS = sem("sZS")
        gEB = sem("gEB"); vEB = sem("vEB"); aEB = sem("aEB"); sEB = sem("sEB")
        cAGs = [sem("cAG0"), sem("cAG1")]
        sLXN4 = [sem(f"sLXN{i}") for i in range(4)]
        tQK = sem("tQK"); vEVQ = sem("vEVQ")
        tTRV = sem("tTRV"); vEVV = sem("vEVV"); sWQS = sem("sWQS")
        sEB2 = sem("sEB2"); sEB3 = sem("sEB3")
        sLQp = [sem("sLQa"), sem("sLQb")]
        tSC = sem("tSC"); aEX = sem("aEX"); vMK = sem("vMK")
        tPV = sem("tPV"); aOE = sem("aOE"); vRC = sem("vRC"); tLB = sem("tLB")
        vON = sem("vON"); sWOSp = [sem("sWOSa"), sem("sWOSb")]
        sLWOp = [sem("sLWOa"), sem("sLWOb")]
        sLORp = [sem("sLORa"), sem("sLORb")]
        tS4 = sem("tS4"); aSE = sem("aSE")
        sWSPp = [sem(f"sWSP{i}") for i in range(3)]
        cS = [sem(f"cS{r}") for r in range(8)]
        sLXF = sem("sLXF")
        sLWGp = [sem("sLWGa"), sem("sLWGb")]
        sLWDp = [sem(f"sLWD{i}") for i in range(3)]
        tGU = sem("tGU"); aGU = sem("aGU"); vH = sem("vH")
        tDN = sem("tDN"); vFE = sem("vFE")
        sWFPp = [sem("sWFPa"), sem("sWFPb")]
        cF = [sem(f"cF{r}") for r in range(8)]
        sLSFp = [sem("sLSFa"), sem("sLSFb")]
        aG6 = sem("aG6"); vOUT = sem("vOUT")
        sWOUTp = [sem("sWOUTa"), sem("sWOUTb")]

        def sbt(ctx, name, shape, dtype, side="left"):
            return ctx.enter_context(nc.sbuf_tensor(name, shape, dtype, side=side))

        def pst(ctx, name, shape, dtype):
            return ctx.enter_context(nc.psum_tensor(name, shape, dtype))

        ones_rowf = sbt(top, "ones_rowf", [1, 128], F32)
        ones_colb = sbt(top, "ones_colb", [128, 1], BF16)
        bias_c = sbt(top, "bias_c", [128, 1], F32)
        eps_b = sbt(top, "eps_b", [128, 1], F32)
        ident = sbt(top, "ident", [128, 128], BF16)
        slopes_ng = sbt(top, "slopes_ng", [128, 4], F32)
        slopes_rep = sbt(top, "slopes_rep", [128, 4], F32)

        PH = [0]

        def phase_start(eng):
            if PH[0] > 0:
                eng.wait_ge(sPH.h, 5 * PH[0])

        def phase_end_q(eng):
            sPH.n += 1
            eng.nop().then_inc(sPH.h, 1)

        def phase_end_c(eng, last_sem):
            eng.wait_ge(last_sem.h, last_sem.n)
            sPH.n += 1
            eng.nop().then_inc(sPH.h, 1)

        def bwait(eng, sems, g, U):
            eng.wait_ge(sems[g % 2].h, U * (g // 2 + 1))

        blk = top.enter_context(nc.Block())

        with ExitStack() as scWQ:
            wqkv_sb = sbt(scWQ, "wqkv_sb", [128, 32 * 768], BF16, side="left")

            with ExitStack() as p0:
                x_t = sbt(p0, "x_t", [128, DM], F32, side="right")
                sq_t = sbt(p0, "sq_t", [128, DM], F32, side="right")
                msum = sbt(p0, "msum", [128, 8], F32, side="right")
                qsum = sbt(p0, "qsum", [128, 8], F32, side="right")
                mean8 = sbt(p0, "mean8", [128, 8], F32, side="right")
                eq8 = sbt(p0, "eq8", [128, 8], F32, side="right")
                m28 = sbt(p0, "m28", [128, 8], F32, side="right")
                var8 = sbt(p0, "var8", [128, 8], F32, side="right")
                std8 = sbt(p0, "std8", [128, 8], F32, side="right")
                rinv8 = sbt(p0, "rinv8", [128, 8], F32, side="right")
                bneg8 = sbt(p0, "bneg8", [128, 8], F32, side="right")
                tmp8 = sbt(p0, "tmp8", [128, 8], F32, side="right")
                xn_tm = sbt(p0, "xn_tm", [128, 2 * DM], BF16, side="right")
                xn_stage = sbt(p0, "xn_stage", [128, 2 * 512], BF16, side="right")
                qw_t = sbt(p0, "qw_t", [128, 2 * 512], I32, side="right")
                nib_t = sbt(p0, "nib_t", [128, 2048], I32, side="right")
                d1_t = sbt(p0, "d1_t", [128, 2048], F32, side="right")
                w_t = sbt(p0, "w_t", [128, 2 * 2048], BF16, side="right")
                gz_sb = sbt(p0, "gz_sb", [32, 176], I32, side="right")
                uz_sb = sbt(p0, "uz_sb", [32, 176], I32, side="right")
                dz_sb = sbt(p0, "dz_sb", [IC, 512], I32, side="right")
                zfg_sb = sbt(p0, "zfg_sb", [32, 1408], I32, side="right")
                zfu_sb = sbt(p0, "zfu_sb", [32, 1408], I32, side="right")
                zfd_sb = sbt(p0, "zfd_sb", [IC, DM], I32, side="right")
                zfg_bf = sbt(p0, "zfg_bf", [32, 1408], BF16, side="right")
                zfu_bf = sbt(p0, "zfu_bf", [32, 1408], BF16, side="right")
                zfd_bf = sbt(p0, "zfd_bf", [IC, DM], BF16, side="right")
                sfg_bf = sbt(p0, "sfg_bf", [32, 1408], BF16, side="right")
                sfu_bf = sbt(p0, "sfu_bf", [32, 1408], BF16, side="right")
                sfd_bf = sbt(p0, "sfd_bf", [IC, DM], BF16, side="right")
                ones_rowb = sbt(p0, "ones_rowb", [1, 128], BF16, side="right")
                z_stg = sbt(p0, "z_stg", [1, 2 * 1536], BF16, side="right")
                s_stg = sbt(p0, "s_stg", [1, 2 * 1536], BF16, side="right")
                ramp = sbt(p0, "ramp", [128, 68], F32, side="right")
                rmax = sbt(p0, "rmax", [128, 68], F32, side="right")
                rneg = sbt(p0, "rneg", [128, 68], F32, side="right")
                rmix = sbt(p0, "rmix", [128, 68], F32, side="right")
                sq_row = sbt(p0, "sq_row", [128, 68], F32, side="right")
                e_rowf = sbt(p0, "e_rowf", [128, 68], F32, side="right")
                e_rowb = sbt(p0, "e_rowb", [128, 68], BF16, side="right")
                ps_tr = pst(p0, "ps_tr", [128, 2 * 1024], BF16)
                ps_zb = pst(p0, "ps_zb", [128, 1536], F32)
                ps_sb5 = pst(p0, "ps_sb5", [128, 1536], F32)

                dq_jobs = []
                for g in range(32):
                    dq_jobs.append(("g", g, 176, 1408, 0))
                for g in range(32):
                    dq_jobs.append(("u", g, 176, 1408, 0))
                for it in range(IC):
                    for oh in range(4):
                        dq_jobs.append(("d", it, 128, 1024, oh))

                @blk.gpsimd
                def _(gpsimd):
                    for g in range(32):
                        sLQW.inc(gpsimd.dma_start(
                            out=wqkv_sb[:, 768 * g:768 * (g + 1)],
                            in_=wqkv[128 * g:128 * (g + 1), :]), 16)
                    sLQW.inc(gpsimd.dma_start(out=wo_bf, in_=wo), 16)
                    sSF.inc(gpsimd.dma_start(out=sfg_bf[:, :], in_=g_sc), 16)
                    sSF.inc(gpsimd.dma_start(out=sfu_bf[:, :], in_=u_sc), 16)
                    sSF.inc(gpsimd.dma_start(out=sfd_bf[:, :], in_=d_sc), 16)
                    gpsimd.memset(ident[:, :], 1.0)
                    gSELF.inc(gpsimd.iota(
                        ramp[:, :], [[1, 68]], base=-512, channel_multiplier=68,
                        allow_small_or_imprecise_dtypes=True))
                    gpsimd.wait_ge(gSELF.h, gSELF.n)
                    gEB.inc(gpsimd.affine_select(
                        out=ident[:, :], in_=ident[:, :],
                        compare_op=ALU.is_equal, fill=0.0, base=0,
                        pattern=[[-1, 128]], channel_multiplier=1))
                    for tb in range(8):
                        for fg in range(8):
                            idx = tb * 8 + fg
                            gpsimd.wait_ge(vTS.h, idx + 1)
                            for j in range(4):
                                fb = 4 * fg + j
                                sWXNp[idx % 2].inc(gpsimd.dma_start(
                                    out=cc_xn[tb // 4][
                                        128 * fb:128 * (fb + 1),
                                        128 * (tb % 4):128 * (tb % 4 + 1)],
                                    in_=xn_stage[:, 512 * (idx % 2) + 128 * j:
                                                 512 * (idx % 2) + 128 * (j + 1)]),
                                    16)
                    gpsimd.wait_ge(sWXNp[0].h, 64 * 16)
                    gpsimd.wait_ge(sWXNp[1].h, 64 * 16)
                    cAGs[0].inc(gpsimd.collective_compute(
                        "AllGather", ALU.bypass, replica_groups=groups,
                        ins=[cc_xn[0].opt()], outs=[xn_g[0].opt()]))
                    gpsimd.wait_ge(sWXNp[0].h, 64 * 32)
                    gpsimd.wait_ge(sWXNp[1].h, 64 * 32)
                    cAGs[1].inc(gpsimd.collective_compute(
                        "AllGather", ALU.bypass, replica_groups=groups,
                        ins=[cc_xn[1].opt()], outs=[xn_g[1].opt()]))
                    gpsimd.wait_ge(sLQW.h, sLQW.n)
                    phase_end_q(gpsimd)

                @blk.sync
                def _(sync):
                    sLDQz.inc(sync.dma_start(out=slopes_rep[:, :],
                                             in_=slopes.broadcast_to([128, 4])), 16)
                    for tb in range(8):
                        if tb >= 1:
                            sync.wait_ge(vST.h, tb)
                            sync.wait_ge(aSQ.h, tb)
                        sLX.inc(sync.dma_start(
                            out=x_t[:, :],
                            in_=x_own[128 * tb:128 * (tb + 1), :]), 16)
                    for tb in range(8):
                        sync.wait_ge(vST.h, 8)
                        sync.wait_ge(aXN.h, tb)
                        sLX2.inc(sync.dma_start(
                            out=x_t[:, :],
                            in_=x_own[128 * tb:128 * (tb + 1), :]), 16)
                    sLDQz.inc(sync.dma_start(out=gz_sb[:, :], in_=g_qz), 16)
                    sLDQz.inc(sync.dma_start(out=uz_sb[:, :], in_=u_qz), 16)
                    sLDQz.inc(sync.dma_start(out=dz_sb[:, :], in_=d_qz), 16)

                    sync.wait_ge(vDQz.h, 3)
                    sync.wait_ge(sSF.h, 48)
                    for k, (tag, i0, nw, wout, oh) in enumerate(dq_jobs):
                        if k >= 2:
                            sync.wait_ge(vDQs.h, k - 1)
                        b = k % 2
                        if tag == "g":
                            qsrc = g_qw[128 * i0:128 * (i0 + 1), :]
                            zrow, srow, off = zfg_bf, sfg_bf, 0
                        elif tag == "u":
                            qsrc = u_qw[128 * i0:128 * (i0 + 1), :]
                            zrow, srow, off = zfu_bf, sfu_bf, 0
                        else:
                            qsrc = d_qw[128 * i0:128 * (i0 + 1),
                                        128 * oh:128 * (oh + 1)]
                            zrow, srow, off = zfd_bf, sfd_bf, 1024 * oh
                        if k >= 2:
                            sync.wait_ge(tDQ.h, k - 1)
                        sZS.inc(sync.dma_start(
                            out=z_stg[0:1, 1536 * b:1536 * b + wout],
                            in_=zrow[i0:i0 + 1, off:off + wout]), 16)
                        sZS.inc(sync.dma_start(
                            out=s_stg[0:1, 1536 * b:1536 * b + wout],
                            in_=srow[i0:i0 + 1, off:off + wout]), 16)
                        sLDQp[b].inc(sync.dma_start(
                            out=qw_t[:, 512 * b:512 * b + nw], in_=qsrc), 16)
                        sync.wait_ge(vDQm.h, k + 1)
                        if tag == "g":
                            dst = bass.AP(wg_dq.tensor, i0 * 128 * 1408,
                                          [[1408, 128], [1, 1408]])
                        elif tag == "u":
                            dst = bass.AP(wu_dq.tensor, i0 * 128 * 1408,
                                          [[1408, 128], [1, 1408]])
                        else:
                            dst = bass.AP(wd_dq.tensor,
                                          i0 * 128 * DM + 1024 * oh,
                                          [[DM, 128], [1, 1024]])
                        sWDQp[b].inc(sync.dma_start(
                            out=dst, in_=w_t[:, 2048 * b:2048 * b + wout]), 16)
                    for h in range(4):
                        sync.wait_ge(vEB.h, 2 + h + 1)
                        sEB.inc(sync.dma_start(
                            out=bass.AP(e_lin.tensor, h * EL,
                                        [[68, 128], [1, 68]]),
                            in_=e_rowb[:, :]), 16)
                    for s_ in (sLX, sLX2, sLDQz, sLDQp[0], sLDQp[1], sZS,
                               sWDQp[0], sWDQp[1], sEB):
                        sync.wait_ge(s_.h, s_.n)
                    phase_end_q(sync)

                @blk.tensor
                def _(tensor):
                    tensor.wait_ge(gEB.h, 1)
                    last = None
                    for tb in range(8):
                        tensor.wait_ge(aXN.h, tb + 1)
                        for fg in range(8):
                            idx = tb * 8 + fg
                            if idx >= 2:
                                tensor.wait_ge(vTS.h, idx - 1)
                            for j in range(4):
                                fb = 4 * fg + j
                                last = tensor.transpose(
                                    out=ps_tr[:, 1024 * (idx % 2) + 128 * j:
                                              1024 * (idx % 2) + 128 * (j + 1)],
                                    in_=xn_tm[:, DM * (tb % 2) + 128 * fb:
                                              DM * (tb % 2) + 128 * (fb + 1)],
                                    identity=ident[:, :])
                            tTR.inc(last)
                    for k, (tag, i0, nw, wout, oh) in enumerate(dq_jobs):
                        b = k % 2
                        tensor.wait_ge(sZS.h, 32 * (k + 1))
                        if k >= 1:
                            tensor.wait_ge(vDQb.h, k)
                            tensor.wait_ge(vDQm.h, k)
                        last = None
                        for c0 in range(0, wout, 512):
                            w = min(512, wout - c0)
                            tensor.matmul(
                                out=ps_zb[:, c0:c0 + w],
                                lhsT=ones_rowb[:, :],
                                rhs=z_stg[0:1, 1536 * b + c0:1536 * b + c0 + w],
                                start=True, stop=True, skip_group_check=True)
                            last = tensor.matmul(
                                out=ps_sb5[:, c0:c0 + w],
                                lhsT=ones_rowb[:, :],
                                rhs=s_stg[0:1, 1536 * b + c0:1536 * b + c0 + w],
                                start=True, stop=True, skip_group_check=True)
                        tDQ.inc(last)
                    phase_end_c(tensor, tDQ)

                @blk.scalar
                def _(scalar):
                    scalar.preload_act_table(AF.Exp)
                    for tb in range(8):
                        scalar.wait_ge(sLX.h, 16 * (tb + 1))
                        if tb >= 1:
                            scalar.wait_ge(vST.h, tb)
                        aSQ.inc(scalar.activation(out=sq_t[:, :], in_=x_t[:, :],
                                                  func=AF.Square))
                    scalar.wait_ge(vSELF.h, 4)
                    scalar.preload_act_table(AF.Sqrt)
                    aST.inc(scalar.activation(out=std8[:, :], in_=var8[:, :],
                                              func=AF.Sqrt, bias=eps_b[:, :]))
                    scalar.preload_act_table(AF.Exp)
                    for tb in range(8):
                        scalar.wait_ge(sLX2.h, 16 * (tb + 1))
                        scalar.wait_ge(vSELF.h, 7)
                        if tb >= 2:
                            scalar.wait_ge(tTR.h, 8 * (tb - 1))
                        aXN.inc(scalar.activation(
                            out=xn_tm[:, DM * (tb % 2):DM * (tb % 2 + 1)],
                            in_=x_t[:, :],
                            func=AF.Identity, scale=rinv8[:, tb:tb + 1],
                            bias=bneg8[:, tb:tb + 1]))
                    scalar.wait_ge(vEB.h, 1)
                    scalar.preload_act_table(AF.Sqrt)
                    aSELF.inc(scalar.activation(out=sq_row[:, :], in_=rmix[:, :],
                                                func=AF.Sqrt))
                    scalar.wait_ge(aSELF.h, aSELF.n)
                    scalar.preload_act_table(AF.Exp)
                    last = None
                    for h in range(4):
                        scalar.wait_ge(vEB.h, 2 + h)
                        last = scalar.activation(
                            out=e_rowf[:, :],
                            in_=sq_row[:, :], func=AF.Exp,
                            scale=slopes_ng[:, h:h + 1])
                        aEB.inc(last)
                    phase_end_c(scalar, aEB)

                @blk.vector
                def _(vector):
                    vector.memset(ones_rowf[:, :], 1.0)
                    vector.memset(ones_colb[:, :], 1.0)
                    vector.memset(bias_c[:, :], -CSHIFT)
                    vector.memset(eps_b[:, :], 1e-5)
                    for tb in range(8):
                        vector.wait_ge(sLX.h, 16 * (tb + 1))
                        vector.wait_ge(aSQ.h, tb + 1)
                        vector.reduce_sum(out=msum[:, tb:tb + 1], in_=x_t[:, :],
                                          axis=AX.X)
                        vST.inc(vector.reduce_sum(out=qsum[:, tb:tb + 1],
                                                  in_=sq_t[:, :], axis=AX.X))
                    vector.wait_ge(vST.h, 8)
                    vSELF.inc(vector.tensor_scalar_mul(
                        out=mean8[:, :], in0=msum[:, :], scalar1=1.0 / DM))
                    vSELF.inc(vector.tensor_scalar_mul(
                        out=eq8[:, :], in0=qsum[:, :], scalar1=1.0 / DM))
                    vector.wait_ge(vSELF.h, 2)
                    vSELF.inc(vector.tensor_mul(out=m28[:, :], in0=mean8[:, :],
                                                in1=mean8[:, :]))
                    vector.wait_ge(vSELF.h, 3)
                    vSELF.inc(vector.tensor_sub(out=var8[:, :], in0=eq8[:, :],
                                                in1=m28[:, :]))
                    vector.wait_ge(aST.h, 1)
                    vSELF.inc(vector.reciprocal(out=rinv8[:, :], in_=std8[:, :]))
                    vector.wait_ge(vSELF.h, 5)
                    vSELF.inc(vector.tensor_mul(out=tmp8[:, :], in0=mean8[:, :],
                                                in1=rinv8[:, :]))
                    vector.wait_ge(vSELF.h, 6)
                    vSELF.inc(vector.tensor_scalar_mul(
                        out=bneg8[:, :], in0=tmp8[:, :], scalar1=-1.0))
                    for tb in range(8):
                        for fg in range(8):
                            idx = tb * 8 + fg
                            vector.wait_ge(tTR.h, idx + 1)
                            if idx >= 2:
                                bwait(vector, sWXNp, idx - 2, 64)
                            vTS.inc(vector.tensor_copy(
                                out=xn_stage[:, 512 * (idx % 2):
                                             512 * (idx % 2 + 1)],
                                in_=ps_tr[:, 1024 * (idx % 2):
                                          1024 * (idx % 2) + 512]))
                    vector.memset(ones_rowb[:, :], 1.0)
                    vector.wait_ge(sLDQz.h, 16 * 4)
                    for zsrc, dstz, dstzb, wtot in (
                            (gz_sb, zfg_sb, zfg_bf, 1408),
                            (uz_sb, zfu_sb, zfu_bf, 1408),
                            (dz_sb, zfd_sb, zfd_bf, DM)):
                        last = None
                        for j in range(8):
                            last = vector.tensor_scalar(
                                out=dstz[:, j:wtot:8], in0=zsrc[:, :],
                                scalar1=4 * j, scalar2=15,
                                op0=ALU.logical_shift_right,
                                op1=ALU.bitwise_and)
                        vSELF.inc(last)
                        vector.wait_ge(vSELF.h, vSELF.n)
                        vDQz.inc(vector.tensor_copy(out=dstzb[:, :],
                                                    in_=dstz[:, :]))
                    for k, (tag, i0, nw, wout, oh) in enumerate(dq_jobs):
                        b = k % 2
                        bwait(vector, sLDQp, k, 16)
                        if k >= 1:
                            vector.wait_ge(vDQm.h, k)
                        last = None
                        for j in range(8):
                            last = vector.tensor_scalar(
                                out=nib_t[:, j:wout:8],
                                in0=qw_t[:, 512 * b:512 * b + nw],
                                scalar1=4 * j, scalar2=15,
                                op0=ALU.logical_shift_right,
                                op1=ALU.bitwise_and)
                        vDQs.inc(last)
                        vector.wait_ge(vDQs.h, k + 1)
                        vector.wait_ge(tDQ.h, k + 1)
                        vDQb.inc(vector.tensor_sub(
                            out=d1_t[:, 0:wout], in0=nib_t[:, 0:wout],
                            in1=ps_zb[:, 0:wout]))
                        vector.wait_ge(vDQb.h, vDQb.n)
                        if k >= 2:
                            bwait(vector, sWDQp, k - 2, 16)
                        vDQm.inc(vector.tensor_mul(
                            out=w_t[:, 2048 * b:2048 * b + wout],
                            in0=d1_t[:, 0:wout],
                            in1=ps_sb5[:, 0:wout]))
                    vector.wait_ge(gEB.h, 1)
                    vector.tensor_scalar_max(out=rmax[:, :], in0=ramp[:, :],
                                             scalar1=0.0)
                    vSELF.inc(vector.tensor_scalar_min(out=rneg[:, :],
                                                       in0=ramp[:, :],
                                                       scalar1=0.0))
                    vector.wait_ge(vSELF.h, vSELF.n)
                    vEB.inc(vector.scalar_tensor_tensor(
                        out=rmix[:, :], in0=rneg[:, :], scalar=BIGNEG,
                        in1=rmax[:, :], op0=ALU.mult, op1=ALU.add))
                    vector.wait_ge(sLDQz.h, 16 * 4)
                    vEB.inc(vector.tensor_scalar_mul(
                        out=slopes_ng[:, :], in0=slopes_rep[:, :],
                        scalar1=-1.0))
                    last = None
                    for h in range(4):
                        vector.wait_ge(aEB.h, h + 1)
                        if h >= 1:
                            vector.wait_ge(sEB.h, 16 * h)
                        last = vector.tensor_copy(
                            out=e_rowb[:, :], in_=e_rowf[:, :])
                        vEB.inc(last)
                    phase_end_c(vector, vEB)

            PH[0] += 1
            assert sPH.n == 5 * PH[0], sPH.n

            with ExitStack() as scKV:
                k_fm = sbt(scKV, "k_fm", [128, S], BF16, side="right")
                v_tm = sbt(scKV, "v_tm", [128, S], BF16, side="right")
                e_sb = sbt(scKV, "e_sb", [128, 4 * EW], BF16, side="right")

                with ExitStack() as p2:
                    xn_t2 = sbt(p2, "xn_t2", [128, 4 * 512], BF16, side="right")
                    q_stage = sbt(p2, "q_stage", [128, 4 * 512], BF16, side="right")
                    v_stage = sbt(p2, "v_stage", [128, 512], BF16, side="right")
                    ps_qkv = pst(p2, "ps_qkv", [128, 6 * 512], F32)
                    ps_vt = pst(p2, "ps_vt", [128, 512], BF16)

                    @blk.sync
                    def _(sync):
                        phase_start(sync)
                        for tt in range(NQT):
                            sync.wait_ge(cAGs[tt % 2].h, 1)
                            for g in range(32):
                                gi = 32 * tt + g
                                if gi % 2 != 0:
                                    continue
                                if gi >= 4:
                                    sync.wait_ge(tQK.h, gi - 3)
                                sLXN4[gi % 4].inc(sync.dma_start(
                                    out=xn_t2[:, 512 * (gi % 4):512 * (gi % 4 + 1)],
                                    in_=xn_g[tt % 2][
                                        DM * (tt // 2) + 128 * g:
                                        DM * (tt // 2) + 128 * (g + 1), :]), 16)
                                r = gi // 2
                                h_, i_ = divmod(r, 128)
                                sEB2.inc(sync.dma_start(
                                    out=e_sb[i_:i_ + 1, EW * h_:EW * (h_ + 1)],
                                    in_=bass.AP(e_lin.tensor, h_ * EL + 128 - i_,
                                                [[1, 1], [1, EW]])), 16)
                            sync.wait_ge(vEVQ.h, tt + 1)
                            for f in range(4):
                                sWQS.inc(sync.dma_start(
                                    out=q_scr[128 * f:128 * (f + 1),
                                              512 * tt:512 * (tt + 1)],
                                    in_=q_stage[:, 512 * f:512 * (f + 1)]), 16)
                        for s_ in (sLXN4[0], sLXN4[2], sWQS, sEB2):
                            sync.wait_ge(s_.h, s_.n)
                        phase_end_q(sync)

                    @blk.gpsimd
                    def _(gpsimd):
                        phase_start(gpsimd)
                        for tt in range(NQT):
                            gpsimd.wait_ge(cAGs[tt % 2].h, 1)
                            for g in range(32):
                                gi = 32 * tt + g
                                if gi % 2 != 1:
                                    continue
                                if gi >= 4:
                                    gpsimd.wait_ge(tQK.h, gi - 3)
                                sLXN4[gi % 4].inc(gpsimd.dma_start(
                                    out=xn_t2[:, 512 * (gi % 4):512 * (gi % 4 + 1)],
                                    in_=xn_g[tt % 2][
                                        DM * (tt // 2) + 128 * g:
                                        DM * (tt // 2) + 128 * (g + 1), :]), 16)
                                r = 256 + gi // 2
                                h_, i_ = divmod(r, 128)
                                sEB3.inc(gpsimd.dma_start(
                                    out=e_sb[i_:i_ + 1, EW * h_:EW * (h_ + 1)],
                                    in_=bass.AP(e_lin.tensor, h_ * EL + 128 - i_,
                                                [[1, 1], [1, EW]])), 16)
                        gpsimd.wait_ge(sEB3.h, sEB3.n)
                        gpsimd.wait_ge(sLXN4[1].h, sLXN4[1].n)
                        gpsimd.wait_ge(sLXN4[3].h, sLXN4[3].n)
                        phase_end_q(gpsimd)

                    @blk.tensor
                    def _(tensor):
                        phase_start(tensor)
                        last = None
                        for tt in range(NQT):
                            for g in range(32):
                                gi = 32 * tt + g
                                tensor.wait_ge(sLXN4[gi % 4].h, 16 * (gi // 4 + 1))
                                if g == 0 and tt >= 1:
                                    tensor.wait_ge(vEVQ.h, tt)
                                for f in range(6):
                                    last = tensor.matmul(
                                        out=ps_qkv[:, 512 * f:512 * (f + 1)],
                                        lhsT=wqkv_sb[:, 768 * g + 128 * f:
                                                     768 * g + 128 * (f + 1)],
                                        rhs=xn_t2[:, 512 * (gi % 4):
                                                  512 * (gi % 4 + 1)],
                                        start=(g == 0), stop=(g == 31),
                                        skip_group_check=True)
                                tQK.inc(last)
                            tensor.wait_ge(vEVQ.h, tt + 1)
                            if tt >= 1:
                                tensor.wait_ge(vEVV.h, tt)
                            for j in range(4):
                                last = tensor.transpose(
                                    out=ps_vt[:, 128 * j:128 * (j + 1)],
                                    in_=v_stage[:, 128 * j:128 * (j + 1)],
                                    identity=ident[:, :])
                            tTRV.inc(last)
                        phase_end_c(tensor, tTRV)

                    @blk.scalar
                    def _(scalar):
                        phase_start(scalar)
                        phase_end_q(scalar)

                    @blk.vector
                    def _(vector):
                        phase_start(vector)
                        last = None
                        for tt in range(NQT):
                            vector.wait_ge(tQK.h, 32 * (tt + 1))
                            if tt >= 1:
                                vector.wait_ge(sWQS.h, 16 * 4 * tt)
                            for f in range(4):
                                vector.tensor_copy(
                                    out=q_stage[:, 512 * f:512 * (f + 1)],
                                    in_=ps_qkv[:, 512 * f:512 * (f + 1)])
                            vector.tensor_copy(
                                out=k_fm[:, 512 * tt:512 * (tt + 1)],
                                in_=ps_qkv[:, 512 * 4:512 * 5])
                            vEVQ.inc(vector.tensor_copy(
                                out=v_stage[:, :], in_=ps_qkv[:, 512 * 5:512 * 6]))
                            vector.wait_ge(tTRV.h, tt + 1)
                            last = vector.tensor_copy(
                                out=v_tm[:, 512 * tt:512 * (tt + 1)],
                                in_=ps_vt[:, :])
                            vEVV.inc(last)
                        phase_end_c(vector, vEVV)

                PH[0] += 1
                assert sPH.n == 5 * PH[0], sPH.n

                if True:
                    with ExitStack() as p3:
                        q_t = sbt(p3, "q_t", [128, 2 * 512], BF16, side="right")
                        p_sb = sbt(p3, "p_sb", [128, 4 * 512], BF16, side="right")
                        pm_sb = sbt(p3, "pm_sb", [128, 4 * 512], BF16, side="right")
                        o_raw = sbt(p3, "o_raw", [128, 2 * 512], F32, side="right")
                        o_stage = sbt(p3, "o_stage", [128, 2 * 512], BF16,
                                      side="right")
                        linv = sbt(p3, "linv", [1, 512], F32, side="right")
                        ps_sc = pst(p3, "ps_sc", [128, 4 * 512], F32)
                        ps_po = pst(p3, "ps_po", [128, 2 * 512], F32)
                        ps_pl = pst(p3, "ps_pl", [1, 512], F32)
                        ps_plb = pst(p3, "ps_plb", [128, 512], F32)

                        @blk.sync
                        def _(sync):
                            phase_start(sync)
                            def o_store(sidx):
                                hh, qq = divmod(sidx, NQT)
                                sync.wait_ge(vON.h, sidx + 1)
                                sWOSp[sidx % 2].inc(sync.dma_start(
                                    out=o_scr[128 * hh:128 * (hh + 1),
                                              512 * qq:512 * (qq + 1)],
                                    in_=o_stage[:, 512 * (sidx % 2):
                                                512 * (sidx % 2 + 1)]), 16)

                            for idx in range(64):
                                if idx >= 2:
                                    sync.wait_ge(tSC.h, PB_CUM[idx - 1])
                                h, qt = divmod(idx, NQT)
                                sLQp[idx % 2].inc(sync.dma_start(
                                    out=q_t[:, 512 * (idx % 2):512 * (idx % 2 + 1)],
                                    in_=q_scr[128 * h:128 * (h + 1),
                                              512 * qt:512 * (qt + 1)]), 16)
                                if idx >= 2:
                                    o_store(idx - 2)
                            o_store(62)
                            o_store(63)
                            for s_ in (sLQp[0], sLQp[1], sWOSp[0], sWOSp[1]):
                                sync.wait_ge(s_.h, s_.n)
                            phase_end_q(sync)

                        @blk.gpsimd
                        def _(gpsimd):
                            phase_start(gpsimd)
                            phase_end_q(gpsimd)

                        @blk.tensor
                        def _(tensor):
                            phase_start(tensor)
                            last = None
                            for idx in range(64):
                                h, qt = divmod(idx, NQT)
                                n = NKB[qt]
                                K0 = KB_CUM[idx]
                                bwait(tensor, sLQp, idx, 16)
                                if idx >= 1:
                                    tensor.wait_ge(aOE.h, idx - 1)
                                    tensor.wait_ge(vRC.h, idx)
                                qs = 512 * (idx % 2)

                                np_ = n // 2
                                P0_ = PB_CUM[idx]

                                def score_pair(p):
                                    pp = P0_ + p
                                    sp = pp % 2
                                    if pp >= 2:
                                        tensor.wait_ge(aEX.h, pp - 1)
                                    for u in range(2):
                                        j = 2 * p + u
                                        ins = tensor.matmul(
                                            out=ps_sc[:, 1024 * sp + 512 * u:
                                                      1024 * sp + 512 * (u + 1)],
                                            lhsT=k_fm[:, 128 * j:128 * (j + 1)],
                                            rhs=q_t[:, qs:qs + 512],
                                            start=True, stop=True,
                                            skip_group_check=True)
                                    tSC.inc(ins)

                                score_pair(0)
                                if np_ > 1:
                                    score_pair(1)
                                for p in range(np_):
                                    pp = P0_ + p
                                    sp = pp % 2
                                    if p + 2 < np_:
                                        score_pair(p + 2)
                                    tensor.wait_ge(vMK.h, pp + 1)
                                    for u in range(2):
                                        j = 2 * p + u
                                        kk = K0 + j
                                        tensor.matmul(
                                            out=ps_po[:, 512 * (idx % 2):
                                                      512 * (idx % 2 + 1)],
                                            lhsT=v_tm[:, 128 * j:128 * (j + 1)],
                                            rhs=pm_sb[:, 1024 * sp + 512 * u:
                                                      1024 * sp + 512 * (u + 1)],
                                            start=(j == 0), stop=(j == n - 1),
                                            skip_group_check=True)
                                        last = tensor.matmul(
                                            out=ps_pl[0:1, :],
                                            lhsT=ones_colb[:, :],
                                            rhs=pm_sb[:, 1024 * sp + 512 * u:
                                                      1024 * sp + 512 * (u + 1)],
                                            start=(j == 0), stop=(j == n - 1),
                                            skip_group_check=True)
                                        tPV.inc(last)
                                tensor.wait_ge(vRC.h, idx + 1)
                                if idx >= 1:
                                    tensor.wait_ge(vON.h, idx)
                                last = tensor.matmul(
                                    out=ps_plb[:, :],
                                    lhsT=ones_rowf[:, :], rhs=linv[0:1, :],
                                    start=True, stop=True,
                                    skip_group_check=True)
                                tLB.inc(last)
                            phase_end_c(tensor, tLB)

                        @blk.scalar
                        def _(scalar):
                            phase_start(scalar)
                            last = None
                            for idx in range(64):
                                h, qt = divmod(idx, NQT)
                                n = NKB[qt]
                                K0 = KB_CUM[idx]
                                np_ = n // 2
                                P0_ = PB_CUM[idx]
                                for p in range(np_):
                                    pp = P0_ + p
                                    sp = pp % 2
                                    scalar.wait_ge(tSC.h, pp + 1)
                                    if pp >= 2:
                                        scalar.wait_ge(vMK.h, pp - 1)
                                    aEX.inc(scalar.activation(
                                        out=p_sb[:, 1024 * sp:1024 * (sp + 1)],
                                        in_=ps_sc[:, 1024 * sp:1024 * (sp + 1)],
                                        func=AF.Exp, scale=SCALE,
                                        bias=bias_c[:, :]))
                                scalar.wait_ge(tPV.h, K0 + n)
                                if idx >= 2:
                                    scalar.wait_ge(vON.h, idx - 1)
                                last = scalar.activation(
                                    out=o_raw[:, 512 * (idx % 2):
                                              512 * (idx % 2 + 1)],
                                    in_=ps_po[:, 512 * (idx % 2):
                                              512 * (idx % 2 + 1)],
                                    func=AF.Copy)
                                aOE.inc(last)
                            phase_end_c(scalar, aOE)

                        @blk.vector
                        def _(vector):
                            phase_start(vector)
                            last = None
                            for idx in range(64):
                                h, qt = divmod(idx, NQT)
                                n = NKB[qt]
                                K0 = KB_CUM[idx]
                                np_ = n // 2
                                P0_ = PB_CUM[idx]
                                for p in range(np_):
                                    pp = P0_ + p
                                    sp = pp % 2
                                    vector.wait_ge(aEX.h, pp + 1)
                                    if pp >= 2:
                                        vector.wait_ge(tPV.h, 2 * (pp - 1))
                                    ins = None
                                    for u in range(2):
                                        j = 2 * p + u
                                        u0 = 512 * qt - 128 * j + 384
                                        ins = vector.tensor_mul(
                                            out=pm_sb[:, 1024 * sp + 512 * u:
                                                      1024 * sp + 512 * (u + 1)],
                                            in0=p_sb[:, 1024 * sp + 512 * u:
                                                     1024 * sp + 512 * (u + 1)],
                                            in1=e_sb[:, EW * h + u0:
                                                     EW * h + u0 + 512])
                                    vMK.inc(ins)
                                vector.wait_ge(tPV.h, K0 + n)
                                vector.wait_ge(tLB.h, idx)
                                vRC.inc(vector.reciprocal(out=linv[0:1, :],
                                                          in_=ps_pl[0:1, :]))
                                vector.wait_ge(tLB.h, idx + 1)
                                vector.wait_ge(aOE.h, idx + 1)
                                if idx >= 2:
                                    bwait(vector, sWOSp, idx - 2, 16)
                                last = vector.tensor_mul(
                                    out=o_stage[:, 512 * (idx % 2):
                                                512 * (idx % 2 + 1)],
                                    in0=o_raw[:, 512 * (idx % 2):
                                              512 * (idx % 2 + 1)],
                                    in1=ps_plb[:, :])
                                vON.inc(last)
                            phase_end_c(vector, vON)

                    PH[0] += 1
                    assert sPH.n == 5 * PH[0], sPH.n

                    with ExitStack() as p4:
                        wo_sb = sbt(p4, "wo_sb", [128, 2 * 4 * 512], BF16,
                                    side="right")
                        o_rc = sbt(p4, "o_rc", [128, 2 * 4 * 1024], BF16,
                                   side="right")
                        s_stage = sbt(p4, "s_stage", [128, 3 * 512], F32,
                                      side="right")
                        ps_s = pst(p4, "ps_s", [128, 3 * 512], F32)

                        @blk.gpsimd
                        def _(gpsimd):
                            phase_start(gpsimd)
                            phase_end_q(gpsimd)

                        @blk.sync
                        def _(sync):
                            phase_start(sync)
                            for rc in range(8):
                                if rc >= 2:
                                    sync.wait_ge(tS4.h, 64 * (rc - 1))
                                for h in range(4):
                                    sLORp[rc % 2].inc(sync.dma_start(
                                        out=o_rc[:, 4096 * (rc % 2) + 1024 * h:
                                                 4096 * (rc % 2)
                                                 + 1024 * (h + 1)],
                                        in_=o_scr[128 * h:128 * (h + 1),
                                                  1024 * rc:1024 * (rc + 1)]), 16)
                                for r1 in range(64):
                                    gtb = 64 * rc + r1
                                    dm8, tb = divmod(r1, 8)
                                    sync.wait_ge(aSE.h, gtb + 1)
                                    if gtb >= 3:
                                        sync.wait_ge(sWSPp[gtb % 3].h,
                                                     16 * ((gtb - 3) // 3 + 1))
                                    sWSPp[gtb % 3].inc(sync.dma_start(
                                        out=s_part[rc][128 * tb:128 * (tb + 1),
                                                       512 * dm8:512 * (dm8 + 1)],
                                        in_=s_stage[:, 512 * (gtb % 3):
                                                    512 * (gtb % 3 + 1)]), 16)
                            for s_ in (sLORp[0], sLORp[1], sWSPp[0], sWSPp[1],
                                       sWSPp[2]):
                                sync.wait_ge(s_.h, s_.n)
                            phase_end_q(sync)

                        @blk.tensor
                        def _(tensor):
                            phase_start(tensor)
                            last = None
                            for rc in range(8):
                                bwait(tensor, sLORp, rc, 64)
                                for dm8 in range(8):
                                    gi = 8 * rc + dm8
                                    bwait(tensor, sLWOp, gi, 64)
                                    for tb in range(8):
                                        gtb = 8 * gi + tb
                                        if gtb >= 3:
                                            tensor.wait_ge(aSE.h, gtb - 2)
                                        for h in range(4):
                                            last = tensor.matmul(
                                                out=ps_s[:, 512 * (gtb % 3):
                                                         512 * (gtb % 3 + 1)],
                                                lhsT=o_rc[:, 4096 * (rc % 2)
                                                          + 1024 * h + 128 * tb:
                                                          4096 * (rc % 2)
                                                          + 1024 * h
                                                          + 128 * (tb + 1)],
                                                rhs=wo_sb[:, 2048 * (gi % 2)
                                                          + 512 * h:
                                                          2048 * (gi % 2)
                                                          + 512 * (h + 1)],
                                                start=(h == 0), stop=(h == 3),
                                                skip_group_check=True)
                                        tS4.inc(last)
                            phase_end_c(tensor, tS4)

                        @blk.scalar
                        def _(scalar):
                            phase_start(scalar)

                            def wo_load(gi):
                                dm8 = gi % 8
                                if gi >= 2:
                                    scalar.wait_ge(tS4.h, 8 * (gi - 1))
                                for h in range(4):
                                    sLWOp[gi % 2].inc(scalar.dma_start(
                                        out=wo_sb[:, 2048 * (gi % 2) + 512 * h:
                                                  2048 * (gi % 2) + 512 * (h + 1)],
                                        in_=wo_bf[128 * h:128 * (h + 1),
                                                  512 * dm8:512 * (dm8 + 1)]), 16)

                            wo_load(0)
                            wo_load(1)
                            last = None
                            for gtb in range(512):
                                scalar.wait_ge(tS4.h, gtb + 1)
                                if gtb >= 3:
                                    scalar.wait_ge(sWSPp[gtb % 3].h,
                                                   16 * ((gtb - 3) // 3 + 1))
                                last = scalar.activation(
                                    out=s_stage[:, 512 * (gtb % 3):
                                                512 * (gtb % 3 + 1)],
                                    in_=ps_s[:, 512 * (gtb % 3):
                                             512 * (gtb % 3 + 1)],
                                    func=AF.Copy)
                                aSE.inc(last)
                                if gtb % 8 == 7 and gtb // 8 + 2 < 64:
                                    wo_load(gtb // 8 + 2)
                            scalar.wait_ge(sLWOp[0].h, sLWOp[0].n)
                            scalar.wait_ge(sLWOp[1].h, sLWOp[1].n)
                            phase_end_c(scalar, aSE)

                        @blk.vector
                        def _(vector):
                            phase_start(vector)
                            phase_end_q(vector)

                    PH[0] += 1
                    assert sPH.n == 5 * PH[0], sPH.n

        NP5 = 8
        IQ6 = [2, 2, 2, 2, 2, 1]
        IS6 = [0, 2, 4, 6, 8, 10]
        P5_SEQ = [(j, hf, IS6[j] + itl) for j in range(6)
                  for hf in range(2) for itl in range(IQ6[j])]
        assert len(P5_SEQ) == 22

        def p5_cons(gc):
            pc, jc = divmod(gc, 6)
            return 22 * pc + 2 * (IS6[jc] + IQ6[jc])

        with ExitStack() as p5:
            xn_f = sbt(p5, "xn_f", [128, 2 * 32 * 512], BF16, side="left")
            wgu_q = sbt(p5, "wgu_q", [128, 2 * 2 * 32 * 256], BF16, side="left")
            wd_sb = sbt(p5, "wd_sb", [128, 3 * IC * 512], BF16, side="left")
            h_sb = sbt(p5, "h_sb", [128, IC * 1024], BF16, side="left")
            sg_sb = sbt(p5, "sg_sb", [128, 2 * 512], BF16, side="left")
            su_sb = sbt(p5, "su_sb", [128, 2 * 512], BF16, side="left")
            f_stage = sbt(p5, "f_stage", [128, 2 * 512], F32, side="left")
            ps_g = pst(p5, "ps_g", [128, 2 * 512], F32)
            ps_u = pst(p5, "ps_u", [128, 2 * 512], F32)
            ps_f = pst(p5, "ps_f", [128, 2 * 512], F32)

            WQ2 = 256

            def gu_sl(buf, w, g, lo, width, nit):
                base = (2 * buf + w) * 32 * WQ2 + g * (128 * nit) + lo
                return wgu_q[:, base:base + width]

            @blk.sync
            def _(sync):
                phase_start(sync)

                def wg_half(gc):
                    if gc >= 6 * NP5:
                        return
                    j = gc % 6
                    nit = IQ6[j]
                    i0 = IS6[j]
                    if gc >= 2:
                        sync.wait_ge(tGU.h, p5_cons(gc - 2))
                    sLWGp[gc % 2].inc(sync.dma_start(
                        out=gu_sl(gc % 2, 0, 0, 0, 32 * 128 * nit, nit),
                        in_=bass.AP(wg_dq.tensor, 128 * i0,
                                    [[1408, 128], [128 * 1408, 32],
                                     [1, 128 * nit]])), 16)

                def xn_load(p):
                    if p >= NP5:
                        return
                    if p >= 1:
                        sync.wait_ge(tGU.h, 22 * p)
                    for hf in range(2):
                        sync.wait_ge(cAGs[hf].h, 1)
                        sLXF.inc(sync.dma_start(
                            out=xn_f[:, 32 * 512 * hf:32 * 512 * (hf + 1)],
                            in_=bass.AP(xn_g[hf].tensor, DM * p * 512,
                                        [[512, 128], [128 * 512, 32],
                                         [1, 512]])), 16)

                xn_load(0)
                wg_half(0)
                wg_half(1)
                for p in range(NP5):
                    wg_half(6 * p + 2)
                    wg_half(6 * p + 3)
                    wg_half(6 * p + 4)
                    wg_half(6 * p + 5)
                    wg_half(6 * p + 6)
                    wg_half(6 * p + 7)
                    xn_load(p + 1)
                    for ftl in range(64):
                        q8, tb2 = divmod(ftl, 8)
                        ftb = 64 * p + ftl
                        sync.wait_ge(vFE.h, ftb + 1)
                        sWFPp[ftb % 2].inc(sync.dma_start(
                            out=f_part[p][128 * tb2:128 * (tb2 + 1),
                                          512 * q8:512 * (q8 + 1)],
                            in_=f_stage[:, 512 * (ftb % 2):
                                        512 * (ftb % 2 + 1)]), 16)
                for s_ in (sWFPp[0], sWFPp[1], sLWGp[0], sLWGp[1], sLXF):
                    sync.wait_ge(s_.h, s_.n)
                phase_end_q(sync)

            @blk.gpsimd
            def _(gpsimd):
                phase_start(gpsimd)
                for p in range(NP5):
                    for _p, _sw in enumerate(sWSPp):
                        _cnt = sum(1 for _g in range(64 * (p + 1))
                                   if _g % 3 == _p)
                        gpsimd.wait_ge(_sw.h, 16 * _cnt)
                    cS[p].inc(gpsimd.collective_compute(
                        "ReduceScatter", ALU.add, replica_groups=groups,
                        ins=[s_part[p].opt()], outs=[s_ownr[p].opt()]))
                    if p >= 1:
                        gpsimd.wait_ge(sWFPp[0].h, 16 * 32 * p)
                        gpsimd.wait_ge(sWFPp[1].h, 16 * 32 * p)
                        cF[p - 1].inc(gpsimd.collective_compute(
                            "ReduceScatter", ALU.add, replica_groups=groups,
                            ins=[f_part[p - 1].opt()],
                            outs=[f_ownr[p - 1].opt()]))
                gpsimd.wait_ge(sWFPp[0].h, 16 * 32 * 8)
                gpsimd.wait_ge(sWFPp[1].h, 16 * 32 * 8)
                cF[7].inc(gpsimd.collective_compute(
                    "ReduceScatter", ALU.add, replica_groups=groups,
                    ins=[f_part[7].opt()], outs=[f_ownr[7].opt()]))
                phase_end_q(gpsimd)

            @blk.tensor
            def _(tensor):
                phase_start(tensor)
                last = None
                for p in range(NP5):
                    tensor.wait_ge(sLXF.h, 16 * 2 * (p + 1))
                    gidx = 22 * p
                    for j in range(6):
                        gc = 6 * p + j
                        nit = IQ6[j]
                        i0 = IS6[j]
                        bwait(tensor, sLWGp, gc, 16 * 2)
                        for hf in range(2):
                            for itl in range(nit):
                                if gidx >= 2:
                                    tensor.wait_ge(vH.h, gidx - 1)
                                for g in range(32):
                                    tensor.matmul(
                                        out=ps_g[:, 512 * (gidx % 2):
                                                 512 * (gidx % 2 + 1)],
                                        lhsT=gu_sl(gc % 2, 0, g,
                                                   128 * itl, 128, nit),
                                        rhs=xn_f[:, 32 * 512 * hf + 512 * g:
                                                 32 * 512 * hf + 512 * (g + 1)],
                                        start=(g == 0), stop=(g == 31),
                                        skip_group_check=True)
                                    last = tensor.matmul(
                                        out=ps_u[:, 512 * (gidx % 2):
                                                 512 * (gidx % 2 + 1)],
                                        lhsT=gu_sl(gc % 2, 1, g,
                                                   128 * itl, 128, nit),
                                        rhs=xn_f[:, 32 * 512 * hf + 512 * g:
                                                 32 * 512 * hf + 512 * (g + 1)],
                                        start=(g == 0), stop=(g == 31),
                                        skip_group_check=True)
                                tGU.inc(last)
                                gidx += 1
                    tensor.wait_ge(vH.h, 22 * (p + 1))
                    for q8 in range(8):
                        gq8 = 8 * p + q8
                        tensor.wait_ge(sLWDp[gq8 % 3].h, 16 * (gq8 // 3 + 1))
                        for tb2 in range(8):
                            ftb = 64 * p + 8 * q8 + tb2
                            if ftb >= 2:
                                tensor.wait_ge(vFE.h, ftb - 1)
                            for it in range(IC):
                                last = tensor.matmul(
                                    out=ps_f[:, 512 * (ftb % 2):
                                             512 * (ftb % 2 + 1)],
                                    lhsT=h_sb[:, 1024 * it + 128 * tb2:
                                              1024 * it + 128 * (tb2 + 1)],
                                    rhs=wd_sb[:, IC * 512 * (gq8 % 3)
                                              + 512 * it:
                                              IC * 512 * (gq8 % 3)
                                              + 512 * (it + 1)],
                                    start=(it == 0), stop=(it == IC - 1),
                                    skip_group_check=True)
                            tDN.inc(last)
                phase_end_c(tensor, tDN)

            @blk.scalar
            def _(scalar):
                phase_start(scalar)
                scalar.preload_act_table(AF.Silu)

                def wu_half(gc):
                    if gc >= 6 * NP5:
                        return
                    j = gc % 6
                    nit = IQ6[j]
                    i0 = IS6[j]
                    if gc >= 2:
                        scalar.wait_ge(tGU.h, p5_cons(gc - 2))
                    sLWGp[gc % 2].inc(scalar.dma_start(
                        out=gu_sl(gc % 2, 1, 0, 0, 32 * 128 * nit, nit),
                        in_=bass.AP(wu_dq.tensor, 128 * i0,
                                    [[1408, 128], [128 * 1408, 32],
                                     [1, 128 * nit]])), 16)

                def wd_load(gq8):
                    if gq8 >= 16 * NP5 // 2:
                        return
                    if gq8 >= 3:
                        scalar.wait_ge(tDN.h, 8 * (gq8 - 2))
                    sLWDp[gq8 % 3].inc(scalar.dma_start(
                        out=wd_sb[:, IC * 512 * (gq8 % 3):
                                  IC * 512 * (gq8 % 3 + 1)],
                        in_=bass.AP(wd_dq.tensor, 512 * (gq8 % 8),
                                    [[DM, 128], [128 * DM, IC],
                                     [1, 512]])), 16)

                wu_half(0)
                wu_half(1)
                last = None
                for p in range(NP5):
                    for w_, (j, hf, it) in enumerate(P5_SEQ):
                        gidx = 22 * p + w_
                        scalar.wait_ge(tGU.h, gidx + 1)
                        if gidx >= 2:
                            scalar.wait_ge(vH.h, gidx - 1)
                        scalar.activation(
                            out=sg_sb[:, 512 * (gidx % 2):
                                      512 * (gidx % 2 + 1)],
                            in_=ps_g[:, 512 * (gidx % 2):
                                     512 * (gidx % 2 + 1)],
                            func=AF.Silu)
                        last = scalar.activation(
                            out=su_sb[:, 512 * (gidx % 2):
                                      512 * (gidx % 2 + 1)],
                            in_=ps_u[:, 512 * (gidx % 2):
                                     512 * (gidx % 2 + 1)],
                            func=AF.Copy)
                        aGU.inc(last)
                        if w_ == 0:
                            wd_load(8 * p)
                            wd_load(8 * p + 1)
                        elif w_ == 3:
                            wu_half(6 * p + 2)
                        elif w_ == 7:
                            wu_half(6 * p + 3)
                        elif w_ == 11:
                            wu_half(6 * p + 4)
                        elif w_ == 15:
                            wu_half(6 * p + 5)
                        elif w_ == 19:
                            wu_half(6 * p + 6)
                        elif w_ == 21:
                            wu_half(6 * p + 7)
                            for q8 in range(2, 8):
                                wd_load(8 * p + q8)
                phase_end_c(scalar, aGU)

            @blk.vector
            def _(vector):
                phase_start(vector)
                last = None
                for p in range(NP5):
                    for w_, (j, hf, it) in enumerate(P5_SEQ):
                        gidx = 22 * p + w_
                        vector.wait_ge(aGU.h, gidx + 1)
                        vH.inc(vector.tensor_mul(
                            out=h_sb[:, 1024 * it + 512 * hf:
                                     1024 * it + 512 * (hf + 1)],
                            in0=sg_sb[:, 512 * (gidx % 2):
                                      512 * (gidx % 2 + 1)],
                            in1=su_sb[:, 512 * (gidx % 2):
                                      512 * (gidx % 2 + 1)]))
                    for ftl in range(64):
                        ftb = 64 * p + ftl
                        vector.wait_ge(tDN.h, ftb + 1)
                        if ftb >= 2:
                            bwait(vector, sWFPp, ftb - 2, 16)
                        last = vector.tensor_copy(
                            out=f_stage[:, 512 * (ftb % 2):
                                        512 * (ftb % 2 + 1)],
                            in_=ps_f[:, 512 * (ftb % 2):
                                     512 * (ftb % 2 + 1)])
                        vFE.inc(last)
                phase_end_c(vector, vFE)

        PH[0] += 1
        assert sPH.n == 5 * PH[0], sPH.n

        with ExitStack() as p6:
            s_t6 = sbt(p6, "s_t6", [128, 2 * DM], F32, side="left")
            f_t6 = sbt(p6, "f_t6", [128, 2 * DM], F32, side="left")
            x_t6 = sbt(p6, "x_t6", [128, 2 * DM], F32, side="left")
            g_t6 = sbt(p6, "g_t6", [128, 2 * DM], F32, side="left")
            sg_t6 = sbt(p6, "sg_t6", [128, 2 * DM], F32, side="left")
            out_t6 = sbt(p6, "out_t6", [128, 2 * DM], F32, side="left")

            @blk.sync
            def _(sync):
                phase_start(sync)
                def st6(rc):
                    b = DM * (rc % 2)
                    sync.wait_ge(vOUT.h, rc + 1)
                    sWOUTp[rc % 2].inc(sync.dma_start(
                        out=out_own[128 * rc:128 * (rc + 1), :],
                        in_=out_t6[:, b:b + DM]), 16)

                for rc in range(8):
                    b = DM * (rc % 2)
                    sync.wait_ge(cS[rc].h, 1)
                    sync.wait_ge(cF[rc].h, 1)
                    if rc >= 2:
                        sync.wait_ge(vOUT.h, rc - 1)
                    sLSFp[rc % 2].inc(sync.dma_start(out=s_t6[:, b:b + DM],
                                                     in_=s_ownr[rc]), 16)
                    sLSFp[rc % 2].inc(sync.dma_start(out=f_t6[:, b:b + DM],
                                                     in_=f_ownr[rc]), 16)
                    sLSFp[rc % 2].inc(sync.dma_start(
                        out=x_t6[:, b:b + DM],
                        in_=x_sc[128 * rc:128 * (rc + 1), :]), 16)
                    if rc >= 2:
                        st6(rc - 2)
                st6(6)
                st6(7)
                for name, dst in dbg_outs.items():
                    sWOUTp[0].inc(sync.dma_start(out=dst, in_=dbg_srcs[name]), 16)
                sync.wait_ge(sWOUTp[0].h, sWOUTp[0].n)
                sync.wait_ge(sWOUTp[1].h, sWOUTp[1].n)

            @blk.gpsimd
            def _(gpsimd):
                phase_start(gpsimd)

            @blk.tensor
            def _(tensor):
                phase_start(tensor)

            @blk.scalar
            def _(scalar):
                phase_start(scalar)
                for rc in range(8):
                    b = DM * (rc % 2)
                    scalar.wait_ge(sLSFp[rc % 2].h, 48 * (rc // 2 + 1))
                    if rc >= 2:
                        scalar.wait_ge(vOUT.h, rc - 1)
                    aG6.inc(scalar.activation(out=g_t6[:, b:b + DM],
                                              in_=f_t6[:, b:b + DM],
                                              func=AF.Silu))

            @blk.vector
            def _(vector):
                phase_start(vector)
                for rc in range(8):
                    b = DM * (rc % 2)
                    vector.wait_ge(aG6.h, rc + 1)
                    vSELF.inc(vector.tensor_mul(out=sg_t6[:, b:b + DM],
                                                in0=s_t6[:, b:b + DM],
                                                in1=g_t6[:, b:b + DM]))
                    vector.wait_ge(vSELF.h, vSELF.n)
                    if rc >= 2:
                        vector.wait_ge(sWOUTp[rc % 2].h, 16 * (rc // 2))
                    vOUT.inc(vector.tensor_add(out=out_t6[:, b:b + DM],
                                               in0=sg_t6[:, b:b + DM],
                                               in1=x_t6[:, b:b + DM]))

    return nc


def _alibi_slopes(n):
    cp2 = 2 ** math.floor(math.log2(n))
    base = 2.0 ** (-8.0 / cp2)
    slopes = base ** np.arange(1, cp2 + 1, dtype=np.float32)
    if cp2 != n:
        extra_base = 2.0 ** (-4.0 / cp2)
        extra = extra_base ** np.arange(1, 1 + 2 * (n - cp2), 2, dtype=np.float32)
        slopes = np.concatenate([slopes, extra])
    return slopes.astype(np.float32)


def make_in_maps(x_full, w_qkv, w_o, gqw, gsc, gqz, uqw, usc, uqz, dqw, dsc, dqz):
    x = np.ascontiguousarray(np.asarray(x_full, dtype=np.float32).reshape(S, DM))
    sl = _alibi_slopes(32)
    x4 = x.reshape(8, 8, 128, DM)
    in_maps = []
    for c in range(NCORE):
        g0, gn = G0S[c], GROUPS[c]
        m = {}
        m["x_own"] = np.ascontiguousarray(x[1024 * c:1024 * (c + 1)])
        m["x_sc"] = np.ascontiguousarray(x4[:, c].reshape(1024, DM))
        m["wqkv"] = np.ascontiguousarray(np.concatenate(
            [w_qkv[:, 512 * c:512 * (c + 1)],
             w_qkv[:, 4096 + 128 * c:4096 + 128 * (c + 1)],
             w_qkv[:, 5120 + 128 * c:5120 + 128 * (c + 1)]], axis=1))
        m["wo"] = np.ascontiguousarray(w_o[512 * c:512 * (c + 1)])
        m["slopes"] = np.ascontiguousarray(sl[4 * c:4 * (c + 1)].reshape(1, 4))
        gw = np.zeros((DM, 176), np.int32)
        gw[:, :16 * gn] = gqw[:, 16 * g0:16 * (g0 + gn)]
        m["g_qw"] = gw
        gz = np.zeros((32, 176), np.int32)
        gz[:, :16 * gn] = gqz[:, 16 * g0:16 * (g0 + gn)]
        m["g_qz"] = gz
        gs = np.zeros((32, 1408), np.float32)
        gs[:, :128 * gn] = gsc[:, 128 * g0:128 * (g0 + gn)]
        m["g_sc"] = gs
        uw = np.zeros((DM, 176), np.int32)
        uw[:, :16 * gn] = uqw[:, 16 * g0:16 * (g0 + gn)]
        m["u_qw"] = uw
        uz = np.zeros((32, 176), np.int32)
        uz[:, :16 * gn] = uqz[:, 16 * g0:16 * (g0 + gn)]
        m["u_qz"] = uz
        us = np.zeros((32, 1408), np.float32)
        us[:, :128 * gn] = usc[:, 128 * g0:128 * (g0 + gn)]
        m["u_sc"] = us
        dw = np.zeros((1408, 512), np.int32)
        dw[:128 * gn] = dqw[128 * g0:128 * (g0 + gn)]
        m["d_qw"] = dw
        dz = np.zeros((IC, 512), np.int32)
        dz[:gn] = dqz[g0:g0 + gn]
        m["d_qz"] = dz
        ds = np.zeros((IC, DM), np.float32)
        ds[:gn] = dsc[g0:g0 + gn]
        m["d_sc"] = ds
        in_maps.append(m)
    return in_maps


def kernel(L0_attn_hidden_states, L0_attn_w_qkv, L0_attn_w_o,
           L0_ffn_gate_qweight, L0_ffn_gate_scales, L0_ffn_gate_qzeros,
           L0_ffn_up_qweight, L0_ffn_up_scales, L0_ffn_up_qzeros,
           L0_ffn_down_qweight, L0_ffn_down_scales, L0_ffn_down_qzeros):
    in_maps = make_in_maps(
        np.asarray(L0_attn_hidden_states, dtype=np.float32),
        np.asarray(L0_attn_w_qkv, dtype=np.float32),
        np.asarray(L0_attn_w_o, dtype=np.float32),
        np.asarray(L0_ffn_gate_qweight, dtype=np.int32),
        np.asarray(L0_ffn_gate_scales, dtype=np.float32),
        np.asarray(L0_ffn_gate_qzeros, dtype=np.int32),
        np.asarray(L0_ffn_up_qweight, dtype=np.int32),
        np.asarray(L0_ffn_up_scales, dtype=np.float32),
        np.asarray(L0_ffn_up_qzeros, dtype=np.int32),
        np.asarray(L0_ffn_down_qweight, dtype=np.int32),
        np.asarray(L0_ffn_down_scales, dtype=np.float32),
        np.asarray(L0_ffn_down_qzeros, dtype=np.int32))

    nc = build_program()
    res = run_bass_kernel_spmd(nc, in_maps, core_ids=list(range(NCORE)))

    out = np.empty((8, 8, 128, DM), dtype=np.float32)
    for c in range(NCORE):
        out[:, c] = res.results[c]["out_own"].reshape(8, 128, DM)
    return out.reshape(1, S, DM)


# version 14
# speedup vs baseline: 1.0330x; 1.0032x over previous
import math
import numpy as np
from contextlib import ExitStack

import concourse.bass as bass
import concourse.mybir as mybir
from concourse.bass_utils import run_bass_kernel_spmd

F32 = mybir.dt.float32
BF16 = mybir.dt.bfloat16
I32 = mybir.dt.int32
AF = mybir.ActivationFunctionType
ALU = mybir.AluOpType
AX = mybir.AxisListType

NCORE = 8
S = 8192
DM = 4096
TOWN = 1024
NQT = 16
SCALE = 1.0 / math.sqrt(128.0)
CSHIFT = 12.0
IC = 11
GROUPS = [11, 11, 11, 11, 11, 11, 10, 10]
G0S = [0, 11, 22, 33, 44, 55, 66, 76]
EW = 8576
EL = 8704
NSB = 16
IQ_CHUNKS = [3, 3, 3, 2]
IQ_STARTS = [0, 3, 6, 9]
BIGNEG = -6.6e8


class Sem:
    __slots__ = ("h", "n")

    def __init__(self, ctx, nc, name):
        self.h = ctx.enter_context(nc.semaphore(name))
        self.n = 0

    def inc(self, ins, k=1):
        ins.then_inc(self.h, k)
        self.n += k
        return ins


def build_program(dbg=(), upto=6):
    nc = bass.Bass("TRN2", target_bir_lowering=False)

    x_own = nc.dram_tensor("x_own", [TOWN, DM], F32, kind="ExternalInput").ap()
    x_sc = nc.dram_tensor("x_sc", [TOWN, DM], F32, kind="ExternalInput").ap()
    wqkv = nc.dram_tensor("wqkv", [DM, 768], F32, kind="ExternalInput").ap()
    wo = nc.dram_tensor("wo", [512, DM], F32, kind="ExternalInput").ap()
    slopes = nc.dram_tensor("slopes", [1, 4], F32, kind="ExternalInput").ap()
    g_qw = nc.dram_tensor("g_qw", [DM, 176], I32, kind="ExternalInput").ap()
    g_qz = nc.dram_tensor("g_qz", [32, 176], I32, kind="ExternalInput").ap()
    g_sc = nc.dram_tensor("g_sc", [32, 1408], F32, kind="ExternalInput").ap()
    u_qw = nc.dram_tensor("u_qw", [DM, 176], I32, kind="ExternalInput").ap()
    u_qz = nc.dram_tensor("u_qz", [32, 176], I32, kind="ExternalInput").ap()
    u_sc = nc.dram_tensor("u_sc", [32, 1408], F32, kind="ExternalInput").ap()
    d_qw = nc.dram_tensor("d_qw", [1408, 512], I32, kind="ExternalInput").ap()
    d_qz = nc.dram_tensor("d_qz", [IC, 512], I32, kind="ExternalInput").ap()
    d_sc = nc.dram_tensor("d_sc", [IC, DM], F32, kind="ExternalInput").ap()

    out_own = nc.dram_tensor("out_own", [TOWN, DM], F32, kind="ExternalOutput").ap()

    cc_xn = [nc.dram_tensor(f"cc_xn{i}", [DM, 512], BF16, kind="Internal").ap()
             for i in range(2)]
    xn_g = [nc.dram_tensor(f"xn_g{i}", [NCORE * DM, 512], BF16,
                           kind="Internal").ap() for i in range(2)]
    q_scr = nc.dram_tensor("q_scr", [512, S], BF16, kind="Internal").ap()
    o_scr = nc.dram_tensor("o_scr", [512, S], BF16, kind="Internal").ap()
    wo_bf = nc.dram_tensor("wo_bf", [512, DM], BF16, kind="Internal").ap()
    e_lin = nc.dram_tensor("e_lin", [4, EL], BF16, kind="Internal").ap()
    zf_g = nc.dram_tensor("zf_g", [32, 1408], I32, kind="Internal").ap()
    zf_u = nc.dram_tensor("zf_u", [32, 1408], I32, kind="Internal").ap()
    zf_d = nc.dram_tensor("zf_d", [IC, DM], I32, kind="Internal").ap()
    wg_dq = nc.dram_tensor("wg_dq", [32, 128, 1408], BF16, kind="Internal").ap()
    wu_dq = nc.dram_tensor("wu_dq", [32, 128, 1408], BF16, kind="Internal").ap()
    wd_dq = nc.dram_tensor("wd_dq", [IC, 128, DM], BF16, kind="Internal").ap()
    s_part = [nc.dram_tensor(f"s_part{r}", [1024, DM], F32, kind="Internal").ap()
              for r in range(8)]
    f_part = [nc.dram_tensor(f"f_part{r}", [1024, DM], F32, kind="Internal").ap()
              for r in range(8)]
    s_ownr = [nc.dram_tensor(f"s_ownr{r}", [128, DM], F32,
                             kind="Internal").ap() for r in range(8)]
    f_ownr = [nc.dram_tensor(f"f_ownr{r}", [128, DM], F32,
                             kind="Internal").ap() for r in range(8)]

    dbg_outs = {}
    dbg_srcs = {"xn_g0": xn_g[0], "xn_g1": xn_g[1], "q_scr": q_scr, "o_scr": o_scr,
                "e_lin": e_lin, "wg_dq": wg_dq, "wd_dq": wd_dq,
                "s_part0": s_part[0], "f_part0": f_part[0],
                "s_ownr0": s_ownr[0], "f_ownr0": f_ownr[0]}
    for name in dbg:
        src = dbg_srcs[name]
        dbg_outs[name] = nc.dram_tensor(
            "dbg_" + name, list(src.shape), src.dtype, kind="ExternalOutput").ap()

    groups = [list(range(NCORE))]

    NKB = [4 * (qt + 1) for qt in range(NQT)]
    KB_CUM = [0]
    PB_CUM = [0]
    for h in range(4):
        for qt in range(NQT):
            KB_CUM.append(KB_CUM[-1] + NKB[qt])
            PB_CUM.append(PB_CUM[-1] + NKB[qt] // 2)

    with ExitStack() as top:
        def sem(name):
            return Sem(top, nc, name)

        sPH = sem("sPH")
        tSELF = sem("tSELF"); aSELF = sem("aSELF"); vSELF = sem("vSELF")
        gSELF = sem("gSELF")
        sLX = sem("sLX"); sLX2 = sem("sLX2")
        aSQ = sem("aSQ"); vST = sem("vST"); aST = sem("aST"); aXN = sem("aXN")
        tTR = sem("tTR"); vTS = sem("vTS")
        sWXNp = [sem("sWXNa"), sem("sWXNb")]
        sLQW = sem("sLQW")
        sLDQz = sem("sLDQz"); vDQz = sem("vDQz")
        sLDQp = [sem("sLDQa"), sem("sLDQb")]
        sWDQp = [sem("sWDQa"), sem("sWDQb")]
        vDQs = sem("vDQs"); vDQm = sem("vDQm"); tDQ = sem("tDQ")
        sSF = sem("sSF"); vDQb = sem("vDQb"); sZS = sem("sZS")
        gEB = sem("gEB"); vEB = sem("vEB"); aEB = sem("aEB"); sEB = sem("sEB")
        cAGs = [sem("cAG0"), sem("cAG1")]
        sLXN4 = [sem(f"sLXN{i}") for i in range(4)]
        tQK = sem("tQK"); vEVQ = sem("vEVQ")
        tTRV = sem("tTRV"); vEVV = sem("vEVV"); sWQS = sem("sWQS")
        sEB2 = sem("sEB2"); sEB3 = sem("sEB3")
        sLQp = [sem("sLQa"), sem("sLQb")]
        tSC = sem("tSC"); aEX = sem("aEX"); vMK = sem("vMK")
        tPV = sem("tPV"); aOE = sem("aOE"); vRC = sem("vRC"); tLB = sem("tLB")
        vON = sem("vON"); sWOSp = [sem("sWOSa"), sem("sWOSb")]
        sLWOp = [sem("sLWOa"), sem("sLWOb")]
        sLORp = [sem("sLORa"), sem("sLORb")]
        tS4 = sem("tS4"); aSE = sem("aSE")
        sWSPp = [sem(f"sWSP{i}") for i in range(3)]
        cS = [sem(f"cS{r}") for r in range(8)]
        sLXF = sem("sLXF")
        sLWGp = [sem("sLWGa"), sem("sLWGb")]
        sLWDp = [sem(f"sLWD{i}") for i in range(3)]
        tGU = sem("tGU"); aGU = sem("aGU"); vH = sem("vH")
        tDN = sem("tDN"); vFE = sem("vFE")
        sWFPp = [sem("sWFPa"), sem("sWFPb")]
        cF = [sem(f"cF{r}") for r in range(8)]
        sLSFp = [sem("sLSFa"), sem("sLSFb")]
        aG6 = sem("aG6"); vOUT = sem("vOUT")
        sWOUTp = [sem("sWOUTa"), sem("sWOUTb")]

        def sbt(ctx, name, shape, dtype, side="left"):
            return ctx.enter_context(nc.sbuf_tensor(name, shape, dtype, side=side))

        def pst(ctx, name, shape, dtype):
            return ctx.enter_context(nc.psum_tensor(name, shape, dtype))

        ones_rowf = sbt(top, "ones_rowf", [1, 128], F32)
        ones_colb = sbt(top, "ones_colb", [128, 1], BF16)
        bias_c = sbt(top, "bias_c", [128, 1], F32)
        eps_b = sbt(top, "eps_b", [128, 1], F32)
        ident = sbt(top, "ident", [128, 128], BF16)
        slopes_ng = sbt(top, "slopes_ng", [128, 4], F32)
        slopes_rep = sbt(top, "slopes_rep", [128, 4], F32)

        PH = [0]

        def phase_start(eng):
            if PH[0] > 0:
                eng.wait_ge(sPH.h, 5 * PH[0])

        def phase_end_q(eng):
            sPH.n += 1
            eng.nop().then_inc(sPH.h, 1)

        def phase_end_c(eng, last_sem):
            eng.wait_ge(last_sem.h, last_sem.n)
            sPH.n += 1
            eng.nop().then_inc(sPH.h, 1)

        def bwait(eng, sems, g, U):
            eng.wait_ge(sems[g % 2].h, U * (g // 2 + 1))

        blk = top.enter_context(nc.Block())

        with ExitStack() as scWQ:
            wqkv_sb = sbt(scWQ, "wqkv_sb", [128, 32 * 768], BF16, side="left")

            with ExitStack() as p0:
                x_t = sbt(p0, "x_t", [128, DM], F32, side="right")
                sq_t = sbt(p0, "sq_t", [128, DM], F32, side="right")
                msum = sbt(p0, "msum", [128, 8], F32, side="right")
                qsum = sbt(p0, "qsum", [128, 8], F32, side="right")
                mean8 = sbt(p0, "mean8", [128, 8], F32, side="right")
                eq8 = sbt(p0, "eq8", [128, 8], F32, side="right")
                m28 = sbt(p0, "m28", [128, 8], F32, side="right")
                var8 = sbt(p0, "var8", [128, 8], F32, side="right")
                std8 = sbt(p0, "std8", [128, 8], F32, side="right")
                rinv8 = sbt(p0, "rinv8", [128, 8], F32, side="right")
                bneg8 = sbt(p0, "bneg8", [128, 8], F32, side="right")
                tmp8 = sbt(p0, "tmp8", [128, 8], F32, side="right")
                xn_tm = sbt(p0, "xn_tm", [128, 2 * DM], BF16, side="right")
                xn_stage = sbt(p0, "xn_stage", [128, 2 * 512], BF16, side="right")
                qw_t = sbt(p0, "qw_t", [128, 2 * 512], I32, side="right")
                nib_t = sbt(p0, "nib_t", [128, 2048], I32, side="right")
                d1_t = sbt(p0, "d1_t", [128, 2048], F32, side="right")
                w_t = sbt(p0, "w_t", [128, 2 * 2048], BF16, side="right")
                gz_sb = sbt(p0, "gz_sb", [32, 176], I32, side="right")
                uz_sb = sbt(p0, "uz_sb", [32, 176], I32, side="right")
                dz_sb = sbt(p0, "dz_sb", [IC, 512], I32, side="right")
                zfg_sb = sbt(p0, "zfg_sb", [32, 1408], I32, side="right")
                zfu_sb = sbt(p0, "zfu_sb", [32, 1408], I32, side="right")
                zfd_sb = sbt(p0, "zfd_sb", [IC, DM], I32, side="right")
                zfg_bf = sbt(p0, "zfg_bf", [32, 1408], BF16, side="right")
                zfu_bf = sbt(p0, "zfu_bf", [32, 1408], BF16, side="right")
                zfd_bf = sbt(p0, "zfd_bf", [IC, DM], BF16, side="right")
                sfg_bf = sbt(p0, "sfg_bf", [32, 1408], BF16, side="right")
                sfu_bf = sbt(p0, "sfu_bf", [32, 1408], BF16, side="right")
                sfd_bf = sbt(p0, "sfd_bf", [IC, DM], BF16, side="right")
                ones_rowb = sbt(p0, "ones_rowb", [1, 128], BF16, side="right")
                z_stg = sbt(p0, "z_stg", [1, 2 * 1536], BF16, side="right")
                s_stg = sbt(p0, "s_stg", [1, 2 * 1536], BF16, side="right")
                ramp = sbt(p0, "ramp", [128, 68], F32, side="right")
                rmax = sbt(p0, "rmax", [128, 68], F32, side="right")
                rneg = sbt(p0, "rneg", [128, 68], F32, side="right")
                rmix = sbt(p0, "rmix", [128, 68], F32, side="right")
                sq_row = sbt(p0, "sq_row", [128, 68], F32, side="right")
                e_rowf = sbt(p0, "e_rowf", [128, 68], F32, side="right")
                e_rowb = sbt(p0, "e_rowb", [128, 68], BF16, side="right")
                ps_tr = pst(p0, "ps_tr", [128, 2 * 1024], BF16)
                ps_zb = pst(p0, "ps_zb", [128, 1536], F32)
                ps_sb5 = pst(p0, "ps_sb5", [128, 1536], F32)

                dq_jobs = []
                for g in range(32):
                    dq_jobs.append(("g", g, 176, 1408, 0))
                for g in range(32):
                    dq_jobs.append(("u", g, 176, 1408, 0))
                for it in range(IC):
                    for oh in range(4):
                        dq_jobs.append(("d", it, 128, 1024, oh))

                @blk.gpsimd
                def _(gpsimd):
                    for g in range(32):
                        sLQW.inc(gpsimd.dma_start(
                            out=wqkv_sb[:, 768 * g:768 * (g + 1)],
                            in_=wqkv[128 * g:128 * (g + 1), :]), 16)
                    sLQW.inc(gpsimd.dma_start(out=wo_bf, in_=wo), 16)
                    sSF.inc(gpsimd.dma_start(out=sfg_bf[:, :], in_=g_sc), 16)
                    sSF.inc(gpsimd.dma_start(out=sfu_bf[:, :], in_=u_sc), 16)
                    sSF.inc(gpsimd.dma_start(out=sfd_bf[:, :], in_=d_sc), 16)
                    gpsimd.memset(ident[:, :], 1.0)
                    gSELF.inc(gpsimd.iota(
                        ramp[:, :], [[1, 68]], base=-512, channel_multiplier=68,
                        allow_small_or_imprecise_dtypes=True))
                    gpsimd.wait_ge(gSELF.h, gSELF.n)
                    gEB.inc(gpsimd.affine_select(
                        out=ident[:, :], in_=ident[:, :],
                        compare_op=ALU.is_equal, fill=0.0, base=0,
                        pattern=[[-1, 128]], channel_multiplier=1))
                    for tb in range(8):
                        for fg in range(8):
                            idx = tb * 8 + fg
                            gpsimd.wait_ge(vTS.h, idx + 1)
                            for j in range(4):
                                fb = 4 * fg + j
                                sWXNp[idx % 2].inc(gpsimd.dma_start(
                                    out=cc_xn[tb // 4][
                                        128 * fb:128 * (fb + 1),
                                        128 * (tb % 4):128 * (tb % 4 + 1)],
                                    in_=xn_stage[:, 512 * (idx % 2) + 128 * j:
                                                 512 * (idx % 2) + 128 * (j + 1)]),
                                    16)
                    gpsimd.wait_ge(sWXNp[0].h, 64 * 16)
                    gpsimd.wait_ge(sWXNp[1].h, 64 * 16)
                    cAGs[0].inc(gpsimd.collective_compute(
                        "AllGather", ALU.bypass, replica_groups=groups,
                        ins=[cc_xn[0].opt()], outs=[xn_g[0].opt()]))
                    gpsimd.wait_ge(sWXNp[0].h, 64 * 32)
                    gpsimd.wait_ge(sWXNp[1].h, 64 * 32)
                    cAGs[1].inc(gpsimd.collective_compute(
                        "AllGather", ALU.bypass, replica_groups=groups,
                        ins=[cc_xn[1].opt()], outs=[xn_g[1].opt()]))
                    gpsimd.wait_ge(sLQW.h, sLQW.n)
                    phase_end_q(gpsimd)

                @blk.sync
                def _(sync):
                    sLDQz.inc(sync.dma_start(out=slopes_rep[:, :],
                                             in_=slopes.broadcast_to([128, 4])), 16)
                    for tb in range(8):
                        if tb >= 1:
                            sync.wait_ge(vST.h, tb)
                            sync.wait_ge(aSQ.h, tb)
                        sLX.inc(sync.dma_start(
                            out=x_t[:, :],
                            in_=x_own[128 * tb:128 * (tb + 1), :]), 16)
                    for tb in range(8):
                        sync.wait_ge(vST.h, 8)
                        sync.wait_ge(aXN.h, tb)
                        sLX2.inc(sync.dma_start(
                            out=x_t[:, :],
                            in_=x_own[128 * tb:128 * (tb + 1), :]), 16)
                    sLDQz.inc(sync.dma_start(out=gz_sb[:, :], in_=g_qz), 16)
                    sLDQz.inc(sync.dma_start(out=uz_sb[:, :], in_=u_qz), 16)
                    sLDQz.inc(sync.dma_start(out=dz_sb[:, :], in_=d_qz), 16)

                    sync.wait_ge(vDQz.h, 3)
                    sync.wait_ge(sSF.h, 48)
                    for k, (tag, i0, nw, wout, oh) in enumerate(dq_jobs):
                        if k >= 2:
                            sync.wait_ge(vDQs.h, k - 1)
                        b = k % 2
                        if tag == "g":
                            qsrc = g_qw[128 * i0:128 * (i0 + 1), :]
                            zrow, srow, off = zfg_bf, sfg_bf, 0
                        elif tag == "u":
                            qsrc = u_qw[128 * i0:128 * (i0 + 1), :]
                            zrow, srow, off = zfu_bf, sfu_bf, 0
                        else:
                            qsrc = d_qw[128 * i0:128 * (i0 + 1),
                                        128 * oh:128 * (oh + 1)]
                            zrow, srow, off = zfd_bf, sfd_bf, 1024 * oh
                        if k >= 2:
                            sync.wait_ge(tDQ.h, k - 1)
                        sZS.inc(sync.dma_start(
                            out=z_stg[0:1, 1536 * b:1536 * b + wout],
                            in_=zrow[i0:i0 + 1, off:off + wout]), 16)
                        sZS.inc(sync.dma_start(
                            out=s_stg[0:1, 1536 * b:1536 * b + wout],
                            in_=srow[i0:i0 + 1, off:off + wout]), 16)
                        sLDQp[b].inc(sync.dma_start(
                            out=qw_t[:, 512 * b:512 * b + nw], in_=qsrc), 16)
                        sync.wait_ge(vDQm.h, k + 1)
                        if tag == "g":
                            dst = bass.AP(wg_dq.tensor, i0 * 128 * 1408,
                                          [[1408, 128], [1, 1408]])
                        elif tag == "u":
                            dst = bass.AP(wu_dq.tensor, i0 * 128 * 1408,
                                          [[1408, 128], [1, 1408]])
                        else:
                            dst = bass.AP(wd_dq.tensor,
                                          i0 * 128 * DM + 1024 * oh,
                                          [[DM, 128], [1, 1024]])
                        sWDQp[b].inc(sync.dma_start(
                            out=dst, in_=w_t[:, 2048 * b:2048 * b + wout]), 16)
                    for h in range(4):
                        sync.wait_ge(vEB.h, 2 + h + 1)
                        sEB.inc(sync.dma_start(
                            out=bass.AP(e_lin.tensor, h * EL,
                                        [[68, 128], [1, 68]]),
                            in_=e_rowb[:, :]), 16)
                    for s_ in (sLX, sLX2, sLDQz, sLDQp[0], sLDQp[1], sZS,
                               sWDQp[0], sWDQp[1], sEB):
                        sync.wait_ge(s_.h, s_.n)
                    phase_end_q(sync)

                @blk.tensor
                def _(tensor):
                    tensor.wait_ge(gEB.h, 1)
                    last = None
                    for tb in range(8):
                        tensor.wait_ge(aXN.h, tb + 1)
                        for fg in range(8):
                            idx = tb * 8 + fg
                            if idx >= 2:
                                tensor.wait_ge(vTS.h, idx - 1)
                            for j in range(4):
                                fb = 4 * fg + j
                                last = tensor.transpose(
                                    out=ps_tr[:, 1024 * (idx % 2) + 128 * j:
                                              1024 * (idx % 2) + 128 * (j + 1)],
                                    in_=xn_tm[:, DM * (tb % 2) + 128 * fb:
                                              DM * (tb % 2) + 128 * (fb + 1)],
                                    identity=ident[:, :])
                            tTR.inc(last)
                    for k, (tag, i0, nw, wout, oh) in enumerate(dq_jobs):
                        b = k % 2
                        tensor.wait_ge(sZS.h, 32 * (k + 1))
                        if k >= 1:
                            tensor.wait_ge(vDQb.h, k)
                            tensor.wait_ge(vDQm.h, k)
                        last = None
                        for c0 in range(0, wout, 512):
                            w = min(512, wout - c0)
                            tensor.matmul(
                                out=ps_zb[:, c0:c0 + w],
                                lhsT=ones_rowb[:, :],
                                rhs=z_stg[0:1, 1536 * b + c0:1536 * b + c0 + w],
                                start=True, stop=True, skip_group_check=True)
                            last = tensor.matmul(
                                out=ps_sb5[:, c0:c0 + w],
                                lhsT=ones_rowb[:, :],
                                rhs=s_stg[0:1, 1536 * b + c0:1536 * b + c0 + w],
                                start=True, stop=True, skip_group_check=True)
                        tDQ.inc(last)
                    phase_end_c(tensor, tDQ)

                @blk.scalar
                def _(scalar):
                    scalar.preload_act_table(AF.Exp)
                    for tb in range(8):
                        scalar.wait_ge(sLX.h, 16 * (tb + 1))
                        if tb >= 1:
                            scalar.wait_ge(vST.h, tb)
                        aSQ.inc(scalar.activation(out=sq_t[:, :], in_=x_t[:, :],
                                                  func=AF.Square))
                    scalar.wait_ge(vSELF.h, 4)
                    scalar.preload_act_table(AF.Sqrt)
                    aST.inc(scalar.activation(out=std8[:, :], in_=var8[:, :],
                                              func=AF.Sqrt, bias=eps_b[:, :]))
                    scalar.preload_act_table(AF.Exp)
                    for tb in range(8):
                        scalar.wait_ge(sLX2.h, 16 * (tb + 1))
                        scalar.wait_ge(vSELF.h, 7)
                        if tb >= 2:
                            scalar.wait_ge(tTR.h, 8 * (tb - 1))
                        aXN.inc(scalar.activation(
                            out=xn_tm[:, DM * (tb % 2):DM * (tb % 2 + 1)],
                            in_=x_t[:, :],
                            func=AF.Identity, scale=rinv8[:, tb:tb + 1],
                            bias=bneg8[:, tb:tb + 1]))
                    scalar.wait_ge(vEB.h, 1)
                    scalar.preload_act_table(AF.Sqrt)
                    aSELF.inc(scalar.activation(out=sq_row[:, :], in_=rmix[:, :],
                                                func=AF.Sqrt))
                    scalar.wait_ge(aSELF.h, aSELF.n)
                    scalar.preload_act_table(AF.Exp)
                    last = None
                    for h in range(4):
                        scalar.wait_ge(vEB.h, 2 + h)
                        last = scalar.activation(
                            out=e_rowf[:, :],
                            in_=sq_row[:, :], func=AF.Exp,
                            scale=slopes_ng[:, h:h + 1])
                        aEB.inc(last)
                    phase_end_c(scalar, aEB)

                @blk.vector
                def _(vector):
                    vector.memset(ones_rowf[:, :], 1.0)
                    vector.memset(ones_colb[:, :], 1.0)
                    vector.memset(bias_c[:, :], -CSHIFT)
                    vector.memset(eps_b[:, :], 1e-5)
                    for tb in range(8):
                        vector.wait_ge(sLX.h, 16 * (tb + 1))
                        vector.wait_ge(aSQ.h, tb + 1)
                        vector.reduce_sum(out=msum[:, tb:tb + 1], in_=x_t[:, :],
                                          axis=AX.X)
                        vST.inc(vector.reduce_sum(out=qsum[:, tb:tb + 1],
                                                  in_=sq_t[:, :], axis=AX.X))
                    vector.wait_ge(vST.h, 8)
                    vSELF.inc(vector.tensor_scalar_mul(
                        out=mean8[:, :], in0=msum[:, :], scalar1=1.0 / DM))
                    vSELF.inc(vector.tensor_scalar_mul(
                        out=eq8[:, :], in0=qsum[:, :], scalar1=1.0 / DM))
                    vector.wait_ge(vSELF.h, 2)
                    vSELF.inc(vector.tensor_mul(out=m28[:, :], in0=mean8[:, :],
                                                in1=mean8[:, :]))
                    vector.wait_ge(vSELF.h, 3)
                    vSELF.inc(vector.tensor_sub(out=var8[:, :], in0=eq8[:, :],
                                                in1=m28[:, :]))
                    vector.wait_ge(aST.h, 1)
                    vSELF.inc(vector.reciprocal(out=rinv8[:, :], in_=std8[:, :]))
                    vector.wait_ge(vSELF.h, 5)
                    vSELF.inc(vector.tensor_mul(out=tmp8[:, :], in0=mean8[:, :],
                                                in1=rinv8[:, :]))
                    vector.wait_ge(vSELF.h, 6)
                    vSELF.inc(vector.tensor_scalar_mul(
                        out=bneg8[:, :], in0=tmp8[:, :], scalar1=-1.0))
                    for tb in range(8):
                        for fg in range(8):
                            idx = tb * 8 + fg
                            vector.wait_ge(tTR.h, idx + 1)
                            if idx >= 2:
                                bwait(vector, sWXNp, idx - 2, 64)
                            vTS.inc(vector.tensor_copy(
                                out=xn_stage[:, 512 * (idx % 2):
                                             512 * (idx % 2 + 1)],
                                in_=ps_tr[:, 1024 * (idx % 2):
                                          1024 * (idx % 2) + 512]))
                    vector.memset(ones_rowb[:, :], 1.0)
                    vector.wait_ge(sLDQz.h, 16 * 4)
                    for zsrc, dstz, dstzb, wtot in (
                            (gz_sb, zfg_sb, zfg_bf, 1408),
                            (uz_sb, zfu_sb, zfu_bf, 1408),
                            (dz_sb, zfd_sb, zfd_bf, DM)):
                        last = None
                        for j in range(8):
                            last = vector.tensor_scalar(
                                out=dstz[:, j:wtot:8], in0=zsrc[:, :],
                                scalar1=4 * j, scalar2=15,
                                op0=ALU.logical_shift_right,
                                op1=ALU.bitwise_and)
                        vSELF.inc(last)
                        vector.wait_ge(vSELF.h, vSELF.n)
                        vDQz.inc(vector.tensor_copy(out=dstzb[:, :],
                                                    in_=dstz[:, :]))
                    for k, (tag, i0, nw, wout, oh) in enumerate(dq_jobs):
                        b = k % 2
                        bwait(vector, sLDQp, k, 16)
                        if k >= 1:
                            vector.wait_ge(vDQm.h, k)
                        last = None
                        for j in range(8):
                            last = vector.tensor_scalar(
                                out=nib_t[:, j:wout:8],
                                in0=qw_t[:, 512 * b:512 * b + nw],
                                scalar1=4 * j, scalar2=15,
                                op0=ALU.logical_shift_right,
                                op1=ALU.bitwise_and)
                        vDQs.inc(last)
                        vector.wait_ge(vDQs.h, k + 1)
                        vector.wait_ge(tDQ.h, k + 1)
                        vDQb.inc(vector.tensor_sub(
                            out=d1_t[:, 0:wout], in0=nib_t[:, 0:wout],
                            in1=ps_zb[:, 0:wout]))
                        vector.wait_ge(vDQb.h, vDQb.n)
                        if k >= 2:
                            bwait(vector, sWDQp, k - 2, 16)
                        vDQm.inc(vector.tensor_mul(
                            out=w_t[:, 2048 * b:2048 * b + wout],
                            in0=d1_t[:, 0:wout],
                            in1=ps_sb5[:, 0:wout]))
                    vector.wait_ge(gEB.h, 1)
                    vector.tensor_scalar_max(out=rmax[:, :], in0=ramp[:, :],
                                             scalar1=0.0)
                    vSELF.inc(vector.tensor_scalar_min(out=rneg[:, :],
                                                       in0=ramp[:, :],
                                                       scalar1=0.0))
                    vector.wait_ge(vSELF.h, vSELF.n)
                    vEB.inc(vector.scalar_tensor_tensor(
                        out=rmix[:, :], in0=rneg[:, :], scalar=BIGNEG,
                        in1=rmax[:, :], op0=ALU.mult, op1=ALU.add))
                    vector.wait_ge(sLDQz.h, 16 * 4)
                    vEB.inc(vector.tensor_scalar_mul(
                        out=slopes_ng[:, :], in0=slopes_rep[:, :],
                        scalar1=-1.0))
                    last = None
                    for h in range(4):
                        vector.wait_ge(aEB.h, h + 1)
                        if h >= 1:
                            vector.wait_ge(sEB.h, 16 * h)
                        last = vector.tensor_copy(
                            out=e_rowb[:, :], in_=e_rowf[:, :])
                        vEB.inc(last)
                    phase_end_c(vector, vEB)

            PH[0] += 1
            assert sPH.n == 5 * PH[0], sPH.n

            with ExitStack() as scKV:
                k_fm = sbt(scKV, "k_fm", [128, S], BF16, side="right")
                v_tm = sbt(scKV, "v_tm", [128, S], BF16, side="right")
                e_sb = sbt(scKV, "e_sb", [128, 4 * EW], BF16, side="right")

                with ExitStack() as p2:
                    xn_t2 = sbt(p2, "xn_t2", [128, 4 * 512], BF16, side="right")
                    q_stage = sbt(p2, "q_stage", [128, 4 * 512], BF16, side="right")
                    v_stage = sbt(p2, "v_stage", [128, 512], BF16, side="right")
                    ps_qkv = pst(p2, "ps_qkv", [128, 6 * 512], F32)
                    ps_vt = pst(p2, "ps_vt", [128, 512], BF16)

                    @blk.sync
                    def _(sync):
                        phase_start(sync)
                        for tt in range(NQT):
                            sync.wait_ge(cAGs[tt % 2].h, 1)
                            for g in range(32):
                                gi = 32 * tt + g
                                if gi % 2 != 0:
                                    continue
                                if gi >= 4:
                                    sync.wait_ge(tQK.h, gi - 3)
                                sLXN4[gi % 4].inc(sync.dma_start(
                                    out=xn_t2[:, 512 * (gi % 4):512 * (gi % 4 + 1)],
                                    in_=xn_g[tt % 2][
                                        DM * (tt // 2) + 128 * g:
                                        DM * (tt // 2) + 128 * (g + 1), :]), 16)
                                r = gi // 2
                                h_, i_ = divmod(r, 128)
                                sEB2.inc(sync.dma_start(
                                    out=e_sb[i_:i_ + 1, EW * h_:EW * (h_ + 1)],
                                    in_=bass.AP(e_lin.tensor, h_ * EL + 128 - i_,
                                                [[1, 1], [1, EW]])), 16)
                            sync.wait_ge(vEVQ.h, tt + 1)
                            for f in range(4):
                                sWQS.inc(sync.dma_start(
                                    out=q_scr[128 * f:128 * (f + 1),
                                              512 * tt:512 * (tt + 1)],
                                    in_=q_stage[:, 512 * f:512 * (f + 1)]), 16)
                        for s_ in (sLXN4[0], sLXN4[2], sWQS, sEB2):
                            sync.wait_ge(s_.h, s_.n)
                        phase_end_q(sync)

                    @blk.gpsimd
                    def _(gpsimd):
                        phase_start(gpsimd)
                        for tt in range(NQT):
                            gpsimd.wait_ge(cAGs[tt % 2].h, 1)
                            for g in range(32):
                                gi = 32 * tt + g
                                if gi % 2 != 1:
                                    continue
                                if gi >= 4:
                                    gpsimd.wait_ge(tQK.h, gi - 3)
                                sLXN4[gi % 4].inc(gpsimd.dma_start(
                                    out=xn_t2[:, 512 * (gi % 4):512 * (gi % 4 + 1)],
                                    in_=xn_g[tt % 2][
                                        DM * (tt // 2) + 128 * g:
                                        DM * (tt // 2) + 128 * (g + 1), :]), 16)
                                r = 256 + gi // 2
                                h_, i_ = divmod(r, 128)
                                sEB3.inc(gpsimd.dma_start(
                                    out=e_sb[i_:i_ + 1, EW * h_:EW * (h_ + 1)],
                                    in_=bass.AP(e_lin.tensor, h_ * EL + 128 - i_,
                                                [[1, 1], [1, EW]])), 16)
                        gpsimd.wait_ge(sEB3.h, sEB3.n)
                        gpsimd.wait_ge(sLXN4[1].h, sLXN4[1].n)
                        gpsimd.wait_ge(sLXN4[3].h, sLXN4[3].n)
                        phase_end_q(gpsimd)

                    @blk.tensor
                    def _(tensor):
                        phase_start(tensor)
                        last = None
                        for tt in range(NQT):
                            for g in range(32):
                                gi = 32 * tt + g
                                tensor.wait_ge(sLXN4[gi % 4].h, 16 * (gi // 4 + 1))
                                if g == 0 and tt >= 1:
                                    tensor.wait_ge(vEVQ.h, tt)
                                for f in range(6):
                                    last = tensor.matmul(
                                        out=ps_qkv[:, 512 * f:512 * (f + 1)],
                                        lhsT=wqkv_sb[:, 768 * g + 128 * f:
                                                     768 * g + 128 * (f + 1)],
                                        rhs=xn_t2[:, 512 * (gi % 4):
                                                  512 * (gi % 4 + 1)],
                                        start=(g == 0), stop=(g == 31),
                                        skip_group_check=True)
                                tQK.inc(last)
                            tensor.wait_ge(vEVQ.h, tt + 1)
                            if tt >= 1:
                                tensor.wait_ge(vEVV.h, tt)
                            for j in range(4):
                                last = tensor.transpose(
                                    out=ps_vt[:, 128 * j:128 * (j + 1)],
                                    in_=v_stage[:, 128 * j:128 * (j + 1)],
                                    identity=ident[:, :])
                            tTRV.inc(last)
                        phase_end_c(tensor, tTRV)

                    @blk.scalar
                    def _(scalar):
                        phase_start(scalar)
                        phase_end_q(scalar)

                    @blk.vector
                    def _(vector):
                        phase_start(vector)
                        last = None
                        for tt in range(NQT):
                            vector.wait_ge(tQK.h, 32 * (tt + 1))
                            if tt >= 1:
                                vector.wait_ge(sWQS.h, 16 * 4 * tt)
                            for f in range(4):
                                vector.tensor_copy(
                                    out=q_stage[:, 512 * f:512 * (f + 1)],
                                    in_=ps_qkv[:, 512 * f:512 * (f + 1)])
                            vector.tensor_copy(
                                out=k_fm[:, 512 * tt:512 * (tt + 1)],
                                in_=ps_qkv[:, 512 * 4:512 * 5])
                            vEVQ.inc(vector.tensor_copy(
                                out=v_stage[:, :], in_=ps_qkv[:, 512 * 5:512 * 6]))
                            vector.wait_ge(tTRV.h, tt + 1)
                            last = vector.tensor_copy(
                                out=v_tm[:, 512 * tt:512 * (tt + 1)],
                                in_=ps_vt[:, :])
                            vEVV.inc(last)
                        phase_end_c(vector, vEVV)

                PH[0] += 1
                assert sPH.n == 5 * PH[0], sPH.n

                if True:
                    with ExitStack() as p3:
                        q_t = sbt(p3, "q_t", [128, 2 * 512], BF16, side="right")
                        p_sb = sbt(p3, "p_sb", [128, 4 * 512], BF16, side="right")
                        pm_sb = sbt(p3, "pm_sb", [128, 4 * 512], BF16, side="right")
                        o_raw = sbt(p3, "o_raw", [128, 2 * 512], F32, side="right")
                        o_stage = sbt(p3, "o_stage", [128, 2 * 512], BF16,
                                      side="right")
                        linv = sbt(p3, "linv", [1, 512], F32, side="right")
                        ps_sc = pst(p3, "ps_sc", [128, 4 * 512], F32)
                        ps_po = pst(p3, "ps_po", [128, 2 * 512], F32)
                        ps_pl = pst(p3, "ps_pl", [1, 512], F32)
                        ps_plb = pst(p3, "ps_plb", [128, 512], F32)

                        @blk.sync
                        def _(sync):
                            phase_start(sync)
                            def o_store(sidx):
                                hh, qq = divmod(sidx, NQT)
                                sync.wait_ge(vON.h, sidx + 1)
                                sWOSp[sidx % 2].inc(sync.dma_start(
                                    out=o_scr[128 * hh:128 * (hh + 1),
                                              512 * qq:512 * (qq + 1)],
                                    in_=o_stage[:, 512 * (sidx % 2):
                                                512 * (sidx % 2 + 1)]), 16)

                            for idx in range(64):
                                if idx >= 2:
                                    sync.wait_ge(tSC.h, PB_CUM[idx - 1])
                                h, qt = divmod(idx, NQT)
                                sLQp[idx % 2].inc(sync.dma_start(
                                    out=q_t[:, 512 * (idx % 2):512 * (idx % 2 + 1)],
                                    in_=q_scr[128 * h:128 * (h + 1),
                                              512 * qt:512 * (qt + 1)]), 16)
                                if idx >= 2:
                                    o_store(idx - 2)
                            o_store(62)
                            o_store(63)
                            for s_ in (sLQp[0], sLQp[1], sWOSp[0], sWOSp[1]):
                                sync.wait_ge(s_.h, s_.n)
                            phase_end_q(sync)

                        @blk.gpsimd
                        def _(gpsimd):
                            phase_start(gpsimd)
                            phase_end_q(gpsimd)

                        @blk.tensor
                        def _(tensor):
                            phase_start(tensor)
                            last = None
                            for idx in range(64):
                                h, qt = divmod(idx, NQT)
                                n = NKB[qt]
                                K0 = KB_CUM[idx]
                                bwait(tensor, sLQp, idx, 16)
                                if idx >= 1:
                                    tensor.wait_ge(aOE.h, idx - 1)
                                    tensor.wait_ge(vRC.h, idx)
                                qs = 512 * (idx % 2)

                                np_ = n // 2
                                P0_ = PB_CUM[idx]

                                def score_pair(p):
                                    pp = P0_ + p
                                    sp = pp % 2
                                    if pp >= 2:
                                        tensor.wait_ge(aEX.h, pp - 1)
                                    for u in range(2):
                                        j = 2 * p + u
                                        ins = tensor.matmul(
                                            out=ps_sc[:, 1024 * sp + 512 * u:
                                                      1024 * sp + 512 * (u + 1)],
                                            lhsT=k_fm[:, 128 * j:128 * (j + 1)],
                                            rhs=q_t[:, qs:qs + 512],
                                            start=True, stop=True,
                                            skip_group_check=True)
                                    tSC.inc(ins)

                                score_pair(0)
                                if np_ > 1:
                                    score_pair(1)
                                for p in range(np_):
                                    pp = P0_ + p
                                    sp = pp % 2
                                    if p + 2 < np_:
                                        score_pair(p + 2)
                                    tensor.wait_ge(vMK.h, pp + 1)
                                    for u in range(2):
                                        j = 2 * p + u
                                        kk = K0 + j
                                        tensor.matmul(
                                            out=ps_po[:, 512 * (idx % 2):
                                                      512 * (idx % 2 + 1)],
                                            lhsT=v_tm[:, 128 * j:128 * (j + 1)],
                                            rhs=pm_sb[:, 1024 * sp + 512 * u:
                                                      1024 * sp + 512 * (u + 1)],
                                            start=(j == 0), stop=(j == n - 1),
                                            skip_group_check=True)
                                        last = tensor.matmul(
                                            out=ps_pl[0:1, :],
                                            lhsT=ones_colb[:, :],
                                            rhs=pm_sb[:, 1024 * sp + 512 * u:
                                                      1024 * sp + 512 * (u + 1)],
                                            start=(j == 0), stop=(j == n - 1),
                                            skip_group_check=True)
                                        tPV.inc(last)
                                tensor.wait_ge(vRC.h, idx + 1)
                                if idx >= 1:
                                    tensor.wait_ge(vON.h, idx)
                                last = tensor.matmul(
                                    out=ps_plb[:, :],
                                    lhsT=ones_rowf[:, :], rhs=linv[0:1, :],
                                    start=True, stop=True,
                                    skip_group_check=True)
                                tLB.inc(last)
                            phase_end_c(tensor, tLB)

                        @blk.scalar
                        def _(scalar):
                            phase_start(scalar)
                            last = None
                            for idx in range(64):
                                h, qt = divmod(idx, NQT)
                                n = NKB[qt]
                                K0 = KB_CUM[idx]
                                np_ = n // 2
                                P0_ = PB_CUM[idx]
                                for p in range(np_):
                                    pp = P0_ + p
                                    sp = pp % 2
                                    scalar.wait_ge(tSC.h, pp + 1)
                                    if pp >= 2:
                                        scalar.wait_ge(vMK.h, pp - 1)
                                    aEX.inc(scalar.activation(
                                        out=p_sb[:, 1024 * sp:1024 * (sp + 1)],
                                        in_=ps_sc[:, 1024 * sp:1024 * (sp + 1)],
                                        func=AF.Exp, scale=SCALE,
                                        bias=bias_c[:, :]))
                                scalar.wait_ge(tPV.h, K0 + n)
                                if idx >= 2:
                                    scalar.wait_ge(vON.h, idx - 1)
                                last = scalar.activation(
                                    out=o_raw[:, 512 * (idx % 2):
                                              512 * (idx % 2 + 1)],
                                    in_=ps_po[:, 512 * (idx % 2):
                                              512 * (idx % 2 + 1)],
                                    func=AF.Copy)
                                aOE.inc(last)
                            phase_end_c(scalar, aOE)

                        @blk.vector
                        def _(vector):
                            phase_start(vector)
                            last = None
                            for idx in range(64):
                                h, qt = divmod(idx, NQT)
                                n = NKB[qt]
                                K0 = KB_CUM[idx]
                                np_ = n // 2
                                P0_ = PB_CUM[idx]
                                for p in range(np_):
                                    pp = P0_ + p
                                    sp = pp % 2
                                    vector.wait_ge(aEX.h, pp + 1)
                                    if pp >= 2:
                                        vector.wait_ge(tPV.h, 2 * (pp - 1))
                                    ins = None
                                    for u in range(2):
                                        j = 2 * p + u
                                        u0 = 512 * qt - 128 * j + 384
                                        ins = vector.tensor_mul(
                                            out=pm_sb[:, 1024 * sp + 512 * u:
                                                      1024 * sp + 512 * (u + 1)],
                                            in0=p_sb[:, 1024 * sp + 512 * u:
                                                     1024 * sp + 512 * (u + 1)],
                                            in1=e_sb[:, EW * h + u0:
                                                     EW * h + u0 + 512])
                                    vMK.inc(ins)
                                vector.wait_ge(tPV.h, K0 + n)
                                vector.wait_ge(tLB.h, idx)
                                vRC.inc(vector.reciprocal(out=linv[0:1, :],
                                                          in_=ps_pl[0:1, :]))
                                vector.wait_ge(tLB.h, idx + 1)
                                vector.wait_ge(aOE.h, idx + 1)
                                if idx >= 2:
                                    bwait(vector, sWOSp, idx - 2, 16)
                                last = vector.tensor_mul(
                                    out=o_stage[:, 512 * (idx % 2):
                                                512 * (idx % 2 + 1)],
                                    in0=o_raw[:, 512 * (idx % 2):
                                              512 * (idx % 2 + 1)],
                                    in1=ps_plb[:, :])
                                vON.inc(last)
                            phase_end_c(vector, vON)

                    PH[0] += 1
                    assert sPH.n == 5 * PH[0], sPH.n

                    with ExitStack() as p4:
                        wo_sb = sbt(p4, "wo_sb", [128, 2 * 4 * 512], BF16,
                                    side="right")
                        o_rc = sbt(p4, "o_rc", [128, 2 * 4 * 1024], BF16,
                                   side="right")
                        s_stage = sbt(p4, "s_stage", [128, 3 * 512], F32,
                                      side="right")
                        ps_s = pst(p4, "ps_s", [128, 3 * 512], F32)

                        @blk.gpsimd
                        def _(gpsimd):
                            phase_start(gpsimd)
                            phase_end_q(gpsimd)

                        @blk.sync
                        def _(sync):
                            phase_start(sync)
                            for rc in range(8):
                                if rc >= 2:
                                    sync.wait_ge(tS4.h, 64 * (rc - 1))
                                for h in range(4):
                                    sLORp[rc % 2].inc(sync.dma_start(
                                        out=o_rc[:, 4096 * (rc % 2) + 1024 * h:
                                                 4096 * (rc % 2)
                                                 + 1024 * (h + 1)],
                                        in_=o_scr[128 * h:128 * (h + 1),
                                                  1024 * rc:1024 * (rc + 1)]), 16)
                                for r1 in range(64):
                                    gtb = 64 * rc + r1
                                    dm8, tb = divmod(r1, 8)
                                    sync.wait_ge(aSE.h, gtb + 1)
                                    if gtb >= 3:
                                        sync.wait_ge(sWSPp[gtb % 3].h,
                                                     16 * ((gtb - 3) // 3 + 1))
                                    sWSPp[gtb % 3].inc(sync.dma_start(
                                        out=s_part[rc][128 * tb:128 * (tb + 1),
                                                       512 * dm8:512 * (dm8 + 1)],
                                        in_=s_stage[:, 512 * (gtb % 3):
                                                    512 * (gtb % 3 + 1)]), 16)
                            for s_ in (sLORp[0], sLORp[1], sWSPp[0], sWSPp[1],
                                       sWSPp[2]):
                                sync.wait_ge(s_.h, s_.n)
                            phase_end_q(sync)

                        @blk.tensor
                        def _(tensor):
                            phase_start(tensor)
                            last = None
                            for rc in range(8):
                                bwait(tensor, sLORp, rc, 64)
                                for dm8 in range(8):
                                    gi = 8 * rc + dm8
                                    bwait(tensor, sLWOp, gi, 64)
                                    for tb in range(8):
                                        gtb = 8 * gi + tb
                                        if gtb >= 3:
                                            tensor.wait_ge(aSE.h, gtb - 2)
                                        for h in range(4):
                                            last = tensor.matmul(
                                                out=ps_s[:, 512 * (gtb % 3):
                                                         512 * (gtb % 3 + 1)],
                                                lhsT=o_rc[:, 4096 * (rc % 2)
                                                          + 1024 * h + 128 * tb:
                                                          4096 * (rc % 2)
                                                          + 1024 * h
                                                          + 128 * (tb + 1)],
                                                rhs=wo_sb[:, 2048 * (gi % 2)
                                                          + 512 * h:
                                                          2048 * (gi % 2)
                                                          + 512 * (h + 1)],
                                                start=(h == 0), stop=(h == 3),
                                                skip_group_check=True)
                                        tS4.inc(last)
                            phase_end_c(tensor, tS4)

                        @blk.scalar
                        def _(scalar):
                            phase_start(scalar)

                            def wo_load(gi):
                                dm8 = gi % 8
                                if gi >= 2:
                                    scalar.wait_ge(tS4.h, 8 * (gi - 1))
                                for h in range(4):
                                    sLWOp[gi % 2].inc(scalar.dma_start(
                                        out=wo_sb[:, 2048 * (gi % 2) + 512 * h:
                                                  2048 * (gi % 2) + 512 * (h + 1)],
                                        in_=wo_bf[128 * h:128 * (h + 1),
                                                  512 * dm8:512 * (dm8 + 1)]), 16)

                            wo_load(0)
                            wo_load(1)
                            last = None
                            for gtb in range(512):
                                scalar.wait_ge(tS4.h, gtb + 1)
                                if gtb >= 3:
                                    scalar.wait_ge(sWSPp[gtb % 3].h,
                                                   16 * ((gtb - 3) // 3 + 1))
                                last = scalar.activation(
                                    out=s_stage[:, 512 * (gtb % 3):
                                                512 * (gtb % 3 + 1)],
                                    in_=ps_s[:, 512 * (gtb % 3):
                                             512 * (gtb % 3 + 1)],
                                    func=AF.Copy)
                                aSE.inc(last)
                                if gtb % 8 == 7 and gtb // 8 + 2 < 64:
                                    wo_load(gtb // 8 + 2)
                            scalar.wait_ge(sLWOp[0].h, sLWOp[0].n)
                            scalar.wait_ge(sLWOp[1].h, sLWOp[1].n)
                            phase_end_c(scalar, aSE)

                        @blk.vector
                        def _(vector):
                            phase_start(vector)
                            phase_end_q(vector)

                    PH[0] += 1
                    assert sPH.n == 5 * PH[0], sPH.n

        NP5 = 8
        IQ6 = [2, 2, 2, 2, 2, 1]
        IS6 = [0, 2, 4, 6, 8, 10]
        P5_SEQ = [(j, hf, IS6[j] + itl) for j in range(6)
                  for hf in range(2) for itl in range(IQ6[j])]
        assert len(P5_SEQ) == 22

        def p5_cons(gc):
            pc, jc = divmod(gc, 6)
            return 22 * pc + 2 * (IS6[jc] + IQ6[jc])

        with ExitStack() as p5:
            xn_f = sbt(p5, "xn_f", [128, 2 * 32 * 512], BF16, side="left")
            wgu_q = sbt(p5, "wgu_q", [128, 2 * 2 * 32 * 256], BF16, side="left")
            wd_sb = sbt(p5, "wd_sb", [128, 3 * IC * 512], BF16, side="left")
            h_sb = sbt(p5, "h_sb", [128, IC * 1024], BF16, side="left")
            sg_sb = sbt(p5, "sg_sb", [128, 2 * 512], BF16, side="left")
            su_sb = sbt(p5, "su_sb", [128, 2 * 512], BF16, side="left")
            f_stage = sbt(p5, "f_stage", [128, 2 * 512], F32, side="left")
            ps_g = pst(p5, "ps_g", [128, 2 * 512], F32)
            ps_u = pst(p5, "ps_u", [128, 2 * 512], F32)
            ps_f = pst(p5, "ps_f", [128, 2 * 512], F32)

            WQ2 = 256

            def gu_sl(buf, w, g, lo, width, nit):
                base = (2 * buf + w) * 32 * WQ2 + g * (128 * nit) + lo
                return wgu_q[:, base:base + width]

            @blk.sync
            def _(sync):
                phase_start(sync)

                def wg_half(gc):
                    if gc >= 6 * NP5:
                        return
                    j = gc % 6
                    nit = IQ6[j]
                    i0 = IS6[j]
                    if gc >= 2:
                        sync.wait_ge(tGU.h, p5_cons(gc - 2))
                    sLWGp[gc % 2].inc(sync.dma_start(
                        out=gu_sl(gc % 2, 0, 0, 0, 32 * 128 * nit, nit),
                        in_=bass.AP(wg_dq.tensor, 128 * i0,
                                    [[1408, 128], [128 * 1408, 32],
                                     [1, 128 * nit]])), 16)

                def xn_load(p):
                    if p >= NP5:
                        return
                    if p >= 1:
                        sync.wait_ge(tGU.h, 22 * p)
                    for hf in range(2):
                        sync.wait_ge(cAGs[hf].h, 1)
                        sLXF.inc(sync.dma_start(
                            out=xn_f[:, 32 * 512 * hf:32 * 512 * (hf + 1)],
                            in_=bass.AP(xn_g[hf].tensor, DM * p * 512,
                                        [[512, 128], [128 * 512, 32],
                                         [1, 512]])), 16)

                xn_load(0)
                wg_half(0)
                wg_half(1)
                for p in range(NP5):
                    wg_half(6 * p + 2)
                    wg_half(6 * p + 3)
                    wg_half(6 * p + 4)
                    wg_half(6 * p + 5)
                    wg_half(6 * p + 6)
                    wg_half(6 * p + 7)
                    xn_load(p + 1)
                    for ftl in range(64):
                        q8, tb2 = divmod(ftl, 8)
                        ftb = 64 * p + ftl
                        sync.wait_ge(vFE.h, ftb + 1)
                        sWFPp[ftb % 2].inc(sync.dma_start(
                            out=f_part[p][128 * tb2:128 * (tb2 + 1),
                                          512 * q8:512 * (q8 + 1)],
                            in_=f_stage[:, 512 * (ftb % 2):
                                        512 * (ftb % 2 + 1)]), 16)
                for s_ in (sWFPp[0], sWFPp[1], sLWGp[0], sLWGp[1], sLXF):
                    sync.wait_ge(s_.h, s_.n)
                phase_end_q(sync)

            @blk.gpsimd
            def _(gpsimd):
                phase_start(gpsimd)
                for p in range(NP5):
                    for _p, _sw in enumerate(sWSPp):
                        _cnt = sum(1 for _g in range(64 * (p + 1))
                                   if _g % 3 == _p)
                        gpsimd.wait_ge(_sw.h, 16 * _cnt)
                    gpsimd.wait_ge(tGU.h, 22 * p + 2)
                    cS[p].inc(gpsimd.collective_compute(
                        "ReduceScatter", ALU.add, replica_groups=groups,
                        ins=[s_part[p].opt()], outs=[s_ownr[p].opt()]))
                    if p >= 1:
                        gpsimd.wait_ge(sWFPp[0].h, 16 * 32 * p)
                        gpsimd.wait_ge(sWFPp[1].h, 16 * 32 * p)
                        gpsimd.wait_ge(tGU.h, 22 * p + 12)
                        cF[p - 1].inc(gpsimd.collective_compute(
                            "ReduceScatter", ALU.add, replica_groups=groups,
                            ins=[f_part[p - 1].opt()],
                            outs=[f_ownr[p - 1].opt()]))
                gpsimd.wait_ge(sWFPp[0].h, 16 * 32 * 8)
                gpsimd.wait_ge(sWFPp[1].h, 16 * 32 * 8)
                cF[7].inc(gpsimd.collective_compute(
                    "ReduceScatter", ALU.add, replica_groups=groups,
                    ins=[f_part[7].opt()], outs=[f_ownr[7].opt()]))
                phase_end_q(gpsimd)

            @blk.tensor
            def _(tensor):
                phase_start(tensor)
                last = None
                for p in range(NP5):
                    tensor.wait_ge(sLXF.h, 16 * 2 * (p + 1))
                    gidx = 22 * p
                    for j in range(6):
                        gc = 6 * p + j
                        nit = IQ6[j]
                        i0 = IS6[j]
                        bwait(tensor, sLWGp, gc, 16 * 2)
                        for hf in range(2):
                            for itl in range(nit):
                                if gidx >= 2:
                                    tensor.wait_ge(vH.h, gidx - 1)
                                for g in range(32):
                                    tensor.matmul(
                                        out=ps_g[:, 512 * (gidx % 2):
                                                 512 * (gidx % 2 + 1)],
                                        lhsT=gu_sl(gc % 2, 0, g,
                                                   128 * itl, 128, nit),
                                        rhs=xn_f[:, 32 * 512 * hf + 512 * g:
                                                 32 * 512 * hf + 512 * (g + 1)],
                                        start=(g == 0), stop=(g == 31),
                                        skip_group_check=True)
                                    last = tensor.matmul(
                                        out=ps_u[:, 512 * (gidx % 2):
                                                 512 * (gidx % 2 + 1)],
                                        lhsT=gu_sl(gc % 2, 1, g,
                                                   128 * itl, 128, nit),
                                        rhs=xn_f[:, 32 * 512 * hf + 512 * g:
                                                 32 * 512 * hf + 512 * (g + 1)],
                                        start=(g == 0), stop=(g == 31),
                                        skip_group_check=True)
                                tGU.inc(last)
                                gidx += 1
                    tensor.wait_ge(vH.h, 22 * (p + 1))
                    for q8 in range(8):
                        gq8 = 8 * p + q8
                        tensor.wait_ge(sLWDp[gq8 % 3].h, 16 * (gq8 // 3 + 1))
                        for tb2 in range(8):
                            ftb = 64 * p + 8 * q8 + tb2
                            if ftb >= 2:
                                tensor.wait_ge(vFE.h, ftb - 1)
                            for it in range(IC):
                                last = tensor.matmul(
                                    out=ps_f[:, 512 * (ftb % 2):
                                             512 * (ftb % 2 + 1)],
                                    lhsT=h_sb[:, 1024 * it + 128 * tb2:
                                              1024 * it + 128 * (tb2 + 1)],
                                    rhs=wd_sb[:, IC * 512 * (gq8 % 3)
                                              + 512 * it:
                                              IC * 512 * (gq8 % 3)
                                              + 512 * (it + 1)],
                                    start=(it == 0), stop=(it == IC - 1),
                                    skip_group_check=True)
                            tDN.inc(last)
                phase_end_c(tensor, tDN)

            @blk.scalar
            def _(scalar):
                phase_start(scalar)
                scalar.preload_act_table(AF.Silu)

                def wu_half(gc):
                    if gc >= 6 * NP5:
                        return
                    j = gc % 6
                    nit = IQ6[j]
                    i0 = IS6[j]
                    if gc >= 2:
                        scalar.wait_ge(tGU.h, p5_cons(gc - 2))
                    sLWGp[gc % 2].inc(scalar.dma_start(
                        out=gu_sl(gc % 2, 1, 0, 0, 32 * 128 * nit, nit),
                        in_=bass.AP(wu_dq.tensor, 128 * i0,
                                    [[1408, 128], [128 * 1408, 32],
                                     [1, 128 * nit]])), 16)

                def wd_load(gq8):
                    if gq8 >= 16 * NP5 // 2:
                        return
                    if gq8 >= 3:
                        scalar.wait_ge(tDN.h, 8 * (gq8 - 2))
                    sLWDp[gq8 % 3].inc(scalar.dma_start(
                        out=wd_sb[:, IC * 512 * (gq8 % 3):
                                  IC * 512 * (gq8 % 3 + 1)],
                        in_=bass.AP(wd_dq.tensor, 512 * (gq8 % 8),
                                    [[DM, 128], [128 * DM, IC],
                                     [1, 512]])), 16)

                wu_half(0)
                wu_half(1)
                last = None
                for p in range(NP5):
                    for w_, (j, hf, it) in enumerate(P5_SEQ):
                        gidx = 22 * p + w_
                        scalar.wait_ge(tGU.h, gidx + 1)
                        if gidx >= 2:
                            scalar.wait_ge(vH.h, gidx - 1)
                        scalar.activation(
                            out=sg_sb[:, 512 * (gidx % 2):
                                      512 * (gidx % 2 + 1)],
                            in_=ps_g[:, 512 * (gidx % 2):
                                     512 * (gidx % 2 + 1)],
                            func=AF.Silu)
                        last = scalar.activation(
                            out=su_sb[:, 512 * (gidx % 2):
                                      512 * (gidx % 2 + 1)],
                            in_=ps_u[:, 512 * (gidx % 2):
                                     512 * (gidx % 2 + 1)],
                            func=AF.Copy)
                        aGU.inc(last)
                        if w_ == 0:
                            wd_load(8 * p)
                            wd_load(8 * p + 1)
                        elif w_ == 3:
                            wu_half(6 * p + 2)
                        elif w_ == 7:
                            wu_half(6 * p + 3)
                        elif w_ == 11:
                            wu_half(6 * p + 4)
                        elif w_ == 15:
                            wu_half(6 * p + 5)
                        elif w_ == 19:
                            wu_half(6 * p + 6)
                        elif w_ == 21:
                            wu_half(6 * p + 7)
                            for q8 in range(2, 8):
                                wd_load(8 * p + q8)
                phase_end_c(scalar, aGU)

            @blk.vector
            def _(vector):
                phase_start(vector)
                last = None
                for p in range(NP5):
                    for w_, (j, hf, it) in enumerate(P5_SEQ):
                        gidx = 22 * p + w_
                        vector.wait_ge(aGU.h, gidx + 1)
                        vH.inc(vector.tensor_mul(
                            out=h_sb[:, 1024 * it + 512 * hf:
                                     1024 * it + 512 * (hf + 1)],
                            in0=sg_sb[:, 512 * (gidx % 2):
                                      512 * (gidx % 2 + 1)],
                            in1=su_sb[:, 512 * (gidx % 2):
                                      512 * (gidx % 2 + 1)]))
                    for ftl in range(64):
                        ftb = 64 * p + ftl
                        vector.wait_ge(tDN.h, ftb + 1)
                        if ftb >= 2:
                            bwait(vector, sWFPp, ftb - 2, 16)
                        last = vector.tensor_copy(
                            out=f_stage[:, 512 * (ftb % 2):
                                        512 * (ftb % 2 + 1)],
                            in_=ps_f[:, 512 * (ftb % 2):
                                     512 * (ftb % 2 + 1)])
                        vFE.inc(last)
                phase_end_c(vector, vFE)

        PH[0] += 1
        assert sPH.n == 5 * PH[0], sPH.n

        with ExitStack() as p6:
            s_t6 = sbt(p6, "s_t6", [128, 2 * DM], F32, side="left")
            f_t6 = sbt(p6, "f_t6", [128, 2 * DM], F32, side="left")
            x_t6 = sbt(p6, "x_t6", [128, 2 * DM], F32, side="left")
            g_t6 = sbt(p6, "g_t6", [128, 2 * DM], F32, side="left")
            sg_t6 = sbt(p6, "sg_t6", [128, 2 * DM], F32, side="left")
            out_t6 = sbt(p6, "out_t6", [128, 2 * DM], F32, side="left")

            @blk.sync
            def _(sync):
                phase_start(sync)
                def st6(rc):
                    b = DM * (rc % 2)
                    sync.wait_ge(vOUT.h, rc + 1)
                    sWOUTp[rc % 2].inc(sync.dma_start(
                        out=out_own[128 * rc:128 * (rc + 1), :],
                        in_=out_t6[:, b:b + DM]), 16)

                for rc in range(8):
                    b = DM * (rc % 2)
                    sync.wait_ge(cS[rc].h, 1)
                    sync.wait_ge(cF[rc].h, 1)
                    if rc >= 2:
                        sync.wait_ge(vOUT.h, rc - 1)
                    sLSFp[rc % 2].inc(sync.dma_start(out=s_t6[:, b:b + DM],
                                                     in_=s_ownr[rc]), 16)
                    sLSFp[rc % 2].inc(sync.dma_start(out=f_t6[:, b:b + DM],
                                                     in_=f_ownr[rc]), 16)
                    sLSFp[rc % 2].inc(sync.dma_start(
                        out=x_t6[:, b:b + DM],
                        in_=x_sc[128 * rc:128 * (rc + 1), :]), 16)
                    if rc >= 2:
                        st6(rc - 2)
                st6(6)
                st6(7)
                for name, dst in dbg_outs.items():
                    sWOUTp[0].inc(sync.dma_start(out=dst, in_=dbg_srcs[name]), 16)
                sync.wait_ge(sWOUTp[0].h, sWOUTp[0].n)
                sync.wait_ge(sWOUTp[1].h, sWOUTp[1].n)

            @blk.gpsimd
            def _(gpsimd):
                phase_start(gpsimd)

            @blk.tensor
            def _(tensor):
                phase_start(tensor)

            @blk.scalar
            def _(scalar):
                phase_start(scalar)
                for rc in range(8):
                    b = DM * (rc % 2)
                    scalar.wait_ge(sLSFp[rc % 2].h, 48 * (rc // 2 + 1))
                    if rc >= 2:
                        scalar.wait_ge(vOUT.h, rc - 1)
                    aG6.inc(scalar.activation(out=g_t6[:, b:b + DM],
                                              in_=f_t6[:, b:b + DM],
                                              func=AF.Silu))

            @blk.vector
            def _(vector):
                phase_start(vector)
                for rc in range(8):
                    b = DM * (rc % 2)
                    vector.wait_ge(aG6.h, rc + 1)
                    vSELF.inc(vector.tensor_mul(out=sg_t6[:, b:b + DM],
                                                in0=s_t6[:, b:b + DM],
                                                in1=g_t6[:, b:b + DM]))
                    vector.wait_ge(vSELF.h, vSELF.n)
                    if rc >= 2:
                        vector.wait_ge(sWOUTp[rc % 2].h, 16 * (rc // 2))
                    vOUT.inc(vector.tensor_add(out=out_t6[:, b:b + DM],
                                               in0=sg_t6[:, b:b + DM],
                                               in1=x_t6[:, b:b + DM]))

    return nc


def _alibi_slopes(n):
    cp2 = 2 ** math.floor(math.log2(n))
    base = 2.0 ** (-8.0 / cp2)
    slopes = base ** np.arange(1, cp2 + 1, dtype=np.float32)
    if cp2 != n:
        extra_base = 2.0 ** (-4.0 / cp2)
        extra = extra_base ** np.arange(1, 1 + 2 * (n - cp2), 2, dtype=np.float32)
        slopes = np.concatenate([slopes, extra])
    return slopes.astype(np.float32)


def make_in_maps(x_full, w_qkv, w_o, gqw, gsc, gqz, uqw, usc, uqz, dqw, dsc, dqz):
    x = np.ascontiguousarray(np.asarray(x_full, dtype=np.float32).reshape(S, DM))
    sl = _alibi_slopes(32)
    x4 = x.reshape(8, 8, 128, DM)
    in_maps = []
    for c in range(NCORE):
        g0, gn = G0S[c], GROUPS[c]
        m = {}
        m["x_own"] = np.ascontiguousarray(x[1024 * c:1024 * (c + 1)])
        m["x_sc"] = np.ascontiguousarray(x4[:, c].reshape(1024, DM))
        m["wqkv"] = np.ascontiguousarray(np.concatenate(
            [w_qkv[:, 512 * c:512 * (c + 1)],
             w_qkv[:, 4096 + 128 * c:4096 + 128 * (c + 1)],
             w_qkv[:, 5120 + 128 * c:5120 + 128 * (c + 1)]], axis=1))
        m["wo"] = np.ascontiguousarray(w_o[512 * c:512 * (c + 1)])
        m["slopes"] = np.ascontiguousarray(sl[4 * c:4 * (c + 1)].reshape(1, 4))
        gw = np.zeros((DM, 176), np.int32)
        gw[:, :16 * gn] = gqw[:, 16 * g0:16 * (g0 + gn)]
        m["g_qw"] = gw
        gz = np.zeros((32, 176), np.int32)
        gz[:, :16 * gn] = gqz[:, 16 * g0:16 * (g0 + gn)]
        m["g_qz"] = gz
        gs = np.zeros((32, 1408), np.float32)
        gs[:, :128 * gn] = gsc[:, 128 * g0:128 * (g0 + gn)]
        m["g_sc"] = gs
        uw = np.zeros((DM, 176), np.int32)
        uw[:, :16 * gn] = uqw[:, 16 * g0:16 * (g0 + gn)]
        m["u_qw"] = uw
        uz = np.zeros((32, 176), np.int32)
        uz[:, :16 * gn] = uqz[:, 16 * g0:16 * (g0 + gn)]
        m["u_qz"] = uz
        us = np.zeros((32, 1408), np.float32)
        us[:, :128 * gn] = usc[:, 128 * g0:128 * (g0 + gn)]
        m["u_sc"] = us
        dw = np.zeros((1408, 512), np.int32)
        dw[:128 * gn] = dqw[128 * g0:128 * (g0 + gn)]
        m["d_qw"] = dw
        dz = np.zeros((IC, 512), np.int32)
        dz[:gn] = dqz[g0:g0 + gn]
        m["d_qz"] = dz
        ds = np.zeros((IC, DM), np.float32)
        ds[:gn] = dsc[g0:g0 + gn]
        m["d_sc"] = ds
        in_maps.append(m)
    return in_maps


def kernel(L0_attn_hidden_states, L0_attn_w_qkv, L0_attn_w_o,
           L0_ffn_gate_qweight, L0_ffn_gate_scales, L0_ffn_gate_qzeros,
           L0_ffn_up_qweight, L0_ffn_up_scales, L0_ffn_up_qzeros,
           L0_ffn_down_qweight, L0_ffn_down_scales, L0_ffn_down_qzeros):
    in_maps = make_in_maps(
        np.asarray(L0_attn_hidden_states, dtype=np.float32),
        np.asarray(L0_attn_w_qkv, dtype=np.float32),
        np.asarray(L0_attn_w_o, dtype=np.float32),
        np.asarray(L0_ffn_gate_qweight, dtype=np.int32),
        np.asarray(L0_ffn_gate_scales, dtype=np.float32),
        np.asarray(L0_ffn_gate_qzeros, dtype=np.int32),
        np.asarray(L0_ffn_up_qweight, dtype=np.int32),
        np.asarray(L0_ffn_up_scales, dtype=np.float32),
        np.asarray(L0_ffn_up_qzeros, dtype=np.int32),
        np.asarray(L0_ffn_down_qweight, dtype=np.int32),
        np.asarray(L0_ffn_down_scales, dtype=np.float32),
        np.asarray(L0_ffn_down_qzeros, dtype=np.int32))

    nc = build_program()
    res = run_bass_kernel_spmd(nc, in_maps, core_ids=list(range(NCORE)))

    out = np.empty((8, 8, 128, DM), dtype=np.float32)
    for c in range(NCORE):
        out[:, c] = res.results[c]["out_own"].reshape(8, 128, DM)
    return out.reshape(1, S, DM)


# version 16
# speedup vs baseline: 1.0392x; 1.0060x over previous
import math
import numpy as np
from contextlib import ExitStack

import concourse.bass as bass
import concourse.mybir as mybir
from concourse.bass_utils import run_bass_kernel_spmd

F32 = mybir.dt.float32
BF16 = mybir.dt.bfloat16
I32 = mybir.dt.int32
AF = mybir.ActivationFunctionType
ALU = mybir.AluOpType
AX = mybir.AxisListType

NCORE = 8
S = 8192
DM = 4096
TOWN = 1024
NQT = 16
SCALE = 1.0 / math.sqrt(128.0)
CSHIFT = 12.0
IC = 11
GROUPS = [11, 11, 11, 11, 11, 11, 10, 10]
G0S = [0, 11, 22, 33, 44, 55, 66, 76]
EW = 8576
EL = 8704
NSB = 16
IQ_CHUNKS = [3, 3, 3, 2]
IQ_STARTS = [0, 3, 6, 9]
BIGNEG = -6.6e8


class Sem:
    __slots__ = ("h", "n")

    def __init__(self, ctx, nc, name):
        self.h = ctx.enter_context(nc.semaphore(name))
        self.n = 0

    def inc(self, ins, k=1):
        ins.then_inc(self.h, k)
        self.n += k
        return ins


def build_program(dbg=(), upto=6):
    nc = bass.Bass("TRN2", target_bir_lowering=False)

    x_own = nc.dram_tensor("x_own", [TOWN, DM], F32, kind="ExternalInput").ap()
    x_sc = nc.dram_tensor("x_sc", [TOWN, DM], F32, kind="ExternalInput").ap()
    wqkv = nc.dram_tensor("wqkv", [DM, 768], F32, kind="ExternalInput").ap()
    wo = nc.dram_tensor("wo", [512, DM], F32, kind="ExternalInput").ap()
    slopes = nc.dram_tensor("slopes", [1, 4], F32, kind="ExternalInput").ap()
    g_qw = nc.dram_tensor("g_qw", [DM, 176], I32, kind="ExternalInput").ap()
    g_qz = nc.dram_tensor("g_qz", [32, 176], I32, kind="ExternalInput").ap()
    g_sc = nc.dram_tensor("g_sc", [32, 1408], F32, kind="ExternalInput").ap()
    u_qw = nc.dram_tensor("u_qw", [DM, 176], I32, kind="ExternalInput").ap()
    u_qz = nc.dram_tensor("u_qz", [32, 176], I32, kind="ExternalInput").ap()
    u_sc = nc.dram_tensor("u_sc", [32, 1408], F32, kind="ExternalInput").ap()
    d_qw = nc.dram_tensor("d_qw", [1408, 512], I32, kind="ExternalInput").ap()
    d_qz = nc.dram_tensor("d_qz", [IC, 512], I32, kind="ExternalInput").ap()
    d_sc = nc.dram_tensor("d_sc", [IC, DM], F32, kind="ExternalInput").ap()

    out_own = nc.dram_tensor("out_own", [TOWN, DM], F32, kind="ExternalOutput").ap()

    cc_xn = [nc.dram_tensor(f"cc_xn{i}", [DM, 512], BF16, kind="Internal").ap()
             for i in range(2)]
    xn_g = [nc.dram_tensor(f"xn_g{i}", [NCORE * DM, 512], BF16,
                           kind="Internal").ap() for i in range(2)]
    q_scr = nc.dram_tensor("q_scr", [512, S], BF16, kind="Internal").ap()
    o_scr = nc.dram_tensor("o_scr", [512, S], BF16, kind="Internal").ap()
    wo_bf = nc.dram_tensor("wo_bf", [512, DM], BF16, kind="Internal").ap()
    e_lin = nc.dram_tensor("e_lin", [4, EL], BF16, kind="Internal").ap()
    zf_g = nc.dram_tensor("zf_g", [32, 1408], I32, kind="Internal").ap()
    zf_u = nc.dram_tensor("zf_u", [32, 1408], I32, kind="Internal").ap()
    zf_d = nc.dram_tensor("zf_d", [IC, DM], I32, kind="Internal").ap()
    wg_dq = nc.dram_tensor("wg_dq", [32, 128, 1408], BF16, kind="Internal").ap()
    wu_dq = nc.dram_tensor("wu_dq", [32, 128, 1408], BF16, kind="Internal").ap()
    wd_dq = nc.dram_tensor("wd_dq", [IC, 128, DM], BF16, kind="Internal").ap()
    s_part = [nc.dram_tensor(f"s_part{r}", [1024, DM], F32, kind="Internal").ap()
              for r in range(8)]
    f_part = [nc.dram_tensor(f"f_part{r}", [1024, DM], F32, kind="Internal").ap()
              for r in range(8)]
    s_ownr = [nc.dram_tensor(f"s_ownr{r}", [128, DM], F32,
                             kind="Internal").ap() for r in range(8)]
    f_ownr = [nc.dram_tensor(f"f_ownr{r}", [128, DM], F32,
                             kind="Internal").ap() for r in range(8)]

    dbg_outs = {}
    dbg_srcs = {"xn_g0": xn_g[0], "xn_g1": xn_g[1], "q_scr": q_scr, "o_scr": o_scr,
                "e_lin": e_lin, "wg_dq": wg_dq, "wd_dq": wd_dq,
                "s_part0": s_part[0], "f_part0": f_part[0],
                "s_ownr0": s_ownr[0], "f_ownr0": f_ownr[0]}
    for name in dbg:
        src = dbg_srcs[name]
        dbg_outs[name] = nc.dram_tensor(
            "dbg_" + name, list(src.shape), src.dtype, kind="ExternalOutput").ap()

    groups = [list(range(NCORE))]

    NKB = [4 * (qt + 1) for qt in range(NQT)]
    KB_CUM = [0]
    PB_CUM = [0]
    for h in range(4):
        for qt in range(NQT):
            KB_CUM.append(KB_CUM[-1] + NKB[qt])
            PB_CUM.append(PB_CUM[-1] + NKB[qt] // 2)

    with ExitStack() as top:
        def sem(name):
            return Sem(top, nc, name)

        sPH = sem("sPH")
        tSELF = sem("tSELF"); aSELF = sem("aSELF"); vSELF = sem("vSELF")
        gSELF = sem("gSELF")
        sLX = sem("sLX"); sLX2 = sLX
        aSQ = sem("aSQ"); vST = sem("vST"); aST = sem("aST"); aXN = sem("aXN")
        tTR = sem("tTR"); vTS = sem("vTS")
        sWXNp = [sem("sWXNa"), sem("sWXNb")]
        sLQW = sem("sLQW")
        sLDQz = sem("sLDQz"); vDQz = sem("vDQz")
        sLDQp = [sem("sLDQa"), sem("sLDQb")]
        sWDQp = [sem("sWDQa"), sem("sWDQb")]
        vDQs = sem("vDQs"); vDQm = sem("vDQm"); tDQ = sem("tDQ")
        sSF = sem("sSF"); vDQb = sem("vDQb"); sZSp = [sem("sZSa"), sem("sZSb")]
        gEB = sem("gEB"); vEB = sem("vEB"); aEB = sem("aEB"); sEB = sem("sEB")
        cAGs = [sem("cAG0"), sem("cAG1")]
        sLXN4 = [sem(f"sLXN{i}") for i in range(4)]
        tQK = sem("tQK"); vEVQ = sem("vEVQ")
        tTRV = sem("tTRV"); vEVV = sem("vEVV"); sWQS = sem("sWQS")
        sEB2 = sem("sEB2"); sEB3 = sem("sEB3")
        sLQp = [sem("sLQa"), sem("sLQb")]
        tSC = sem("tSC"); aEX = sem("aEX"); vMK = sem("vMK")
        tPV = sem("tPV"); aOE = sem("aOE"); vRC = sem("vRC"); tLB = sem("tLB")
        vON = sem("vON"); sWOSp = [sem("sWOSa"), sem("sWOSb")]
        sLWOp = [sem("sLWOa"), sem("sLWOb")]
        sLORp = [sem("sLORa"), sem("sLORb")]
        tS4 = sem("tS4"); aSE = sem("aSE")
        sWSPp = [sem(f"sWSP{i}") for i in range(3)]
        cS = [sem(f"cS{r}") for r in range(8)]
        sLXF = sem("sLXF")
        sLWGp = [sem("sLWGa"), sem("sLWGb")]
        sLWDp = [sem(f"sLWD{i}") for i in range(3)]
        tGU = sem("tGU"); aGU = sem("aGU"); vH = sem("vH")
        tDN = sem("tDN"); vFE = sem("vFE")
        sWFPp = [sem("sWFPa"), sem("sWFPb")]
        cF = [sem(f"cF{r}") for r in range(8)]
        sLSFp = [sem("sLSFa"), sem("sLSFb")]
        aG6 = sem("aG6"); vOUT = sem("vOUT")
        sWOUTp = [sem("sWOUTa"), sem("sWOUTb")]

        def sbt(ctx, name, shape, dtype, side="left"):
            return ctx.enter_context(nc.sbuf_tensor(name, shape, dtype, side=side))

        def pst(ctx, name, shape, dtype):
            return ctx.enter_context(nc.psum_tensor(name, shape, dtype))

        ones_rowf = sbt(top, "ones_rowf", [1, 128], F32)
        ones_colb = sbt(top, "ones_colb", [128, 1], BF16)
        bias_c = sbt(top, "bias_c", [128, 1], F32)
        eps_b = sbt(top, "eps_b", [128, 1], F32)
        ident = sbt(top, "ident", [128, 128], BF16)
        slopes_ng = sbt(top, "slopes_ng", [128, 4], F32)
        slopes_rep = sbt(top, "slopes_rep", [128, 4], F32)

        PH = [0]

        def phase_start(eng):
            if PH[0] > 0:
                eng.wait_ge(sPH.h, 5 * PH[0])

        def phase_end_q(eng):
            sPH.n += 1
            eng.nop().then_inc(sPH.h, 1)

        def phase_end_c(eng, last_sem):
            eng.wait_ge(last_sem.h, last_sem.n)
            sPH.n += 1
            eng.nop().then_inc(sPH.h, 1)

        def bwait(eng, sems, g, U):
            eng.wait_ge(sems[g % 2].h, U * (g // 2 + 1))

        blk = top.enter_context(nc.Block())

        with ExitStack() as scWQ:
            wqkv_sb = sbt(scWQ, "wqkv_sb", [128, 32 * 768], BF16, side="left")

            with ExitStack() as p0:
                x_t = sbt(p0, "x_t", [128, DM], F32, side="right")
                sq_t = sbt(p0, "sq_t", [128, DM], F32, side="right")
                msum = sbt(p0, "msum", [128, 8], F32, side="right")
                qsum = sbt(p0, "qsum", [128, 8], F32, side="right")
                mean8 = sbt(p0, "mean8", [128, 8], F32, side="right")
                eq8 = sbt(p0, "eq8", [128, 8], F32, side="right")
                m28 = sbt(p0, "m28", [128, 8], F32, side="right")
                var8 = sbt(p0, "var8", [128, 8], F32, side="right")
                std8 = sbt(p0, "std8", [128, 8], F32, side="right")
                rinv8 = sbt(p0, "rinv8", [128, 8], F32, side="right")
                bneg8 = sbt(p0, "bneg8", [128, 8], F32, side="right")
                tmp8 = sbt(p0, "tmp8", [128, 8], F32, side="right")
                xn_tm = sbt(p0, "xn_tm", [128, 2 * DM], BF16, side="right")
                xn_stage = sbt(p0, "xn_stage", [128, 2 * 512], BF16, side="right")
                qw_t = sbt(p0, "qw_t", [128, 2 * 512], I32, side="right")
                nib_t = sbt(p0, "nib_t", [128, 2 * 2048], I32, side="right")
                d1_t = sbt(p0, "d1_t", [128, 2048], F32, side="right")
                w_t = sbt(p0, "w_t", [128, 2 * 2048], BF16, side="right")
                gz_sb = sbt(p0, "gz_sb", [32, 176], I32, side="right")
                uz_sb = sbt(p0, "uz_sb", [32, 176], I32, side="right")
                dz_sb = sbt(p0, "dz_sb", [IC, 512], I32, side="right")
                zfg_sb = sbt(p0, "zfg_sb", [32, 1408], I32, side="right")
                zfu_sb = sbt(p0, "zfu_sb", [32, 1408], I32, side="right")
                zfd_sb = sbt(p0, "zfd_sb", [IC, DM], I32, side="right")
                zfg_bf = sbt(p0, "zfg_bf", [32, 1408], BF16, side="right")
                zfu_bf = sbt(p0, "zfu_bf", [32, 1408], BF16, side="right")
                zfd_bf = sbt(p0, "zfd_bf", [IC, DM], BF16, side="right")
                sfg_bf = sbt(p0, "sfg_bf", [32, 1408], BF16, side="right")
                sfu_bf = sbt(p0, "sfu_bf", [32, 1408], BF16, side="right")
                sfd_bf = sbt(p0, "sfd_bf", [IC, DM], BF16, side="right")
                ones_rowb = sbt(p0, "ones_rowb", [1, 128], BF16, side="right")
                z_stg = sbt(p0, "z_stg", [1, 2 * 1536], BF16, side="right")
                s_stg = sbt(p0, "s_stg", [1, 2 * 1536], BF16, side="right")
                ramp = sbt(p0, "ramp", [128, 68], F32, side="right")
                rmax = sbt(p0, "rmax", [128, 68], F32, side="right")
                rneg = sbt(p0, "rneg", [128, 68], F32, side="right")
                rmix = sbt(p0, "rmix", [128, 68], F32, side="right")
                sq_row = sbt(p0, "sq_row", [128, 68], F32, side="right")
                e_rowf = sbt(p0, "e_rowf", [128, 68], F32, side="right")
                e_rowb = sbt(p0, "e_rowb", [128, 68], BF16, side="right")
                ps_tr = pst(p0, "ps_tr", [128, 2 * 1024], BF16)
                ps_zb = pst(p0, "ps_zb", [128, 1536], F32)
                ps_sb5 = pst(p0, "ps_sb5", [128, 1536], F32)

                dq_jobs = []
                for g in range(32):
                    dq_jobs.append(("g", g, 176, 1408, 0))
                for g in range(32):
                    dq_jobs.append(("u", g, 176, 1408, 0))
                for it in range(IC):
                    for oh in range(4):
                        dq_jobs.append(("d", it, 128, 1024, oh))

                @blk.gpsimd
                def _(gpsimd):
                    for g in range(32):
                        sLQW.inc(gpsimd.dma_start(
                            out=wqkv_sb[:, 768 * g:768 * (g + 1)],
                            in_=wqkv[128 * g:128 * (g + 1), :]), 16)
                    sLQW.inc(gpsimd.dma_start(out=wo_bf, in_=wo), 16)
                    sSF.inc(gpsimd.dma_start(out=sfg_bf[:, :], in_=g_sc), 16)
                    sSF.inc(gpsimd.dma_start(out=sfu_bf[:, :], in_=u_sc), 16)
                    sSF.inc(gpsimd.dma_start(out=sfd_bf[:, :], in_=d_sc), 16)
                    gpsimd.memset(ident[:, :], 1.0)
                    gSELF.inc(gpsimd.iota(
                        ramp[:, :], [[1, 68]], base=-512, channel_multiplier=68,
                        allow_small_or_imprecise_dtypes=True))
                    gpsimd.wait_ge(gSELF.h, gSELF.n)
                    gEB.inc(gpsimd.affine_select(
                        out=ident[:, :], in_=ident[:, :],
                        compare_op=ALU.is_equal, fill=0.0, base=0,
                        pattern=[[-1, 128]], channel_multiplier=1))
                    for tb in range(8):
                        for fg in range(8):
                            idx = tb * 8 + fg
                            gpsimd.wait_ge(vTS.h, idx + 1)
                            for j in range(4):
                                fb = 4 * fg + j
                                sWXNp[idx % 2].inc(gpsimd.dma_start(
                                    out=cc_xn[tb // 4][
                                        128 * fb:128 * (fb + 1),
                                        128 * (tb % 4):128 * (tb % 4 + 1)],
                                    in_=xn_stage[:, 512 * (idx % 2) + 128 * j:
                                                 512 * (idx % 2) + 128 * (j + 1)]),
                                    16)
                    gpsimd.wait_ge(sWXNp[0].h, 64 * 16)
                    gpsimd.wait_ge(sWXNp[1].h, 64 * 16)
                    cAGs[0].inc(gpsimd.collective_compute(
                        "AllGather", ALU.bypass, replica_groups=groups,
                        ins=[cc_xn[0].opt()], outs=[xn_g[0].opt()]))
                    gpsimd.wait_ge(sWXNp[0].h, 64 * 32)
                    gpsimd.wait_ge(sWXNp[1].h, 64 * 32)
                    cAGs[1].inc(gpsimd.collective_compute(
                        "AllGather", ALU.bypass, replica_groups=groups,
                        ins=[cc_xn[1].opt()], outs=[xn_g[1].opt()]))
                    gpsimd.wait_ge(sLQW.h, sLQW.n)
                    phase_end_q(gpsimd)

                @blk.sync
                def _(sync):
                    sLDQz.inc(sync.dma_start(out=slopes_rep[:, :],
                                             in_=slopes.broadcast_to([128, 4])), 16)
                    for tb in range(8):
                        if tb >= 1:
                            sync.wait_ge(vST.h, tb)
                            sync.wait_ge(aSQ.h, tb)
                        sLX.inc(sync.dma_start(
                            out=x_t[:, :],
                            in_=x_own[128 * tb:128 * (tb + 1), :]), 16)
                    for tb in range(8):
                        sync.wait_ge(vST.h, 8)
                        sync.wait_ge(aXN.h, tb)
                        sLX2.inc(sync.dma_start(
                            out=x_t[:, :],
                            in_=x_own[128 * tb:128 * (tb + 1), :]), 16)
                    sLDQz.inc(sync.dma_start(out=gz_sb[:, :], in_=g_qz), 16)
                    sLDQz.inc(sync.dma_start(out=uz_sb[:, :], in_=u_qz), 16)
                    sLDQz.inc(sync.dma_start(out=dz_sb[:, :], in_=d_qz), 16)

                    sync.wait_ge(vDQz.h, 3)
                    sync.wait_ge(sSF.h, 48)
                    def dq_store(k):
                        tag_, i0_, nw_, wout_, oh_ = dq_jobs[k]
                        b_ = k % 2
                        sync.wait_ge(vDQm.h, k + 1)
                        if tag_ == "g":
                            dst = bass.AP(wg_dq.tensor, i0_ * 128 * 1408,
                                          [[1408, 128], [1, 1408]])
                        elif tag_ == "u":
                            dst = bass.AP(wu_dq.tensor, i0_ * 128 * 1408,
                                          [[1408, 128], [1, 1408]])
                        else:
                            dst = bass.AP(wd_dq.tensor,
                                          i0_ * 128 * DM + 1024 * oh_,
                                          [[DM, 128], [1, 1024]])
                        sWDQp[b_].inc(sync.dma_start(
                            out=dst,
                            in_=w_t[:, 2048 * b_:2048 * b_ + wout_]), 16)

                    for k, (tag, i0, nw, wout, oh) in enumerate(dq_jobs):
                        if k >= 2:
                            sync.wait_ge(vDQs.h, k - 1)
                        b = k % 2
                        if tag == "g":
                            qsrc = g_qw[128 * i0:128 * (i0 + 1), :]
                            zrow, srow, off = zfg_bf, sfg_bf, 0
                        elif tag == "u":
                            qsrc = u_qw[128 * i0:128 * (i0 + 1), :]
                            zrow, srow, off = zfu_bf, sfu_bf, 0
                        else:
                            qsrc = d_qw[128 * i0:128 * (i0 + 1),
                                        128 * oh:128 * (oh + 1)]
                            zrow, srow, off = zfd_bf, sfd_bf, 1024 * oh
                        if k >= 2:
                            sync.wait_ge(tDQ.h, k - 1)
                        sZSp[b].inc(sync.dma_start(
                            out=z_stg[0:1, 1536 * b:1536 * b + wout],
                            in_=zrow[i0:i0 + 1, off:off + wout]), 16)
                        sZSp[b].inc(sync.dma_start(
                            out=s_stg[0:1, 1536 * b:1536 * b + wout],
                            in_=srow[i0:i0 + 1, off:off + wout]), 16)
                        sLDQp[b].inc(sync.dma_start(
                            out=qw_t[:, 512 * b:512 * b + nw], in_=qsrc), 16)
                        if k >= 1:
                            dq_store(k - 1)
                    dq_store(len(dq_jobs) - 1)
                    for h in range(4):
                        sync.wait_ge(vEB.h, 2 + h + 1)
                        sEB.inc(sync.dma_start(
                            out=bass.AP(e_lin.tensor, h * EL,
                                        [[68, 128], [1, 68]]),
                            in_=e_rowb[:, :]), 16)
                    for s_ in (sLX, sLDQz, sLDQp[0], sLDQp[1], sZSp[0],
                               sZSp[1], sWDQp[0], sWDQp[1], sEB):
                        sync.wait_ge(s_.h, s_.n)
                    phase_end_q(sync)

                @blk.tensor
                def _(tensor):
                    tensor.wait_ge(gEB.h, 1)
                    last = None
                    for tb in range(8):
                        tensor.wait_ge(aXN.h, tb + 1)
                        for fg in range(8):
                            idx = tb * 8 + fg
                            if idx >= 2:
                                tensor.wait_ge(vTS.h, idx - 1)
                            for j in range(4):
                                fb = 4 * fg + j
                                last = tensor.transpose(
                                    out=ps_tr[:, 1024 * (idx % 2) + 128 * j:
                                              1024 * (idx % 2) + 128 * (j + 1)],
                                    in_=xn_tm[:, DM * (tb % 2) + 128 * fb:
                                              DM * (tb % 2) + 128 * (fb + 1)],
                                    identity=ident[:, :])
                            tTR.inc(last)
                    for k, (tag, i0, nw, wout, oh) in enumerate(dq_jobs):
                        b = k % 2
                        tensor.wait_ge(sZSp[k % 2].h,
                                       32 * (k // 2 + 1))
                        if k >= 1:
                            tensor.wait_ge(vDQb.h, k)
                            tensor.wait_ge(vDQm.h, k)
                        last = None
                        for c0 in range(0, wout, 512):
                            w = min(512, wout - c0)
                            tensor.matmul(
                                out=ps_zb[:, c0:c0 + w],
                                lhsT=ones_rowb[:, :],
                                rhs=z_stg[0:1, 1536 * b + c0:1536 * b + c0 + w],
                                start=True, stop=True, skip_group_check=True)
                            last = tensor.matmul(
                                out=ps_sb5[:, c0:c0 + w],
                                lhsT=ones_rowb[:, :],
                                rhs=s_stg[0:1, 1536 * b + c0:1536 * b + c0 + w],
                                start=True, stop=True, skip_group_check=True)
                        tDQ.inc(last)
                    phase_end_c(tensor, tDQ)

                @blk.scalar
                def _(scalar):
                    scalar.preload_act_table(AF.Exp)
                    for tb in range(8):
                        scalar.wait_ge(sLX.h, 16 * (tb + 1))
                        if tb >= 1:
                            scalar.wait_ge(vST.h, tb)
                        aSQ.inc(scalar.activation(out=sq_t[:, :], in_=x_t[:, :],
                                                  func=AF.Square))
                    scalar.wait_ge(vSELF.h, 4)
                    scalar.preload_act_table(AF.Sqrt)
                    aST.inc(scalar.activation(out=std8[:, :], in_=var8[:, :],
                                              func=AF.Sqrt, bias=eps_b[:, :]))
                    scalar.preload_act_table(AF.Exp)
                    for tb in range(8):
                        scalar.wait_ge(sLX2.h, 16 * (8 + tb + 1))
                        scalar.wait_ge(vSELF.h, 7)
                        if tb >= 2:
                            scalar.wait_ge(tTR.h, 8 * (tb - 1))
                        aXN.inc(scalar.activation(
                            out=xn_tm[:, DM * (tb % 2):DM * (tb % 2 + 1)],
                            in_=x_t[:, :],
                            func=AF.Identity, scale=rinv8[:, tb:tb + 1],
                            bias=bneg8[:, tb:tb + 1]))
                    scalar.wait_ge(vEB.h, 1)
                    scalar.preload_act_table(AF.Sqrt)
                    aSELF.inc(scalar.activation(out=sq_row[:, :], in_=rmix[:, :],
                                                func=AF.Sqrt))
                    scalar.wait_ge(aSELF.h, aSELF.n)
                    scalar.preload_act_table(AF.Exp)
                    last = None
                    for h in range(4):
                        scalar.wait_ge(vEB.h, 2 + h)
                        last = scalar.activation(
                            out=e_rowf[:, :],
                            in_=sq_row[:, :], func=AF.Exp,
                            scale=slopes_ng[:, h:h + 1])
                        aEB.inc(last)
                    phase_end_c(scalar, aEB)

                @blk.vector
                def _(vector):
                    vector.memset(ones_rowf[:, :], 1.0)
                    vector.memset(ones_colb[:, :], 1.0)
                    vector.memset(bias_c[:, :], -CSHIFT)
                    vector.memset(eps_b[:, :], 1e-5)
                    for tb in range(8):
                        vector.wait_ge(sLX.h, 16 * (tb + 1))
                        vector.wait_ge(aSQ.h, tb + 1)
                        vector.reduce_sum(out=msum[:, tb:tb + 1], in_=x_t[:, :],
                                          axis=AX.X)
                        vST.inc(vector.reduce_sum(out=qsum[:, tb:tb + 1],
                                                  in_=sq_t[:, :], axis=AX.X))
                    vector.wait_ge(vST.h, 8)
                    vSELF.inc(vector.tensor_scalar_mul(
                        out=mean8[:, :], in0=msum[:, :], scalar1=1.0 / DM))
                    vSELF.inc(vector.tensor_scalar_mul(
                        out=eq8[:, :], in0=qsum[:, :], scalar1=1.0 / DM))
                    vector.wait_ge(vSELF.h, 2)
                    vSELF.inc(vector.tensor_mul(out=m28[:, :], in0=mean8[:, :],
                                                in1=mean8[:, :]))
                    vector.wait_ge(vSELF.h, 3)
                    vSELF.inc(vector.tensor_sub(out=var8[:, :], in0=eq8[:, :],
                                                in1=m28[:, :]))
                    vector.wait_ge(aST.h, 1)
                    vSELF.inc(vector.reciprocal(out=rinv8[:, :], in_=std8[:, :]))
                    vector.wait_ge(vSELF.h, 5)
                    vSELF.inc(vector.tensor_mul(out=tmp8[:, :], in0=mean8[:, :],
                                                in1=rinv8[:, :]))
                    vector.wait_ge(vSELF.h, 6)
                    vSELF.inc(vector.tensor_scalar_mul(
                        out=bneg8[:, :], in0=tmp8[:, :], scalar1=-1.0))
                    for tb in range(8):
                        for fg in range(8):
                            idx = tb * 8 + fg
                            vector.wait_ge(tTR.h, idx + 1)
                            if idx >= 2:
                                bwait(vector, sWXNp, idx - 2, 64)
                            vTS.inc(vector.tensor_copy(
                                out=xn_stage[:, 512 * (idx % 2):
                                             512 * (idx % 2 + 1)],
                                in_=ps_tr[:, 1024 * (idx % 2):
                                          1024 * (idx % 2) + 512]))
                    vector.memset(ones_rowb[:, :], 1.0)
                    vector.wait_ge(sLDQz.h, 16 * 4)
                    for zsrc, dstz, dstzb, wtot in (
                            (gz_sb, zfg_sb, zfg_bf, 1408),
                            (uz_sb, zfu_sb, zfu_bf, 1408),
                            (dz_sb, zfd_sb, zfd_bf, DM)):
                        last = None
                        for j in range(8):
                            last = vector.tensor_scalar(
                                out=dstz[:, j:wtot:8], in0=zsrc[:, :],
                                scalar1=4 * j, scalar2=15,
                                op0=ALU.logical_shift_right,
                                op1=ALU.bitwise_and)
                        vSELF.inc(last)
                        vector.wait_ge(vSELF.h, vSELF.n)
                        vDQz.inc(vector.tensor_copy(out=dstzb[:, :],
                                                    in_=dstz[:, :]))
                    def nib_job(k):
                        if k >= len(dq_jobs):
                            return
                        tag_, i0_, nw_, wout_, oh_ = dq_jobs[k]
                        bwait(vector, sLDQp, k, 16)
                        if k >= 2:
                            vector.wait_ge(vDQb.h, k - 1)
                        last_ = None
                        for j in range(8):
                            last_ = vector.tensor_scalar(
                                out=nib_t[:, 2048 * (k % 2) + j:
                                          2048 * (k % 2) + wout_:8],
                                in0=qw_t[:, 512 * (k % 2):512 * (k % 2) + nw_],
                                scalar1=4 * j, scalar2=15,
                                op0=ALU.logical_shift_right,
                                op1=ALU.bitwise_and)
                        vDQs.inc(last_)

                    nib_job(0)
                    nib_job(1)
                    for k, (tag, i0, nw, wout, oh) in enumerate(dq_jobs):
                        b = k % 2
                        vector.wait_ge(vDQs.h, k + 1)
                        vector.wait_ge(tDQ.h, k + 1)
                        vDQb.inc(vector.tensor_sub(
                            out=d1_t[:, 0:wout],
                            in0=nib_t[:, 2048 * b:2048 * b + wout],
                            in1=ps_zb[:, 0:wout]))
                        vector.wait_ge(vDQb.h, vDQb.n)
                        if k >= 2:
                            bwait(vector, sWDQp, k - 2, 16)
                        vDQm.inc(vector.tensor_mul(
                            out=w_t[:, 2048 * b:2048 * b + wout],
                            in0=d1_t[:, 0:wout],
                            in1=ps_sb5[:, 0:wout]))
                        nib_job(k + 2)
                    vector.wait_ge(gEB.h, 1)
                    vector.tensor_scalar_max(out=rmax[:, :], in0=ramp[:, :],
                                             scalar1=0.0)
                    vSELF.inc(vector.tensor_scalar_min(out=rneg[:, :],
                                                       in0=ramp[:, :],
                                                       scalar1=0.0))
                    vector.wait_ge(vSELF.h, vSELF.n)
                    vEB.inc(vector.scalar_tensor_tensor(
                        out=rmix[:, :], in0=rneg[:, :], scalar=BIGNEG,
                        in1=rmax[:, :], op0=ALU.mult, op1=ALU.add))
                    vector.wait_ge(sLDQz.h, 16 * 4)
                    vEB.inc(vector.tensor_scalar_mul(
                        out=slopes_ng[:, :], in0=slopes_rep[:, :],
                        scalar1=-1.0))
                    last = None
                    for h in range(4):
                        vector.wait_ge(aEB.h, h + 1)
                        if h >= 1:
                            vector.wait_ge(sEB.h, 16 * h)
                        last = vector.tensor_copy(
                            out=e_rowb[:, :], in_=e_rowf[:, :])
                        vEB.inc(last)
                    phase_end_c(vector, vEB)

            PH[0] += 1
            assert sPH.n == 5 * PH[0], sPH.n

            with ExitStack() as scKV:
                k_fm = sbt(scKV, "k_fm", [128, S], BF16, side="right")
                v_tm = sbt(scKV, "v_tm", [128, S], BF16, side="right")
                e_sb = sbt(scKV, "e_sb", [128, 4 * EW], BF16, side="right")

                with ExitStack() as p2:
                    xn_t2 = sbt(p2, "xn_t2", [128, 4 * 512], BF16, side="right")
                    q_stage = sbt(p2, "q_stage", [128, 4 * 512], BF16, side="right")
                    v_stage = sbt(p2, "v_stage", [128, 512], BF16, side="right")
                    ps_qkv = pst(p2, "ps_qkv", [128, 6 * 512], F32)
                    ps_vt = pst(p2, "ps_vt", [128, 512], BF16)

                    @blk.sync
                    def _(sync):
                        phase_start(sync)
                        for tt in range(NQT):
                            sync.wait_ge(cAGs[tt % 2].h, 1)
                            for g in range(32):
                                gi = 32 * tt + g
                                if gi % 2 != 0:
                                    continue
                                if gi >= 4:
                                    sync.wait_ge(tQK.h, gi - 3)
                                sLXN4[gi % 4].inc(sync.dma_start(
                                    out=xn_t2[:, 512 * (gi % 4):512 * (gi % 4 + 1)],
                                    in_=xn_g[tt % 2][
                                        DM * (tt // 2) + 128 * g:
                                        DM * (tt // 2) + 128 * (g + 1), :]), 16)
                                r = gi // 2
                                h_, i_ = divmod(r, 128)
                                sEB2.inc(sync.dma_start(
                                    out=e_sb[i_:i_ + 1, EW * h_:EW * (h_ + 1)],
                                    in_=bass.AP(e_lin.tensor, h_ * EL + 128 - i_,
                                                [[1, 1], [1, EW]])), 16)
                            sync.wait_ge(vEVQ.h, tt + 1)
                            for f in range(4):
                                sWQS.inc(sync.dma_start(
                                    out=q_scr[128 * f:128 * (f + 1),
                                              512 * tt:512 * (tt + 1)],
                                    in_=q_stage[:, 512 * f:512 * (f + 1)]), 16)
                        for s_ in (sLXN4[0], sLXN4[2], sWQS, sEB2):
                            sync.wait_ge(s_.h, s_.n)
                        phase_end_q(sync)

                    @blk.gpsimd
                    def _(gpsimd):
                        phase_start(gpsimd)
                        for tt in range(NQT):
                            gpsimd.wait_ge(cAGs[tt % 2].h, 1)
                            for g in range(32):
                                gi = 32 * tt + g
                                if gi % 2 != 1:
                                    continue
                                if gi >= 4:
                                    gpsimd.wait_ge(tQK.h, gi - 3)
                                sLXN4[gi % 4].inc(gpsimd.dma_start(
                                    out=xn_t2[:, 512 * (gi % 4):512 * (gi % 4 + 1)],
                                    in_=xn_g[tt % 2][
                                        DM * (tt // 2) + 128 * g:
                                        DM * (tt // 2) + 128 * (g + 1), :]), 16)
                                r = 256 + gi // 2
                                h_, i_ = divmod(r, 128)
                                sEB3.inc(gpsimd.dma_start(
                                    out=e_sb[i_:i_ + 1, EW * h_:EW * (h_ + 1)],
                                    in_=bass.AP(e_lin.tensor, h_ * EL + 128 - i_,
                                                [[1, 1], [1, EW]])), 16)
                        gpsimd.wait_ge(sEB3.h, sEB3.n)
                        gpsimd.wait_ge(sLXN4[1].h, sLXN4[1].n)
                        gpsimd.wait_ge(sLXN4[3].h, sLXN4[3].n)
                        phase_end_q(gpsimd)

                    @blk.tensor
                    def _(tensor):
                        phase_start(tensor)
                        last = None
                        for tt in range(NQT):
                            for g in range(32):
                                gi = 32 * tt + g
                                tensor.wait_ge(sLXN4[gi % 4].h, 16 * (gi // 4 + 1))
                                if g == 0 and tt >= 1:
                                    tensor.wait_ge(vEVQ.h, tt)
                                for f in range(6):
                                    last = tensor.matmul(
                                        out=ps_qkv[:, 512 * f:512 * (f + 1)],
                                        lhsT=wqkv_sb[:, 768 * g + 128 * f:
                                                     768 * g + 128 * (f + 1)],
                                        rhs=xn_t2[:, 512 * (gi % 4):
                                                  512 * (gi % 4 + 1)],
                                        start=(g == 0), stop=(g == 31),
                                        skip_group_check=True)
                                tQK.inc(last)
                            tensor.wait_ge(vEVQ.h, tt + 1)
                            if tt >= 1:
                                tensor.wait_ge(vEVV.h, tt)
                            for j in range(4):
                                last = tensor.transpose(
                                    out=ps_vt[:, 128 * j:128 * (j + 1)],
                                    in_=v_stage[:, 128 * j:128 * (j + 1)],
                                    identity=ident[:, :])
                            tTRV.inc(last)
                        phase_end_c(tensor, tTRV)

                    @blk.scalar
                    def _(scalar):
                        phase_start(scalar)
                        phase_end_q(scalar)

                    @blk.vector
                    def _(vector):
                        phase_start(vector)
                        last = None
                        for tt in range(NQT):
                            vector.wait_ge(tQK.h, 32 * (tt + 1))
                            if tt >= 1:
                                vector.wait_ge(sWQS.h, 16 * 4 * tt)
                            for f in range(4):
                                vector.tensor_copy(
                                    out=q_stage[:, 512 * f:512 * (f + 1)],
                                    in_=ps_qkv[:, 512 * f:512 * (f + 1)])
                            vector.tensor_copy(
                                out=k_fm[:, 512 * tt:512 * (tt + 1)],
                                in_=ps_qkv[:, 512 * 4:512 * 5])
                            vEVQ.inc(vector.tensor_copy(
                                out=v_stage[:, :], in_=ps_qkv[:, 512 * 5:512 * 6]))
                            vector.wait_ge(tTRV.h, tt + 1)
                            last = vector.tensor_copy(
                                out=v_tm[:, 512 * tt:512 * (tt + 1)],
                                in_=ps_vt[:, :])
                            vEVV.inc(last)
                        phase_end_c(vector, vEVV)

                PH[0] += 1
                assert sPH.n == 5 * PH[0], sPH.n

                if True:
                    with ExitStack() as p3:
                        q_t = sbt(p3, "q_t", [128, 2 * 512], BF16, side="right")
                        p_sb = sbt(p3, "p_sb", [128, 4 * 512], BF16, side="right")
                        pm_sb = sbt(p3, "pm_sb", [128, 4 * 512], BF16, side="right")
                        o_raw = sbt(p3, "o_raw", [128, 2 * 512], F32, side="right")
                        o_stage = sbt(p3, "o_stage", [128, 2 * 512], BF16,
                                      side="right")
                        linv = sbt(p3, "linv", [1, 512], F32, side="right")
                        ps_sc = pst(p3, "ps_sc", [128, 4 * 512], F32)
                        ps_po = pst(p3, "ps_po", [128, 2 * 512], F32)
                        ps_pl = pst(p3, "ps_pl", [1, 512], F32)
                        ps_plb = pst(p3, "ps_plb", [128, 512], F32)

                        @blk.sync
                        def _(sync):
                            phase_start(sync)
                            def o_store(sidx):
                                hh, qq = divmod(sidx, NQT)
                                sync.wait_ge(vON.h, sidx + 1)
                                sWOSp[sidx % 2].inc(sync.dma_start(
                                    out=o_scr[128 * hh:128 * (hh + 1),
                                              512 * qq:512 * (qq + 1)],
                                    in_=o_stage[:, 512 * (sidx % 2):
                                                512 * (sidx % 2 + 1)]), 16)

                            for idx in range(64):
                                if idx >= 2:
                                    sync.wait_ge(tSC.h, PB_CUM[idx - 1])
                                h, qt = divmod(idx, NQT)
                                sLQp[idx % 2].inc(sync.dma_start(
                                    out=q_t[:, 512 * (idx % 2):512 * (idx % 2 + 1)],
                                    in_=q_scr[128 * h:128 * (h + 1),
                                              512 * qt:512 * (qt + 1)]), 16)
                                if idx >= 2:
                                    o_store(idx - 2)
                            o_store(62)
                            o_store(63)
                            for s_ in (sLQp[0], sLQp[1], sWOSp[0], sWOSp[1]):
                                sync.wait_ge(s_.h, s_.n)
                            phase_end_q(sync)

                        @blk.gpsimd
                        def _(gpsimd):
                            phase_start(gpsimd)
                            phase_end_q(gpsimd)

                        @blk.tensor
                        def _(tensor):
                            phase_start(tensor)
                            last = None
                            for idx in range(64):
                                h, qt = divmod(idx, NQT)
                                n = NKB[qt]
                                K0 = KB_CUM[idx]
                                bwait(tensor, sLQp, idx, 16)
                                if idx >= 1:
                                    tensor.wait_ge(aOE.h, idx - 1)
                                    tensor.wait_ge(vRC.h, idx)
                                qs = 512 * (idx % 2)

                                np_ = n // 2
                                P0_ = PB_CUM[idx]

                                def score_pair(p):
                                    pp = P0_ + p
                                    sp = pp % 2
                                    if pp >= 2:
                                        tensor.wait_ge(aEX.h, pp - 1)
                                    for u in range(2):
                                        j = 2 * p + u
                                        ins = tensor.matmul(
                                            out=ps_sc[:, 1024 * sp + 512 * u:
                                                      1024 * sp + 512 * (u + 1)],
                                            lhsT=k_fm[:, 128 * j:128 * (j + 1)],
                                            rhs=q_t[:, qs:qs + 512],
                                            start=True, stop=True,
                                            skip_group_check=True)
                                    tSC.inc(ins)

                                score_pair(0)
                                if np_ > 1:
                                    score_pair(1)
                                for p in range(np_):
                                    pp = P0_ + p
                                    sp = pp % 2
                                    if p + 2 < np_:
                                        score_pair(p + 2)
                                    tensor.wait_ge(vMK.h, pp + 1)
                                    for u in range(2):
                                        j = 2 * p + u
                                        kk = K0 + j
                                        tensor.matmul(
                                            out=ps_po[:, 512 * (idx % 2):
                                                      512 * (idx % 2 + 1)],
                                            lhsT=v_tm[:, 128 * j:128 * (j + 1)],
                                            rhs=pm_sb[:, 1024 * sp + 512 * u:
                                                      1024 * sp + 512 * (u + 1)],
                                            start=(j == 0), stop=(j == n - 1),
                                            skip_group_check=True)
                                        last = tensor.matmul(
                                            out=ps_pl[0:1, :],
                                            lhsT=ones_colb[:, :],
                                            rhs=pm_sb[:, 1024 * sp + 512 * u:
                                                      1024 * sp + 512 * (u + 1)],
                                            start=(j == 0), stop=(j == n - 1),
                                            skip_group_check=True)
                                        tPV.inc(last)
                                tensor.wait_ge(vRC.h, idx + 1)
                                if idx >= 1:
                                    tensor.wait_ge(vON.h, idx)
                                last = tensor.matmul(
                                    out=ps_plb[:, :],
                                    lhsT=ones_rowf[:, :], rhs=linv[0:1, :],
                                    start=True, stop=True,
                                    skip_group_check=True)
                                tLB.inc(last)
                            phase_end_c(tensor, tLB)

                        @blk.scalar
                        def _(scalar):
                            phase_start(scalar)
                            last = None
                            for idx in range(64):
                                h, qt = divmod(idx, NQT)
                                n = NKB[qt]
                                K0 = KB_CUM[idx]
                                np_ = n // 2
                                P0_ = PB_CUM[idx]
                                for p in range(np_):
                                    pp = P0_ + p
                                    sp = pp % 2
                                    scalar.wait_ge(tSC.h, pp + 1)
                                    if pp >= 2:
                                        scalar.wait_ge(vMK.h, pp - 1)
                                    aEX.inc(scalar.activation(
                                        out=p_sb[:, 1024 * sp:1024 * (sp + 1)],
                                        in_=ps_sc[:, 1024 * sp:1024 * (sp + 1)],
                                        func=AF.Exp, scale=SCALE,
                                        bias=bias_c[:, :]))
                                scalar.wait_ge(tPV.h, K0 + n)
                                if idx >= 2:
                                    scalar.wait_ge(vON.h, idx - 1)
                                last = scalar.activation(
                                    out=o_raw[:, 512 * (idx % 2):
                                              512 * (idx % 2 + 1)],
                                    in_=ps_po[:, 512 * (idx % 2):
                                              512 * (idx % 2 + 1)],
                                    func=AF.Copy)
                                aOE.inc(last)
                            phase_end_c(scalar, aOE)

                        @blk.vector
                        def _(vector):
                            phase_start(vector)
                            last = None
                            for idx in range(64):
                                h, qt = divmod(idx, NQT)
                                n = NKB[qt]
                                K0 = KB_CUM[idx]
                                np_ = n // 2
                                P0_ = PB_CUM[idx]
                                for p in range(np_):
                                    pp = P0_ + p
                                    sp = pp % 2
                                    vector.wait_ge(aEX.h, pp + 1)
                                    if pp >= 2:
                                        vector.wait_ge(tPV.h, 2 * (pp - 1))
                                    ins = None
                                    for u in range(2):
                                        j = 2 * p + u
                                        u0 = 512 * qt - 128 * j + 384
                                        ins = vector.tensor_mul(
                                            out=pm_sb[:, 1024 * sp + 512 * u:
                                                      1024 * sp + 512 * (u + 1)],
                                            in0=p_sb[:, 1024 * sp + 512 * u:
                                                     1024 * sp + 512 * (u + 1)],
                                            in1=e_sb[:, EW * h + u0:
                                                     EW * h + u0 + 512])
                                    vMK.inc(ins)
                                vector.wait_ge(tPV.h, K0 + n)
                                vector.wait_ge(tLB.h, idx)
                                vRC.inc(vector.reciprocal(out=linv[0:1, :],
                                                          in_=ps_pl[0:1, :]))
                                vector.wait_ge(tLB.h, idx + 1)
                                vector.wait_ge(aOE.h, idx + 1)
                                if idx >= 2:
                                    bwait(vector, sWOSp, idx - 2, 16)
                                last = vector.tensor_mul(
                                    out=o_stage[:, 512 * (idx % 2):
                                                512 * (idx % 2 + 1)],
                                    in0=o_raw[:, 512 * (idx % 2):
                                              512 * (idx % 2 + 1)],
                                    in1=ps_plb[:, :])
                                vON.inc(last)
                            phase_end_c(vector, vON)

                    PH[0] += 1
                    assert sPH.n == 5 * PH[0], sPH.n

                    with ExitStack() as p4:
                        wo_sb = sbt(p4, "wo_sb", [128, 2 * 4 * 512], BF16,
                                    side="right")
                        o_rc = sbt(p4, "o_rc", [128, 2 * 4 * 1024], BF16,
                                   side="right")
                        s_stage = sbt(p4, "s_stage", [128, 3 * 512], F32,
                                      side="right")
                        ps_s = pst(p4, "ps_s", [128, 3 * 512], F32)

                        @blk.gpsimd
                        def _(gpsimd):
                            phase_start(gpsimd)
                            phase_end_q(gpsimd)

                        @blk.sync
                        def _(sync):
                            phase_start(sync)
                            for rc in range(8):
                                if rc >= 2:
                                    sync.wait_ge(tS4.h, 64 * (rc - 1))
                                for h in range(4):
                                    sLORp[rc % 2].inc(sync.dma_start(
                                        out=o_rc[:, 4096 * (rc % 2) + 1024 * h:
                                                 4096 * (rc % 2)
                                                 + 1024 * (h + 1)],
                                        in_=o_scr[128 * h:128 * (h + 1),
                                                  1024 * rc:1024 * (rc + 1)]), 16)
                                for r1 in range(64):
                                    gtb = 64 * rc + r1
                                    dm8, tb = divmod(r1, 8)
                                    sync.wait_ge(aSE.h, gtb + 1)
                                    if gtb >= 3:
                                        sync.wait_ge(sWSPp[gtb % 3].h,
                                                     16 * ((gtb - 3) // 3 + 1))
                                    sWSPp[gtb % 3].inc(sync.dma_start(
                                        out=s_part[rc][128 * tb:128 * (tb + 1),
                                                       512 * dm8:512 * (dm8 + 1)],
                                        in_=s_stage[:, 512 * (gtb % 3):
                                                    512 * (gtb % 3 + 1)]), 16)
                            for s_ in (sLORp[0], sLORp[1], sWSPp[0], sWSPp[1],
                                       sWSPp[2]):
                                sync.wait_ge(s_.h, s_.n)
                            phase_end_q(sync)

                        @blk.tensor
                        def _(tensor):
                            phase_start(tensor)
                            last = None
                            for rc in range(8):
                                bwait(tensor, sLORp, rc, 64)
                                for dm8 in range(8):
                                    gi = 8 * rc + dm8
                                    bwait(tensor, sLWOp, gi, 64)
                                    for tb in range(8):
                                        gtb = 8 * gi + tb
                                        if gtb >= 3:
                                            tensor.wait_ge(aSE.h, gtb - 2)
                                        for h in range(4):
                                            last = tensor.matmul(
                                                out=ps_s[:, 512 * (gtb % 3):
                                                         512 * (gtb % 3 + 1)],
                                                lhsT=o_rc[:, 4096 * (rc % 2)
                                                          + 1024 * h + 128 * tb:
                                                          4096 * (rc % 2)
                                                          + 1024 * h
                                                          + 128 * (tb + 1)],
                                                rhs=wo_sb[:, 2048 * (gi % 2)
                                                          + 512 * h:
                                                          2048 * (gi % 2)
                                                          + 512 * (h + 1)],
                                                start=(h == 0), stop=(h == 3),
                                                skip_group_check=True)
                                        tS4.inc(last)
                            phase_end_c(tensor, tS4)

                        @blk.scalar
                        def _(scalar):
                            phase_start(scalar)

                            def wo_load(gi):
                                dm8 = gi % 8
                                if gi >= 2:
                                    scalar.wait_ge(tS4.h, 8 * (gi - 1))
                                for h in range(4):
                                    sLWOp[gi % 2].inc(scalar.dma_start(
                                        out=wo_sb[:, 2048 * (gi % 2) + 512 * h:
                                                  2048 * (gi % 2) + 512 * (h + 1)],
                                        in_=wo_bf[128 * h:128 * (h + 1),
                                                  512 * dm8:512 * (dm8 + 1)]), 16)

                            wo_load(0)
                            wo_load(1)
                            last = None
                            for gtb in range(512):
                                scalar.wait_ge(tS4.h, gtb + 1)
                                if gtb >= 3:
                                    scalar.wait_ge(sWSPp[gtb % 3].h,
                                                   16 * ((gtb - 3) // 3 + 1))
                                last = scalar.activation(
                                    out=s_stage[:, 512 * (gtb % 3):
                                                512 * (gtb % 3 + 1)],
                                    in_=ps_s[:, 512 * (gtb % 3):
                                             512 * (gtb % 3 + 1)],
                                    func=AF.Copy)
                                aSE.inc(last)
                                if gtb % 8 == 7 and gtb // 8 + 2 < 64:
                                    wo_load(gtb // 8 + 2)
                            scalar.wait_ge(sLWOp[0].h, sLWOp[0].n)
                            scalar.wait_ge(sLWOp[1].h, sLWOp[1].n)
                            phase_end_c(scalar, aSE)

                        @blk.vector
                        def _(vector):
                            phase_start(vector)
                            phase_end_q(vector)

                    PH[0] += 1
                    assert sPH.n == 5 * PH[0], sPH.n

        NP5 = 8
        IQ6 = [2, 2, 2, 2, 2, 1]
        IS6 = [0, 2, 4, 6, 8, 10]
        P5_SEQ = [(j, hf, IS6[j] + itl) for j in range(6)
                  for hf in range(2) for itl in range(IQ6[j])]
        assert len(P5_SEQ) == 22

        def p5_cons(gc):
            pc, jc = divmod(gc, 6)
            return 22 * pc + 2 * (IS6[jc] + IQ6[jc])

        with ExitStack() as p5:
            xn_f = sbt(p5, "xn_f", [128, 2 * 32 * 512], BF16, side="left")
            wgu_q = sbt(p5, "wgu_q", [128, 2 * 2 * 32 * 256], BF16, side="left")
            wd_sb = sbt(p5, "wd_sb", [128, 3 * IC * 512], BF16, side="left")
            h_sb = sbt(p5, "h_sb", [128, IC * 1024], BF16, side="left")
            sg_sb = sbt(p5, "sg_sb", [128, 2 * 512], BF16, side="left")
            su_sb = sbt(p5, "su_sb", [128, 2 * 512], BF16, side="left")
            f_stage = sbt(p5, "f_stage", [128, 2 * 512], F32, side="left")
            ps_g = pst(p5, "ps_g", [128, 2 * 512], F32)
            ps_u = pst(p5, "ps_u", [128, 2 * 512], F32)
            ps_f = pst(p5, "ps_f", [128, 2 * 512], F32)

            WQ2 = 256

            def gu_sl(buf, w, g, lo, width, nit):
                base = (2 * buf + w) * 32 * WQ2 + g * (128 * nit) + lo
                return wgu_q[:, base:base + width]

            @blk.sync
            def _(sync):
                phase_start(sync)

                def wg_half(gc):
                    if gc >= 6 * NP5:
                        return
                    j = gc % 6
                    nit = IQ6[j]
                    i0 = IS6[j]
                    if gc >= 2:
                        sync.wait_ge(tGU.h, p5_cons(gc - 2))
                    sLWGp[gc % 2].inc(sync.dma_start(
                        out=gu_sl(gc % 2, 0, 0, 0, 32 * 128 * nit, nit),
                        in_=bass.AP(wg_dq.tensor, 128 * i0,
                                    [[1408, 128], [128 * 1408, 32],
                                     [1, 128 * nit]])), 16)

                def xn_load(p):
                    if p >= NP5:
                        return
                    if p >= 1:
                        sync.wait_ge(tGU.h, 22 * p)
                    for hf in range(2):
                        sync.wait_ge(cAGs[hf].h, 1)
                        sLXF.inc(sync.dma_start(
                            out=xn_f[:, 32 * 512 * hf:32 * 512 * (hf + 1)],
                            in_=bass.AP(xn_g[hf].tensor, DM * p * 512,
                                        [[512, 128], [128 * 512, 32],
                                         [1, 512]])), 16)

                xn_load(0)
                wg_half(0)
                wg_half(1)
                for p in range(NP5):
                    wg_half(6 * p + 2)
                    wg_half(6 * p + 3)
                    wg_half(6 * p + 4)
                    wg_half(6 * p + 5)
                    wg_half(6 * p + 6)
                    wg_half(6 * p + 7)
                    xn_load(p + 1)
                    for ftl in range(64):
                        q8, tb2 = divmod(ftl, 8)
                        ftb = 64 * p + ftl
                        sync.wait_ge(vFE.h, ftb + 1)
                        sWFPp[ftb % 2].inc(sync.dma_start(
                            out=f_part[p][128 * tb2:128 * (tb2 + 1),
                                          512 * q8:512 * (q8 + 1)],
                            in_=f_stage[:, 512 * (ftb % 2):
                                        512 * (ftb % 2 + 1)]), 16)
                for s_ in (sWFPp[0], sWFPp[1], sLWGp[0], sLWGp[1], sLXF):
                    sync.wait_ge(s_.h, s_.n)
                phase_end_q(sync)

            @blk.gpsimd
            def _(gpsimd):
                phase_start(gpsimd)
                for p in range(NP5):
                    for _p, _sw in enumerate(sWSPp):
                        _cnt = sum(1 for _g in range(64 * (p + 1))
                                   if _g % 3 == _p)
                        gpsimd.wait_ge(_sw.h, 16 * _cnt)
                    cS[p].inc(gpsimd.collective_compute(
                        "ReduceScatter", ALU.add, replica_groups=groups,
                        ins=[s_part[p].opt()], outs=[s_ownr[p].opt()]))
                    if p >= 1:
                        gpsimd.wait_ge(sWFPp[0].h, 16 * 32 * p)
                        gpsimd.wait_ge(sWFPp[1].h, 16 * 32 * p)
                        cF[p - 1].inc(gpsimd.collective_compute(
                            "ReduceScatter", ALU.add, replica_groups=groups,
                            ins=[f_part[p - 1].opt()],
                            outs=[f_ownr[p - 1].opt()]))
                gpsimd.wait_ge(sWFPp[0].h, 16 * 32 * 8)
                gpsimd.wait_ge(sWFPp[1].h, 16 * 32 * 8)
                cF[7].inc(gpsimd.collective_compute(
                    "ReduceScatter", ALU.add, replica_groups=groups,
                    ins=[f_part[7].opt()], outs=[f_ownr[7].opt()]))
                phase_end_q(gpsimd)

            @blk.tensor
            def _(tensor):
                phase_start(tensor)
                last = None
                for p in range(NP5):
                    tensor.wait_ge(sLXF.h, 16 * 2 * (p + 1))
                    gidx = 22 * p
                    for j in range(6):
                        gc = 6 * p + j
                        nit = IQ6[j]
                        i0 = IS6[j]
                        bwait(tensor, sLWGp, gc, 16 * 2)
                        for hf in range(2):
                            for itl in range(nit):
                                if gidx >= 2:
                                    tensor.wait_ge(vH.h, gidx - 1)
                                for g in range(32):
                                    tensor.matmul(
                                        out=ps_g[:, 512 * (gidx % 2):
                                                 512 * (gidx % 2 + 1)],
                                        lhsT=gu_sl(gc % 2, 0, g,
                                                   128 * itl, 128, nit),
                                        rhs=xn_f[:, 32 * 512 * hf + 512 * g:
                                                 32 * 512 * hf + 512 * (g + 1)],
                                        start=(g == 0), stop=(g == 31),
                                        skip_group_check=True)
                                    last = tensor.matmul(
                                        out=ps_u[:, 512 * (gidx % 2):
                                                 512 * (gidx % 2 + 1)],
                                        lhsT=gu_sl(gc % 2, 1, g,
                                                   128 * itl, 128, nit),
                                        rhs=xn_f[:, 32 * 512 * hf + 512 * g:
                                                 32 * 512 * hf + 512 * (g + 1)],
                                        start=(g == 0), stop=(g == 31),
                                        skip_group_check=True)
                                tGU.inc(last)
                                gidx += 1
                    tensor.wait_ge(vH.h, 22 * (p + 1))
                    for q8 in range(8):
                        gq8 = 8 * p + q8
                        tensor.wait_ge(sLWDp[gq8 % 3].h, 16 * (gq8 // 3 + 1))
                        for tb2 in range(8):
                            ftb = 64 * p + 8 * q8 + tb2
                            if ftb >= 2:
                                tensor.wait_ge(vFE.h, ftb - 1)
                            for it in range(IC):
                                last = tensor.matmul(
                                    out=ps_f[:, 512 * (ftb % 2):
                                             512 * (ftb % 2 + 1)],
                                    lhsT=h_sb[:, 1024 * it + 128 * tb2:
                                              1024 * it + 128 * (tb2 + 1)],
                                    rhs=wd_sb[:, IC * 512 * (gq8 % 3)
                                              + 512 * it:
                                              IC * 512 * (gq8 % 3)
                                              + 512 * (it + 1)],
                                    start=(it == 0), stop=(it == IC - 1),
                                    skip_group_check=True)
                            tDN.inc(last)
                phase_end_c(tensor, tDN)

            @blk.scalar
            def _(scalar):
                phase_start(scalar)
                scalar.preload_act_table(AF.Silu)

                def wu_half(gc):
                    if gc >= 6 * NP5:
                        return
                    j = gc % 6
                    nit = IQ6[j]
                    i0 = IS6[j]
                    if gc >= 2:
                        scalar.wait_ge(tGU.h, p5_cons(gc - 2))
                    sLWGp[gc % 2].inc(scalar.dma_start(
                        out=gu_sl(gc % 2, 1, 0, 0, 32 * 128 * nit, nit),
                        in_=bass.AP(wu_dq.tensor, 128 * i0,
                                    [[1408, 128], [128 * 1408, 32],
                                     [1, 128 * nit]])), 16)

                def wd_load(gq8):
                    if gq8 >= 16 * NP5 // 2:
                        return
                    if gq8 >= 3:
                        scalar.wait_ge(tDN.h, 8 * (gq8 - 2))
                    sLWDp[gq8 % 3].inc(scalar.dma_start(
                        out=wd_sb[:, IC * 512 * (gq8 % 3):
                                  IC * 512 * (gq8 % 3 + 1)],
                        in_=bass.AP(wd_dq.tensor, 512 * (gq8 % 8),
                                    [[DM, 128], [128 * DM, IC],
                                     [1, 512]])), 16)

                wu_half(0)
                wu_half(1)
                last = None
                for p in range(NP5):
                    for w_, (j, hf, it) in enumerate(P5_SEQ):
                        gidx = 22 * p + w_
                        scalar.wait_ge(tGU.h, gidx + 1)
                        if gidx >= 2:
                            scalar.wait_ge(vH.h, gidx - 1)
                        scalar.activation(
                            out=sg_sb[:, 512 * (gidx % 2):
                                      512 * (gidx % 2 + 1)],
                            in_=ps_g[:, 512 * (gidx % 2):
                                     512 * (gidx % 2 + 1)],
                            func=AF.Silu)
                        last = scalar.activation(
                            out=su_sb[:, 512 * (gidx % 2):
                                      512 * (gidx % 2 + 1)],
                            in_=ps_u[:, 512 * (gidx % 2):
                                     512 * (gidx % 2 + 1)],
                            func=AF.Copy)
                        aGU.inc(last)
                        if w_ == 0:
                            wd_load(8 * p)
                            wd_load(8 * p + 1)
                        elif w_ == 3:
                            wu_half(6 * p + 2)
                        elif w_ == 7:
                            wu_half(6 * p + 3)
                        elif w_ == 11:
                            wu_half(6 * p + 4)
                        elif w_ == 15:
                            wu_half(6 * p + 5)
                        elif w_ == 19:
                            wu_half(6 * p + 6)
                        elif w_ == 21:
                            wu_half(6 * p + 7)
                            for q8 in range(2, 8):
                                wd_load(8 * p + q8)
                phase_end_c(scalar, aGU)

            @blk.vector
            def _(vector):
                phase_start(vector)
                last = None
                for p in range(NP5):
                    for w_, (j, hf, it) in enumerate(P5_SEQ):
                        gidx = 22 * p + w_
                        vector.wait_ge(aGU.h, gidx + 1)
                        vH.inc(vector.tensor_mul(
                            out=h_sb[:, 1024 * it + 512 * hf:
                                     1024 * it + 512 * (hf + 1)],
                            in0=sg_sb[:, 512 * (gidx % 2):
                                      512 * (gidx % 2 + 1)],
                            in1=su_sb[:, 512 * (gidx % 2):
                                      512 * (gidx % 2 + 1)]))
                    for ftl in range(64):
                        ftb = 64 * p + ftl
                        vector.wait_ge(tDN.h, ftb + 1)
                        if ftb >= 2:
                            bwait(vector, sWFPp, ftb - 2, 16)
                        last = vector.tensor_copy(
                            out=f_stage[:, 512 * (ftb % 2):
                                        512 * (ftb % 2 + 1)],
                            in_=ps_f[:, 512 * (ftb % 2):
                                     512 * (ftb % 2 + 1)])
                        vFE.inc(last)
                phase_end_c(vector, vFE)

        PH[0] += 1
        assert sPH.n == 5 * PH[0], sPH.n

        with ExitStack() as p6:
            s_t6 = sbt(p6, "s_t6", [128, 2 * DM], F32, side="left")
            f_t6 = sbt(p6, "f_t6", [128, 2 * DM], F32, side="left")
            x_t6 = sbt(p6, "x_t6", [128, 2 * DM], F32, side="left")
            g_t6 = sbt(p6, "g_t6", [128, 2 * DM], F32, side="left")
            sg_t6 = sbt(p6, "sg_t6", [128, 2 * DM], F32, side="left")
            out_t6 = sbt(p6, "out_t6", [128, 2 * DM], F32, side="left")

            @blk.sync
            def _(sync):
                phase_start(sync)
                def st6(rc):
                    b = DM * (rc % 2)
                    sync.wait_ge(vOUT.h, rc + 1)
                    sWOUTp[rc % 2].inc(sync.dma_start(
                        out=out_own[128 * rc:128 * (rc + 1), :],
                        in_=out_t6[:, b:b + DM]), 16)

                for rc in range(8):
                    b = DM * (rc % 2)
                    sync.wait_ge(cS[rc].h, 1)
                    sync.wait_ge(cF[rc].h, 1)
                    if rc >= 2:
                        sync.wait_ge(vOUT.h, rc - 1)
                    sLSFp[rc % 2].inc(sync.dma_start(out=s_t6[:, b:b + DM],
                                                     in_=s_ownr[rc]), 16)
                    sLSFp[rc % 2].inc(sync.dma_start(out=f_t6[:, b:b + DM],
                                                     in_=f_ownr[rc]), 16)
                    sLSFp[rc % 2].inc(sync.dma_start(
                        out=x_t6[:, b:b + DM],
                        in_=x_sc[128 * rc:128 * (rc + 1), :]), 16)
                    if rc >= 2:
                        st6(rc - 2)
                st6(6)
                st6(7)
                for name, dst in dbg_outs.items():
                    sWOUTp[0].inc(sync.dma_start(out=dst, in_=dbg_srcs[name]), 16)
                sync.wait_ge(sWOUTp[0].h, sWOUTp[0].n)
                sync.wait_ge(sWOUTp[1].h, sWOUTp[1].n)

            @blk.gpsimd
            def _(gpsimd):
                phase_start(gpsimd)

            @blk.tensor
            def _(tensor):
                phase_start(tensor)

            @blk.scalar
            def _(scalar):
                phase_start(scalar)
                for rc in range(8):
                    b = DM * (rc % 2)
                    scalar.wait_ge(sLSFp[rc % 2].h, 48 * (rc // 2 + 1))
                    if rc >= 2:
                        scalar.wait_ge(vOUT.h, rc - 1)
                    aG6.inc(scalar.activation(out=g_t6[:, b:b + DM],
                                              in_=f_t6[:, b:b + DM],
                                              func=AF.Silu))

            @blk.vector
            def _(vector):
                phase_start(vector)
                for rc in range(8):
                    b = DM * (rc % 2)
                    vector.wait_ge(aG6.h, rc + 1)
                    vSELF.inc(vector.tensor_mul(out=sg_t6[:, b:b + DM],
                                                in0=s_t6[:, b:b + DM],
                                                in1=g_t6[:, b:b + DM]))
                    vector.wait_ge(vSELF.h, vSELF.n)
                    if rc >= 2:
                        vector.wait_ge(sWOUTp[rc % 2].h, 16 * (rc // 2))
                    vOUT.inc(vector.tensor_add(out=out_t6[:, b:b + DM],
                                               in0=sg_t6[:, b:b + DM],
                                               in1=x_t6[:, b:b + DM]))

    return nc


def _alibi_slopes(n):
    cp2 = 2 ** math.floor(math.log2(n))
    base = 2.0 ** (-8.0 / cp2)
    slopes = base ** np.arange(1, cp2 + 1, dtype=np.float32)
    if cp2 != n:
        extra_base = 2.0 ** (-4.0 / cp2)
        extra = extra_base ** np.arange(1, 1 + 2 * (n - cp2), 2, dtype=np.float32)
        slopes = np.concatenate([slopes, extra])
    return slopes.astype(np.float32)


def make_in_maps(x_full, w_qkv, w_o, gqw, gsc, gqz, uqw, usc, uqz, dqw, dsc, dqz):
    x = np.ascontiguousarray(np.asarray(x_full, dtype=np.float32).reshape(S, DM))
    sl = _alibi_slopes(32)
    x4 = x.reshape(8, 8, 128, DM)
    in_maps = []
    for c in range(NCORE):
        g0, gn = G0S[c], GROUPS[c]
        m = {}
        m["x_own"] = np.ascontiguousarray(x[1024 * c:1024 * (c + 1)])
        m["x_sc"] = np.ascontiguousarray(x4[:, c].reshape(1024, DM))
        m["wqkv"] = np.ascontiguousarray(np.concatenate(
            [w_qkv[:, 512 * c:512 * (c + 1)],
             w_qkv[:, 4096 + 128 * c:4096 + 128 * (c + 1)],
             w_qkv[:, 5120 + 128 * c:5120 + 128 * (c + 1)]], axis=1))
        m["wo"] = np.ascontiguousarray(w_o[512 * c:512 * (c + 1)])
        m["slopes"] = np.ascontiguousarray(sl[4 * c:4 * (c + 1)].reshape(1, 4))
        gw = np.zeros((DM, 176), np.int32)
        gw[:, :16 * gn] = gqw[:, 16 * g0:16 * (g0 + gn)]
        m["g_qw"] = gw
        gz = np.zeros((32, 176), np.int32)
        gz[:, :16 * gn] = gqz[:, 16 * g0:16 * (g0 + gn)]
        m["g_qz"] = gz
        gs = np.zeros((32, 1408), np.float32)
        gs[:, :128 * gn] = gsc[:, 128 * g0:128 * (g0 + gn)]
        m["g_sc"] = gs
        uw = np.zeros((DM, 176), np.int32)
        uw[:, :16 * gn] = uqw[:, 16 * g0:16 * (g0 + gn)]
        m["u_qw"] = uw
        uz = np.zeros((32, 176), np.int32)
        uz[:, :16 * gn] = uqz[:, 16 * g0:16 * (g0 + gn)]
        m["u_qz"] = uz
        us = np.zeros((32, 1408), np.float32)
        us[:, :128 * gn] = usc[:, 128 * g0:128 * (g0 + gn)]
        m["u_sc"] = us
        dw = np.zeros((1408, 512), np.int32)
        dw[:128 * gn] = dqw[128 * g0:128 * (g0 + gn)]
        m["d_qw"] = dw
        dz = np.zeros((IC, 512), np.int32)
        dz[:gn] = dqz[g0:g0 + gn]
        m["d_qz"] = dz
        ds = np.zeros((IC, DM), np.float32)
        ds[:gn] = dsc[g0:g0 + gn]
        m["d_sc"] = ds
        in_maps.append(m)
    return in_maps


def kernel(L0_attn_hidden_states, L0_attn_w_qkv, L0_attn_w_o,
           L0_ffn_gate_qweight, L0_ffn_gate_scales, L0_ffn_gate_qzeros,
           L0_ffn_up_qweight, L0_ffn_up_scales, L0_ffn_up_qzeros,
           L0_ffn_down_qweight, L0_ffn_down_scales, L0_ffn_down_qzeros):
    in_maps = make_in_maps(
        np.asarray(L0_attn_hidden_states, dtype=np.float32),
        np.asarray(L0_attn_w_qkv, dtype=np.float32),
        np.asarray(L0_attn_w_o, dtype=np.float32),
        np.asarray(L0_ffn_gate_qweight, dtype=np.int32),
        np.asarray(L0_ffn_gate_scales, dtype=np.float32),
        np.asarray(L0_ffn_gate_qzeros, dtype=np.int32),
        np.asarray(L0_ffn_up_qweight, dtype=np.int32),
        np.asarray(L0_ffn_up_scales, dtype=np.float32),
        np.asarray(L0_ffn_up_qzeros, dtype=np.int32),
        np.asarray(L0_ffn_down_qweight, dtype=np.int32),
        np.asarray(L0_ffn_down_scales, dtype=np.float32),
        np.asarray(L0_ffn_down_qzeros, dtype=np.int32))

    nc = build_program()
    res = run_bass_kernel_spmd(nc, in_maps, core_ids=list(range(NCORE)))

    out = np.empty((8, 8, 128, DM), dtype=np.float32)
    for c in range(NCORE):
        out[:, c] = res.results[c]["out_own"].reshape(8, 128, DM)
    return out.reshape(1, S, DM)


# version 17
# speedup vs baseline: 1.0533x; 1.0136x over previous
import math
import numpy as np
from contextlib import ExitStack

import concourse.bass as bass
import concourse.mybir as mybir
from concourse.bass_utils import run_bass_kernel_spmd

F32 = mybir.dt.float32
BF16 = mybir.dt.bfloat16
I32 = mybir.dt.int32
AF = mybir.ActivationFunctionType
ALU = mybir.AluOpType
AX = mybir.AxisListType

NCORE = 8
S = 8192
DM = 4096
TOWN = 1024
NQT = 16
SCALE = 1.0 / math.sqrt(128.0)
CSHIFT = 12.0
IC = 11
GROUPS = [11, 11, 11, 11, 11, 11, 10, 10]
G0S = [0, 11, 22, 33, 44, 55, 66, 76]
EW = 8576
EL = 8704
NSB = 16
IQ_CHUNKS = [3, 3, 3, 2]
IQ_STARTS = [0, 3, 6, 9]
BIGNEG = -6.6e8


class Sem:
    __slots__ = ("h", "n")

    def __init__(self, ctx, nc, name):
        self.h = ctx.enter_context(nc.semaphore(name))
        self.n = 0

    def inc(self, ins, k=1):
        ins.then_inc(self.h, k)
        self.n += k
        return ins


def build_program(dbg=(), upto=6):
    nc = bass.Bass("TRN2", target_bir_lowering=False)

    x_own = nc.dram_tensor("x_own", [TOWN, DM], F32, kind="ExternalInput").ap()
    x_sc = nc.dram_tensor("x_sc", [TOWN, DM], F32, kind="ExternalInput").ap()
    wqkv = nc.dram_tensor("wqkv", [DM, 768], F32, kind="ExternalInput").ap()
    wo = nc.dram_tensor("wo", [512, DM], F32, kind="ExternalInput").ap()
    slopes = nc.dram_tensor("slopes", [1, 4], F32, kind="ExternalInput").ap()
    g_qw = nc.dram_tensor("g_qw", [DM, 176], I32, kind="ExternalInput").ap()
    g_qz = nc.dram_tensor("g_qz", [32, 176], I32, kind="ExternalInput").ap()
    g_sc = nc.dram_tensor("g_sc", [32, 1408], F32, kind="ExternalInput").ap()
    u_qw = nc.dram_tensor("u_qw", [DM, 176], I32, kind="ExternalInput").ap()
    u_qz = nc.dram_tensor("u_qz", [32, 176], I32, kind="ExternalInput").ap()
    u_sc = nc.dram_tensor("u_sc", [32, 1408], F32, kind="ExternalInput").ap()
    d_qw = nc.dram_tensor("d_qw", [1408, 512], I32, kind="ExternalInput").ap()
    d_qz = nc.dram_tensor("d_qz", [IC, 512], I32, kind="ExternalInput").ap()
    d_sc = nc.dram_tensor("d_sc", [IC, DM], F32, kind="ExternalInput").ap()

    out_own = nc.dram_tensor("out_own", [TOWN, DM], F32, kind="ExternalOutput").ap()

    cc_xn = [nc.dram_tensor(f"cc_xn{i}", [DM, 512], BF16, kind="Internal").ap()
             for i in range(2)]
    xn_g = [nc.dram_tensor(f"xn_g{i}", [NCORE * DM, 512], BF16,
                           kind="Internal").ap() for i in range(2)]
    q_scr = nc.dram_tensor("q_scr", [512, S], BF16, kind="Internal").ap()
    o_scr = nc.dram_tensor("o_scr", [512, S], BF16, kind="Internal").ap()
    wo_bf = nc.dram_tensor("wo_bf", [512, DM], BF16, kind="Internal").ap()
    e_lin = nc.dram_tensor("e_lin", [4, EL], BF16, kind="Internal").ap()
    zf_g = nc.dram_tensor("zf_g", [32, 1408], I32, kind="Internal").ap()
    zf_u = nc.dram_tensor("zf_u", [32, 1408], I32, kind="Internal").ap()
    zf_d = nc.dram_tensor("zf_d", [IC, DM], I32, kind="Internal").ap()
    wg_dq = nc.dram_tensor("wg_dq", [32, 128, 1408], BF16, kind="Internal").ap()
    wu_dq = nc.dram_tensor("wu_dq", [32, 128, 1408], BF16, kind="Internal").ap()
    wd_dq = nc.dram_tensor("wd_dq", [IC, 128, DM], BF16, kind="Internal").ap()
    s_part = [[nc.dram_tensor(f"s_part{r}_{q}", [1024, 2048], F32,
                              kind="Internal").ap() for q in range(2)]
              for r in range(8)]
    f_part = [[nc.dram_tensor(f"f_part{r}_{q}", [1024, 2048], F32,
                              kind="Internal").ap() for q in range(2)]
              for r in range(8)]
    s_ownr = [[nc.dram_tensor(f"s_ownr{r}_{q}", [128, 2048], F32,
                              kind="Internal").ap() for q in range(2)]
              for r in range(8)]
    f_ownr = [[nc.dram_tensor(f"f_ownr{r}_{q}", [128, 2048], F32,
                              kind="Internal").ap() for q in range(2)]
              for r in range(8)]

    dbg_outs = {}
    dbg_srcs = {"xn_g0": xn_g[0], "xn_g1": xn_g[1], "q_scr": q_scr, "o_scr": o_scr,
                "e_lin": e_lin, "wg_dq": wg_dq, "wd_dq": wd_dq,
                "s_part0": s_part[0][0], "f_part0": f_part[0][0],
                "s_ownr0": s_ownr[0][0], "f_ownr0": f_ownr[0][0]}
    for name in dbg:
        src = dbg_srcs[name]
        dbg_outs[name] = nc.dram_tensor(
            "dbg_" + name, list(src.shape), src.dtype, kind="ExternalOutput").ap()

    groups = [list(range(NCORE))]

    NKB = [4 * (qt + 1) for qt in range(NQT)]
    KB_CUM = [0]
    PB_CUM = [0]
    for h in range(4):
        for qt in range(NQT):
            KB_CUM.append(KB_CUM[-1] + NKB[qt])
            PB_CUM.append(PB_CUM[-1] + NKB[qt] // 2)

    with ExitStack() as top:
        def sem(name):
            return Sem(top, nc, name)

        sPH = sem("sPH")
        tSELF = sem("tSELF"); aSELF = sem("aSELF"); vSELF = sem("vSELF")
        gSELF = sem("gSELF")
        sLX = sem("sLX"); sLX2 = sLX
        aSQ = sem("aSQ"); vST = sem("vST"); aST = sem("aST"); aXN = sem("aXN")
        tTR = sem("tTR"); vTS = sem("vTS")
        sWXNp = [sem("sWXNa"), sem("sWXNb")]
        sLQW = sem("sLQW")
        sLDQz = sem("sLDQz"); vDQz = sem("vDQz")
        sLDQp = [sem("sLDQa"), sem("sLDQb")]
        sWDQp = [sem("sWDQa"), sem("sWDQb")]
        vDQs = sem("vDQs"); vDQm = sem("vDQm"); tDQ = sem("tDQ")
        sSF = sem("sSF"); vDQb = sem("vDQb"); sZSp = [sem("sZSa"), sem("sZSb")]
        gEB = sem("gEB"); vEB = sem("vEB"); aEB = sem("aEB"); sEB = sem("sEB")
        cAGs = [sem("cAG0"), sem("cAG1")]
        sLXN4 = [sem(f"sLXN{i}") for i in range(4)]
        tQK = sem("tQK"); vEVQ = sem("vEVQ")
        tTRV = sem("tTRV"); vEVV = sem("vEVV"); sWQS = sem("sWQS")
        sEB2 = sem("sEB2"); sEB3 = sem("sEB3")
        sLQp = [sem("sLQa"), sem("sLQb")]
        tSC = sem("tSC"); aEX = sem("aEX"); vMK = sem("vMK")
        tPV = sem("tPV"); aOE = sem("aOE"); vRC = sem("vRC"); tLB = sem("tLB")
        vON = sem("vON"); sWOSp = [sem("sWOSa"), sem("sWOSb")]
        sLWOp = [sem("sLWOa"), sem("sLWOb")]
        sLORp = [sem("sLORa"), sem("sLORb")]
        tS4 = sem("tS4"); aSE = sem("aSE")
        sWSPp = [sem(f"sWSP{i}") for i in range(3)]
        cS = [sem(f"cS{r}") for r in range(8)]
        sLXF = sem("sLXF")
        sLWGp = [sem("sLWGa"), sem("sLWGb")]
        sLWDp = [sem(f"sLWD{i}") for i in range(3)]
        tGU = sem("tGU"); aGU = sem("aGU"); vH = sem("vH")
        tDN = sem("tDN"); vFE = sem("vFE")
        sWFPp = [sem("sWFPa"), sem("sWFPb")]
        cF = [sem(f"cF{r}") for r in range(8)]
        sLSFp = [sem("sLSFa"), sem("sLSFb")]
        aG6 = sem("aG6"); vOUT = sem("vOUT")
        sWOUTp = [sem("sWOUTa"), sem("sWOUTb")]

        def sbt(ctx, name, shape, dtype, side="left"):
            return ctx.enter_context(nc.sbuf_tensor(name, shape, dtype, side=side))

        def pst(ctx, name, shape, dtype):
            return ctx.enter_context(nc.psum_tensor(name, shape, dtype))

        ones_rowf = sbt(top, "ones_rowf", [1, 128], F32)
        ones_colb = sbt(top, "ones_colb", [128, 1], BF16)
        bias_c = sbt(top, "bias_c", [128, 1], F32)
        eps_b = sbt(top, "eps_b", [128, 1], F32)
        ident = sbt(top, "ident", [128, 128], BF16)
        slopes_ng = sbt(top, "slopes_ng", [128, 4], F32)
        slopes_rep = sbt(top, "slopes_rep", [128, 4], F32)

        PH = [0]

        def phase_start(eng):
            if PH[0] > 0:
                eng.wait_ge(sPH.h, 5 * PH[0])

        def phase_end_q(eng):
            sPH.n += 1
            eng.nop().then_inc(sPH.h, 1)

        def phase_end_c(eng, last_sem):
            eng.wait_ge(last_sem.h, last_sem.n)
            sPH.n += 1
            eng.nop().then_inc(sPH.h, 1)

        def bwait(eng, sems, g, U):
            eng.wait_ge(sems[g % 2].h, U * (g // 2 + 1))

        blk = top.enter_context(nc.Block())

        with ExitStack() as scWQ:
            wqkv_sb = sbt(scWQ, "wqkv_sb", [128, 32 * 768], BF16, side="left")

            with ExitStack() as p0:
                x_t = sbt(p0, "x_t", [128, DM], F32, side="right")
                sq_t = sbt(p0, "sq_t", [128, DM], F32, side="right")
                msum = sbt(p0, "msum", [128, 8], F32, side="right")
                qsum = sbt(p0, "qsum", [128, 8], F32, side="right")
                mean8 = sbt(p0, "mean8", [128, 8], F32, side="right")
                eq8 = sbt(p0, "eq8", [128, 8], F32, side="right")
                m28 = sbt(p0, "m28", [128, 8], F32, side="right")
                var8 = sbt(p0, "var8", [128, 8], F32, side="right")
                std8 = sbt(p0, "std8", [128, 8], F32, side="right")
                rinv8 = sbt(p0, "rinv8", [128, 8], F32, side="right")
                bneg8 = sbt(p0, "bneg8", [128, 8], F32, side="right")
                tmp8 = sbt(p0, "tmp8", [128, 8], F32, side="right")
                xn_tm = sbt(p0, "xn_tm", [128, 2 * DM], BF16, side="right")
                xn_stage = sbt(p0, "xn_stage", [128, 2 * 512], BF16, side="right")
                qw_t = sbt(p0, "qw_t", [128, 2 * 512], I32, side="right")
                nib_t = sbt(p0, "nib_t", [128, 2 * 2048], I32, side="right")
                d1_t = sbt(p0, "d1_t", [128, 2048], F32, side="right")
                w_t = sbt(p0, "w_t", [128, 2 * 2048], BF16, side="right")
                gz_sb = sbt(p0, "gz_sb", [32, 176], I32, side="right")
                uz_sb = sbt(p0, "uz_sb", [32, 176], I32, side="right")
                dz_sb = sbt(p0, "dz_sb", [IC, 512], I32, side="right")
                zfg_sb = sbt(p0, "zfg_sb", [32, 1408], I32, side="right")
                zfu_sb = sbt(p0, "zfu_sb", [32, 1408], I32, side="right")
                zfd_sb = sbt(p0, "zfd_sb", [IC, DM], I32, side="right")
                zfg_bf = sbt(p0, "zfg_bf", [32, 1408], BF16, side="right")
                zfu_bf = sbt(p0, "zfu_bf", [32, 1408], BF16, side="right")
                zfd_bf = sbt(p0, "zfd_bf", [IC, DM], BF16, side="right")
                sfg_bf = sbt(p0, "sfg_bf", [32, 1408], BF16, side="right")
                sfu_bf = sbt(p0, "sfu_bf", [32, 1408], BF16, side="right")
                sfd_bf = sbt(p0, "sfd_bf", [IC, DM], BF16, side="right")
                ones_rowb = sbt(p0, "ones_rowb", [1, 128], BF16, side="right")
                z_stg = sbt(p0, "z_stg", [1, 2 * 1536], BF16, side="right")
                s_stg = sbt(p0, "s_stg", [1, 2 * 1536], BF16, side="right")
                ramp = sbt(p0, "ramp", [128, 68], F32, side="right")
                rmax = sbt(p0, "rmax", [128, 68], F32, side="right")
                rneg = sbt(p0, "rneg", [128, 68], F32, side="right")
                rmix = sbt(p0, "rmix", [128, 68], F32, side="right")
                sq_row = sbt(p0, "sq_row", [128, 68], F32, side="right")
                e_rowf = sbt(p0, "e_rowf", [128, 68], F32, side="right")
                e_rowb = sbt(p0, "e_rowb", [128, 68], BF16, side="right")
                ps_tr = pst(p0, "ps_tr", [128, 2 * 1024], BF16)
                ps_zb = pst(p0, "ps_zb", [128, 1536], F32)
                ps_sb5 = pst(p0, "ps_sb5", [128, 1536], F32)

                dq_jobs = []
                for g in range(32):
                    dq_jobs.append(("g", g, 176, 1408, 0))
                for g in range(32):
                    dq_jobs.append(("u", g, 176, 1408, 0))
                for it in range(IC):
                    for oh in range(4):
                        dq_jobs.append(("d", it, 128, 1024, oh))

                @blk.gpsimd
                def _(gpsimd):
                    for g in range(32):
                        sLQW.inc(gpsimd.dma_start(
                            out=wqkv_sb[:, 768 * g:768 * (g + 1)],
                            in_=wqkv[128 * g:128 * (g + 1), :]), 16)
                    sLQW.inc(gpsimd.dma_start(out=wo_bf, in_=wo), 16)
                    sSF.inc(gpsimd.dma_start(out=sfg_bf[:, :], in_=g_sc), 16)
                    sSF.inc(gpsimd.dma_start(out=sfu_bf[:, :], in_=u_sc), 16)
                    sSF.inc(gpsimd.dma_start(out=sfd_bf[:, :], in_=d_sc), 16)
                    gpsimd.memset(ident[:, :], 1.0)
                    gSELF.inc(gpsimd.iota(
                        ramp[:, :], [[1, 68]], base=-512, channel_multiplier=68,
                        allow_small_or_imprecise_dtypes=True))
                    gpsimd.wait_ge(gSELF.h, gSELF.n)
                    gEB.inc(gpsimd.affine_select(
                        out=ident[:, :], in_=ident[:, :],
                        compare_op=ALU.is_equal, fill=0.0, base=0,
                        pattern=[[-1, 128]], channel_multiplier=1))
                    for tb in range(8):
                        for fg in range(8):
                            idx = tb * 8 + fg
                            gpsimd.wait_ge(vTS.h, idx + 1)
                            for j in range(4):
                                fb = 4 * fg + j
                                sWXNp[idx % 2].inc(gpsimd.dma_start(
                                    out=cc_xn[tb // 4][
                                        128 * fb:128 * (fb + 1),
                                        128 * (tb % 4):128 * (tb % 4 + 1)],
                                    in_=xn_stage[:, 512 * (idx % 2) + 128 * j:
                                                 512 * (idx % 2) + 128 * (j + 1)]),
                                    16)
                    gpsimd.wait_ge(sWXNp[0].h, 64 * 16)
                    gpsimd.wait_ge(sWXNp[1].h, 64 * 16)
                    cAGs[0].inc(gpsimd.collective_compute(
                        "AllGather", ALU.bypass, replica_groups=groups,
                        ins=[cc_xn[0].opt()], outs=[xn_g[0].opt()]))
                    gpsimd.wait_ge(sWXNp[0].h, 64 * 32)
                    gpsimd.wait_ge(sWXNp[1].h, 64 * 32)
                    cAGs[1].inc(gpsimd.collective_compute(
                        "AllGather", ALU.bypass, replica_groups=groups,
                        ins=[cc_xn[1].opt()], outs=[xn_g[1].opt()]))
                    gpsimd.wait_ge(sLQW.h, sLQW.n)
                    phase_end_q(gpsimd)

                @blk.sync
                def _(sync):
                    sLDQz.inc(sync.dma_start(out=slopes_rep[:, :],
                                             in_=slopes.broadcast_to([128, 4])), 16)
                    for tb in range(8):
                        if tb >= 1:
                            sync.wait_ge(vST.h, tb)
                            sync.wait_ge(aSQ.h, tb)
                        sLX.inc(sync.dma_start(
                            out=x_t[:, :],
                            in_=x_own[128 * tb:128 * (tb + 1), :]), 16)
                    for tb in range(8):
                        sync.wait_ge(vST.h, 8)
                        sync.wait_ge(aXN.h, tb)
                        sLX2.inc(sync.dma_start(
                            out=x_t[:, :],
                            in_=x_own[128 * tb:128 * (tb + 1), :]), 16)
                    sLDQz.inc(sync.dma_start(out=gz_sb[:, :], in_=g_qz), 16)
                    sLDQz.inc(sync.dma_start(out=uz_sb[:, :], in_=u_qz), 16)
                    sLDQz.inc(sync.dma_start(out=dz_sb[:, :], in_=d_qz), 16)

                    sync.wait_ge(vDQz.h, 3)
                    sync.wait_ge(sSF.h, 48)
                    def dq_store(k):
                        tag_, i0_, nw_, wout_, oh_ = dq_jobs[k]
                        b_ = k % 2
                        sync.wait_ge(vDQm.h, k + 1)
                        if tag_ == "g":
                            dst = bass.AP(wg_dq.tensor, i0_ * 128 * 1408,
                                          [[1408, 128], [1, 1408]])
                        elif tag_ == "u":
                            dst = bass.AP(wu_dq.tensor, i0_ * 128 * 1408,
                                          [[1408, 128], [1, 1408]])
                        else:
                            dst = bass.AP(wd_dq.tensor,
                                          i0_ * 128 * DM + 1024 * oh_,
                                          [[DM, 128], [1, 1024]])
                        sWDQp[b_].inc(sync.dma_start(
                            out=dst,
                            in_=w_t[:, 2048 * b_:2048 * b_ + wout_]), 16)

                    for k, (tag, i0, nw, wout, oh) in enumerate(dq_jobs):
                        if k >= 2:
                            sync.wait_ge(vDQs.h, k - 1)
                        b = k % 2
                        if tag == "g":
                            qsrc = g_qw[128 * i0:128 * (i0 + 1), :]
                            zrow, srow, off = zfg_bf, sfg_bf, 0
                        elif tag == "u":
                            qsrc = u_qw[128 * i0:128 * (i0 + 1), :]
                            zrow, srow, off = zfu_bf, sfu_bf, 0
                        else:
                            qsrc = d_qw[128 * i0:128 * (i0 + 1),
                                        128 * oh:128 * (oh + 1)]
                            zrow, srow, off = zfd_bf, sfd_bf, 1024 * oh
                        if k >= 2:
                            sync.wait_ge(tDQ.h, k - 1)
                        sZSp[b].inc(sync.dma_start(
                            out=z_stg[0:1, 1536 * b:1536 * b + wout],
                            in_=zrow[i0:i0 + 1, off:off + wout]), 16)
                        sZSp[b].inc(sync.dma_start(
                            out=s_stg[0:1, 1536 * b:1536 * b + wout],
                            in_=srow[i0:i0 + 1, off:off + wout]), 16)
                        sLDQp[b].inc(sync.dma_start(
                            out=qw_t[:, 512 * b:512 * b + nw], in_=qsrc), 16)
                        if k >= 1:
                            dq_store(k - 1)
                    dq_store(len(dq_jobs) - 1)
                    for h in range(4):
                        sync.wait_ge(vEB.h, 2 + h + 1)
                        sEB.inc(sync.dma_start(
                            out=bass.AP(e_lin.tensor, h * EL,
                                        [[68, 128], [1, 68]]),
                            in_=e_rowb[:, :]), 16)
                    for s_ in (sLX, sLDQz, sLDQp[0], sLDQp[1], sZSp[0],
                               sZSp[1], sWDQp[0], sWDQp[1], sEB):
                        sync.wait_ge(s_.h, s_.n)
                    phase_end_q(sync)

                @blk.tensor
                def _(tensor):
                    tensor.wait_ge(gEB.h, 1)
                    last = None
                    for tb in range(8):
                        tensor.wait_ge(aXN.h, tb + 1)
                        for fg in range(8):
                            idx = tb * 8 + fg
                            if idx >= 2:
                                tensor.wait_ge(vTS.h, idx - 1)
                            for j in range(4):
                                fb = 4 * fg + j
                                last = tensor.transpose(
                                    out=ps_tr[:, 1024 * (idx % 2) + 128 * j:
                                              1024 * (idx % 2) + 128 * (j + 1)],
                                    in_=xn_tm[:, DM * (tb % 2) + 128 * fb:
                                              DM * (tb % 2) + 128 * (fb + 1)],
                                    identity=ident[:, :])
                            tTR.inc(last)
                    for k, (tag, i0, nw, wout, oh) in enumerate(dq_jobs):
                        b = k % 2
                        tensor.wait_ge(sZSp[k % 2].h,
                                       32 * (k // 2 + 1))
                        if k >= 1:
                            tensor.wait_ge(vDQb.h, k)
                            tensor.wait_ge(vDQm.h, k)
                        last = None
                        for c0 in range(0, wout, 512):
                            w = min(512, wout - c0)
                            tensor.matmul(
                                out=ps_zb[:, c0:c0 + w],
                                lhsT=ones_rowb[:, :],
                                rhs=z_stg[0:1, 1536 * b + c0:1536 * b + c0 + w],
                                start=True, stop=True, skip_group_check=True)
                            last = tensor.matmul(
                                out=ps_sb5[:, c0:c0 + w],
                                lhsT=ones_rowb[:, :],
                                rhs=s_stg[0:1, 1536 * b + c0:1536 * b + c0 + w],
                                start=True, stop=True, skip_group_check=True)
                        tDQ.inc(last)
                    phase_end_c(tensor, tDQ)

                @blk.scalar
                def _(scalar):
                    scalar.preload_act_table(AF.Exp)
                    for tb in range(8):
                        scalar.wait_ge(sLX.h, 16 * (tb + 1))
                        if tb >= 1:
                            scalar.wait_ge(vST.h, tb)
                        aSQ.inc(scalar.activation(out=sq_t[:, :], in_=x_t[:, :],
                                                  func=AF.Square))
                    scalar.wait_ge(vSELF.h, 4)
                    scalar.preload_act_table(AF.Sqrt)
                    aST.inc(scalar.activation(out=std8[:, :], in_=var8[:, :],
                                              func=AF.Sqrt, bias=eps_b[:, :]))
                    scalar.preload_act_table(AF.Exp)
                    for tb in range(8):
                        scalar.wait_ge(sLX2.h, 16 * (8 + tb + 1))
                        scalar.wait_ge(vSELF.h, 7)
                        if tb >= 2:
                            scalar.wait_ge(tTR.h, 8 * (tb - 1))
                        aXN.inc(scalar.activation(
                            out=xn_tm[:, DM * (tb % 2):DM * (tb % 2 + 1)],
                            in_=x_t[:, :],
                            func=AF.Identity, scale=rinv8[:, tb:tb + 1],
                            bias=bneg8[:, tb:tb + 1]))
                    scalar.wait_ge(vEB.h, 1)
                    scalar.preload_act_table(AF.Sqrt)
                    aSELF.inc(scalar.activation(out=sq_row[:, :], in_=rmix[:, :],
                                                func=AF.Sqrt))
                    scalar.wait_ge(aSELF.h, aSELF.n)
                    scalar.preload_act_table(AF.Exp)
                    last = None
                    for h in range(4):
                        scalar.wait_ge(vEB.h, 2 + h)
                        last = scalar.activation(
                            out=e_rowf[:, :],
                            in_=sq_row[:, :], func=AF.Exp,
                            scale=slopes_ng[:, h:h + 1])
                        aEB.inc(last)
                    phase_end_c(scalar, aEB)

                @blk.vector
                def _(vector):
                    vector.memset(ones_rowf[:, :], 1.0)
                    vector.memset(ones_colb[:, :], 1.0)
                    vector.memset(bias_c[:, :], -CSHIFT)
                    vector.memset(eps_b[:, :], 1e-5)
                    for tb in range(8):
                        vector.wait_ge(sLX.h, 16 * (tb + 1))
                        vector.wait_ge(aSQ.h, tb + 1)
                        vector.reduce_sum(out=msum[:, tb:tb + 1], in_=x_t[:, :],
                                          axis=AX.X)
                        vST.inc(vector.reduce_sum(out=qsum[:, tb:tb + 1],
                                                  in_=sq_t[:, :], axis=AX.X))
                    vector.wait_ge(vST.h, 8)
                    vSELF.inc(vector.tensor_scalar_mul(
                        out=mean8[:, :], in0=msum[:, :], scalar1=1.0 / DM))
                    vSELF.inc(vector.tensor_scalar_mul(
                        out=eq8[:, :], in0=qsum[:, :], scalar1=1.0 / DM))
                    vector.wait_ge(vSELF.h, 2)
                    vSELF.inc(vector.tensor_mul(out=m28[:, :], in0=mean8[:, :],
                                                in1=mean8[:, :]))
                    vector.wait_ge(vSELF.h, 3)
                    vSELF.inc(vector.tensor_sub(out=var8[:, :], in0=eq8[:, :],
                                                in1=m28[:, :]))
                    vector.wait_ge(aST.h, 1)
                    vSELF.inc(vector.reciprocal(out=rinv8[:, :], in_=std8[:, :]))
                    vector.wait_ge(vSELF.h, 5)
                    vSELF.inc(vector.tensor_mul(out=tmp8[:, :], in0=mean8[:, :],
                                                in1=rinv8[:, :]))
                    vector.wait_ge(vSELF.h, 6)
                    vSELF.inc(vector.tensor_scalar_mul(
                        out=bneg8[:, :], in0=tmp8[:, :], scalar1=-1.0))
                    for tb in range(8):
                        for fg in range(8):
                            idx = tb * 8 + fg
                            vector.wait_ge(tTR.h, idx + 1)
                            if idx >= 2:
                                bwait(vector, sWXNp, idx - 2, 64)
                            vTS.inc(vector.tensor_copy(
                                out=xn_stage[:, 512 * (idx % 2):
                                             512 * (idx % 2 + 1)],
                                in_=ps_tr[:, 1024 * (idx % 2):
                                          1024 * (idx % 2) + 512]))
                    vector.memset(ones_rowb[:, :], 1.0)
                    vector.wait_ge(sLDQz.h, 16 * 4)
                    for zsrc, dstz, dstzb, wtot in (
                            (gz_sb, zfg_sb, zfg_bf, 1408),
                            (uz_sb, zfu_sb, zfu_bf, 1408),
                            (dz_sb, zfd_sb, zfd_bf, DM)):
                        last = None
                        for j in range(8):
                            last = vector.tensor_scalar(
                                out=dstz[:, j:wtot:8], in0=zsrc[:, :],
                                scalar1=4 * j, scalar2=15,
                                op0=ALU.logical_shift_right,
                                op1=ALU.bitwise_and)
                        vSELF.inc(last)
                        vector.wait_ge(vSELF.h, vSELF.n)
                        vDQz.inc(vector.tensor_copy(out=dstzb[:, :],
                                                    in_=dstz[:, :]))
                    def nib_job(k):
                        if k >= len(dq_jobs):
                            return
                        tag_, i0_, nw_, wout_, oh_ = dq_jobs[k]
                        bwait(vector, sLDQp, k, 16)
                        if k >= 2:
                            vector.wait_ge(vDQb.h, k - 1)
                        last_ = None
                        for j in range(8):
                            last_ = vector.tensor_scalar(
                                out=nib_t[:, 2048 * (k % 2) + j:
                                          2048 * (k % 2) + wout_:8],
                                in0=qw_t[:, 512 * (k % 2):512 * (k % 2) + nw_],
                                scalar1=4 * j, scalar2=15,
                                op0=ALU.logical_shift_right,
                                op1=ALU.bitwise_and)
                        vDQs.inc(last_)

                    nib_job(0)
                    nib_job(1)
                    for k, (tag, i0, nw, wout, oh) in enumerate(dq_jobs):
                        b = k % 2
                        vector.wait_ge(vDQs.h, k + 1)
                        vector.wait_ge(tDQ.h, k + 1)
                        vDQb.inc(vector.tensor_sub(
                            out=d1_t[:, 0:wout],
                            in0=nib_t[:, 2048 * b:2048 * b + wout],
                            in1=ps_zb[:, 0:wout]))
                        vector.wait_ge(vDQb.h, vDQb.n)
                        if k >= 2:
                            bwait(vector, sWDQp, k - 2, 16)
                        vDQm.inc(vector.tensor_mul(
                            out=w_t[:, 2048 * b:2048 * b + wout],
                            in0=d1_t[:, 0:wout],
                            in1=ps_sb5[:, 0:wout]))
                        nib_job(k + 2)
                    vector.wait_ge(gEB.h, 1)
                    vector.tensor_scalar_max(out=rmax[:, :], in0=ramp[:, :],
                                             scalar1=0.0)
                    vSELF.inc(vector.tensor_scalar_min(out=rneg[:, :],
                                                       in0=ramp[:, :],
                                                       scalar1=0.0))
                    vector.wait_ge(vSELF.h, vSELF.n)
                    vEB.inc(vector.scalar_tensor_tensor(
                        out=rmix[:, :], in0=rneg[:, :], scalar=BIGNEG,
                        in1=rmax[:, :], op0=ALU.mult, op1=ALU.add))
                    vector.wait_ge(sLDQz.h, 16 * 4)
                    vEB.inc(vector.tensor_scalar_mul(
                        out=slopes_ng[:, :], in0=slopes_rep[:, :],
                        scalar1=-1.0))
                    last = None
                    for h in range(4):
                        vector.wait_ge(aEB.h, h + 1)
                        if h >= 1:
                            vector.wait_ge(sEB.h, 16 * h)
                        last = vector.tensor_copy(
                            out=e_rowb[:, :], in_=e_rowf[:, :])
                        vEB.inc(last)
                    phase_end_c(vector, vEB)

            PH[0] += 1
            assert sPH.n == 5 * PH[0], sPH.n

            with ExitStack() as scKV:
                k_fm = sbt(scKV, "k_fm", [128, S], BF16, side="right")
                v_tm = sbt(scKV, "v_tm", [128, S], BF16, side="right")
                e_sb = sbt(scKV, "e_sb", [128, 4 * EW], BF16, side="right")

                with ExitStack() as p2:
                    xn_t2 = sbt(p2, "xn_t2", [128, 4 * 512], BF16, side="right")
                    q_stage = sbt(p2, "q_stage", [128, 4 * 512], BF16, side="right")
                    v_stage = sbt(p2, "v_stage", [128, 512], BF16, side="right")
                    ps_qkv = pst(p2, "ps_qkv", [128, 6 * 512], F32)
                    ps_vt = pst(p2, "ps_vt", [128, 512], BF16)

                    @blk.sync
                    def _(sync):
                        phase_start(sync)
                        for tt in range(NQT):
                            sync.wait_ge(cAGs[tt % 2].h, 1)
                            for g in range(32):
                                gi = 32 * tt + g
                                if gi % 2 != 0:
                                    continue
                                if gi >= 4:
                                    sync.wait_ge(tQK.h, gi - 3)
                                sLXN4[gi % 4].inc(sync.dma_start(
                                    out=xn_t2[:, 512 * (gi % 4):512 * (gi % 4 + 1)],
                                    in_=xn_g[tt % 2][
                                        DM * (tt // 2) + 128 * g:
                                        DM * (tt // 2) + 128 * (g + 1), :]), 16)
                                r = gi // 2
                                h_, i_ = divmod(r, 128)
                                sEB2.inc(sync.dma_start(
                                    out=e_sb[i_:i_ + 1, EW * h_:EW * (h_ + 1)],
                                    in_=bass.AP(e_lin.tensor, h_ * EL + 128 - i_,
                                                [[1, 1], [1, EW]])), 16)
                            sync.wait_ge(vEVQ.h, tt + 1)
                            for f in range(4):
                                sWQS.inc(sync.dma_start(
                                    out=q_scr[128 * f:128 * (f + 1),
                                              512 * tt:512 * (tt + 1)],
                                    in_=q_stage[:, 512 * f:512 * (f + 1)]), 16)
                        for s_ in (sLXN4[0], sLXN4[2], sWQS, sEB2):
                            sync.wait_ge(s_.h, s_.n)
                        phase_end_q(sync)

                    @blk.gpsimd
                    def _(gpsimd):
                        phase_start(gpsimd)
                        for tt in range(NQT):
                            gpsimd.wait_ge(cAGs[tt % 2].h, 1)
                            for g in range(32):
                                gi = 32 * tt + g
                                if gi % 2 != 1:
                                    continue
                                if gi >= 4:
                                    gpsimd.wait_ge(tQK.h, gi - 3)
                                sLXN4[gi % 4].inc(gpsimd.dma_start(
                                    out=xn_t2[:, 512 * (gi % 4):512 * (gi % 4 + 1)],
                                    in_=xn_g[tt % 2][
                                        DM * (tt // 2) + 128 * g:
                                        DM * (tt // 2) + 128 * (g + 1), :]), 16)
                                r = 256 + gi // 2
                                h_, i_ = divmod(r, 128)
                                sEB3.inc(gpsimd.dma_start(
                                    out=e_sb[i_:i_ + 1, EW * h_:EW * (h_ + 1)],
                                    in_=bass.AP(e_lin.tensor, h_ * EL + 128 - i_,
                                                [[1, 1], [1, EW]])), 16)
                        gpsimd.wait_ge(sEB3.h, sEB3.n)
                        gpsimd.wait_ge(sLXN4[1].h, sLXN4[1].n)
                        gpsimd.wait_ge(sLXN4[3].h, sLXN4[3].n)
                        phase_end_q(gpsimd)

                    @blk.tensor
                    def _(tensor):
                        phase_start(tensor)
                        last = None
                        for tt in range(NQT):
                            for g in range(32):
                                gi = 32 * tt + g
                                tensor.wait_ge(sLXN4[gi % 4].h, 16 * (gi // 4 + 1))
                                if g == 0 and tt >= 1:
                                    tensor.wait_ge(vEVQ.h, tt)
                                for f in range(6):
                                    last = tensor.matmul(
                                        out=ps_qkv[:, 512 * f:512 * (f + 1)],
                                        lhsT=wqkv_sb[:, 768 * g + 128 * f:
                                                     768 * g + 128 * (f + 1)],
                                        rhs=xn_t2[:, 512 * (gi % 4):
                                                  512 * (gi % 4 + 1)],
                                        start=(g == 0), stop=(g == 31),
                                        skip_group_check=True)
                                tQK.inc(last)
                            tensor.wait_ge(vEVQ.h, tt + 1)
                            if tt >= 1:
                                tensor.wait_ge(vEVV.h, tt)
                            for j in range(4):
                                last = tensor.transpose(
                                    out=ps_vt[:, 128 * j:128 * (j + 1)],
                                    in_=v_stage[:, 128 * j:128 * (j + 1)],
                                    identity=ident[:, :])
                            tTRV.inc(last)
                        phase_end_c(tensor, tTRV)

                    @blk.scalar
                    def _(scalar):
                        phase_start(scalar)
                        phase_end_q(scalar)

                    @blk.vector
                    def _(vector):
                        phase_start(vector)
                        last = None
                        for tt in range(NQT):
                            vector.wait_ge(tQK.h, 32 * (tt + 1))
                            if tt >= 1:
                                vector.wait_ge(sWQS.h, 16 * 4 * tt)
                            for f in range(4):
                                vector.tensor_copy(
                                    out=q_stage[:, 512 * f:512 * (f + 1)],
                                    in_=ps_qkv[:, 512 * f:512 * (f + 1)])
                            vector.tensor_copy(
                                out=k_fm[:, 512 * tt:512 * (tt + 1)],
                                in_=ps_qkv[:, 512 * 4:512 * 5])
                            vEVQ.inc(vector.tensor_copy(
                                out=v_stage[:, :], in_=ps_qkv[:, 512 * 5:512 * 6]))
                            vector.wait_ge(tTRV.h, tt + 1)
                            last = vector.tensor_copy(
                                out=v_tm[:, 512 * tt:512 * (tt + 1)],
                                in_=ps_vt[:, :])
                            vEVV.inc(last)
                        phase_end_c(vector, vEVV)

                PH[0] += 1
                assert sPH.n == 5 * PH[0], sPH.n

                if True:
                    with ExitStack() as p3:
                        q_t = sbt(p3, "q_t", [128, 2 * 512], BF16, side="right")
                        p_sb = sbt(p3, "p_sb", [128, 4 * 512], BF16, side="right")
                        pm_sb = sbt(p3, "pm_sb", [128, 4 * 512], BF16, side="right")
                        o_raw = sbt(p3, "o_raw", [128, 2 * 512], F32, side="right")
                        o_stage = sbt(p3, "o_stage", [128, 2 * 512], BF16,
                                      side="right")
                        linv = sbt(p3, "linv", [1, 512], F32, side="right")
                        ps_sc = pst(p3, "ps_sc", [128, 4 * 512], F32)
                        ps_po = pst(p3, "ps_po", [128, 2 * 512], F32)
                        ps_pl = pst(p3, "ps_pl", [1, 512], F32)
                        ps_plb = pst(p3, "ps_plb", [128, 512], F32)

                        @blk.sync
                        def _(sync):
                            phase_start(sync)
                            def o_store(sidx):
                                hh, qq = divmod(sidx, NQT)
                                sync.wait_ge(vON.h, sidx + 1)
                                sWOSp[sidx % 2].inc(sync.dma_start(
                                    out=o_scr[128 * hh:128 * (hh + 1),
                                              512 * qq:512 * (qq + 1)],
                                    in_=o_stage[:, 512 * (sidx % 2):
                                                512 * (sidx % 2 + 1)]), 16)

                            for idx in range(64):
                                if idx >= 2:
                                    sync.wait_ge(tSC.h, PB_CUM[idx - 1])
                                h, qt = divmod(idx, NQT)
                                sLQp[idx % 2].inc(sync.dma_start(
                                    out=q_t[:, 512 * (idx % 2):512 * (idx % 2 + 1)],
                                    in_=q_scr[128 * h:128 * (h + 1),
                                              512 * qt:512 * (qt + 1)]), 16)
                                if idx >= 2:
                                    o_store(idx - 2)
                            o_store(62)
                            o_store(63)
                            for s_ in (sLQp[0], sLQp[1], sWOSp[0], sWOSp[1]):
                                sync.wait_ge(s_.h, s_.n)
                            phase_end_q(sync)

                        @blk.gpsimd
                        def _(gpsimd):
                            phase_start(gpsimd)
                            phase_end_q(gpsimd)

                        @blk.tensor
                        def _(tensor):
                            phase_start(tensor)
                            last = None
                            for idx in range(64):
                                h, qt = divmod(idx, NQT)
                                n = NKB[qt]
                                K0 = KB_CUM[idx]
                                bwait(tensor, sLQp, idx, 16)
                                if idx >= 1:
                                    tensor.wait_ge(aOE.h, idx - 1)
                                    tensor.wait_ge(vRC.h, idx)
                                qs = 512 * (idx % 2)

                                np_ = n // 2
                                P0_ = PB_CUM[idx]

                                def score_pair(p):
                                    pp = P0_ + p
                                    sp = pp % 2
                                    if pp >= 2:
                                        tensor.wait_ge(aEX.h, pp - 1)
                                    for u in range(2):
                                        j = 2 * p + u
                                        ins = tensor.matmul(
                                            out=ps_sc[:, 1024 * sp + 512 * u:
                                                      1024 * sp + 512 * (u + 1)],
                                            lhsT=k_fm[:, 128 * j:128 * (j + 1)],
                                            rhs=q_t[:, qs:qs + 512],
                                            start=True, stop=True,
                                            skip_group_check=True)
                                    tSC.inc(ins)

                                score_pair(0)
                                if np_ > 1:
                                    score_pair(1)
                                for p in range(np_):
                                    pp = P0_ + p
                                    sp = pp % 2
                                    if p + 2 < np_:
                                        score_pair(p + 2)
                                    tensor.wait_ge(vMK.h, pp + 1)
                                    for u in range(2):
                                        j = 2 * p + u
                                        kk = K0 + j
                                        tensor.matmul(
                                            out=ps_po[:, 512 * (idx % 2):
                                                      512 * (idx % 2 + 1)],
                                            lhsT=v_tm[:, 128 * j:128 * (j + 1)],
                                            rhs=pm_sb[:, 1024 * sp + 512 * u:
                                                      1024 * sp + 512 * (u + 1)],
                                            start=(j == 0), stop=(j == n - 1),
                                            skip_group_check=True)
                                        last = tensor.matmul(
                                            out=ps_pl[0:1, :],
                                            lhsT=ones_colb[:, :],
                                            rhs=pm_sb[:, 1024 * sp + 512 * u:
                                                      1024 * sp + 512 * (u + 1)],
                                            start=(j == 0), stop=(j == n - 1),
                                            skip_group_check=True)
                                        tPV.inc(last)
                                tensor.wait_ge(vRC.h, idx + 1)
                                if idx >= 1:
                                    tensor.wait_ge(vON.h, idx)
                                last = tensor.matmul(
                                    out=ps_plb[:, :],
                                    lhsT=ones_rowf[:, :], rhs=linv[0:1, :],
                                    start=True, stop=True,
                                    skip_group_check=True)
                                tLB.inc(last)
                            phase_end_c(tensor, tLB)

                        @blk.scalar
                        def _(scalar):
                            phase_start(scalar)
                            last = None
                            for idx in range(64):
                                h, qt = divmod(idx, NQT)
                                n = NKB[qt]
                                K0 = KB_CUM[idx]
                                np_ = n // 2
                                P0_ = PB_CUM[idx]
                                for p in range(np_):
                                    pp = P0_ + p
                                    sp = pp % 2
                                    scalar.wait_ge(tSC.h, pp + 1)
                                    if pp >= 2:
                                        scalar.wait_ge(vMK.h, pp - 1)
                                    aEX.inc(scalar.activation(
                                        out=p_sb[:, 1024 * sp:1024 * (sp + 1)],
                                        in_=ps_sc[:, 1024 * sp:1024 * (sp + 1)],
                                        func=AF.Exp, scale=SCALE,
                                        bias=bias_c[:, :]))
                                scalar.wait_ge(tPV.h, K0 + n)
                                if idx >= 2:
                                    scalar.wait_ge(vON.h, idx - 1)
                                last = scalar.activation(
                                    out=o_raw[:, 512 * (idx % 2):
                                              512 * (idx % 2 + 1)],
                                    in_=ps_po[:, 512 * (idx % 2):
                                              512 * (idx % 2 + 1)],
                                    func=AF.Copy)
                                aOE.inc(last)
                            phase_end_c(scalar, aOE)

                        @blk.vector
                        def _(vector):
                            phase_start(vector)
                            last = None
                            for idx in range(64):
                                h, qt = divmod(idx, NQT)
                                n = NKB[qt]
                                K0 = KB_CUM[idx]
                                np_ = n // 2
                                P0_ = PB_CUM[idx]
                                for p in range(np_):
                                    pp = P0_ + p
                                    sp = pp % 2
                                    vector.wait_ge(aEX.h, pp + 1)
                                    if pp >= 2:
                                        vector.wait_ge(tPV.h, 2 * (pp - 1))
                                    ins = None
                                    for u in range(2):
                                        j = 2 * p + u
                                        u0 = 512 * qt - 128 * j + 384
                                        ins = vector.tensor_mul(
                                            out=pm_sb[:, 1024 * sp + 512 * u:
                                                      1024 * sp + 512 * (u + 1)],
                                            in0=p_sb[:, 1024 * sp + 512 * u:
                                                     1024 * sp + 512 * (u + 1)],
                                            in1=e_sb[:, EW * h + u0:
                                                     EW * h + u0 + 512])
                                    vMK.inc(ins)
                                vector.wait_ge(tPV.h, K0 + n)
                                vector.wait_ge(tLB.h, idx)
                                vRC.inc(vector.reciprocal(out=linv[0:1, :],
                                                          in_=ps_pl[0:1, :]))
                                vector.wait_ge(tLB.h, idx + 1)
                                vector.wait_ge(aOE.h, idx + 1)
                                if idx >= 2:
                                    bwait(vector, sWOSp, idx - 2, 16)
                                last = vector.tensor_mul(
                                    out=o_stage[:, 512 * (idx % 2):
                                                512 * (idx % 2 + 1)],
                                    in0=o_raw[:, 512 * (idx % 2):
                                              512 * (idx % 2 + 1)],
                                    in1=ps_plb[:, :])
                                vON.inc(last)
                            phase_end_c(vector, vON)

                    PH[0] += 1
                    assert sPH.n == 5 * PH[0], sPH.n

                    with ExitStack() as p4:
                        wo_sb = sbt(p4, "wo_sb", [128, 2 * 4 * 512], BF16,
                                    side="right")
                        o_rc = sbt(p4, "o_rc", [128, 2 * 4 * 1024], BF16,
                                   side="right")
                        s_stage = sbt(p4, "s_stage", [128, 3 * 512], F32,
                                      side="right")
                        ps_s = pst(p4, "ps_s", [128, 3 * 512], F32)

                        @blk.gpsimd
                        def _(gpsimd):
                            phase_start(gpsimd)
                            phase_end_q(gpsimd)

                        @blk.sync
                        def _(sync):
                            phase_start(sync)
                            for rc in range(8):
                                if rc >= 2:
                                    sync.wait_ge(tS4.h, 64 * (rc - 1))
                                for h in range(4):
                                    sLORp[rc % 2].inc(sync.dma_start(
                                        out=o_rc[:, 4096 * (rc % 2) + 1024 * h:
                                                 4096 * (rc % 2)
                                                 + 1024 * (h + 1)],
                                        in_=o_scr[128 * h:128 * (h + 1),
                                                  1024 * rc:1024 * (rc + 1)]), 16)
                                for r1 in range(64):
                                    gtb = 64 * rc + r1
                                    dm8, tb = divmod(r1, 8)
                                    sync.wait_ge(aSE.h, gtb + 1)
                                    if gtb >= 3:
                                        sync.wait_ge(sWSPp[gtb % 3].h,
                                                     16 * ((gtb - 3) // 3 + 1))
                                    sWSPp[gtb % 3].inc(sync.dma_start(
                                        out=s_part[rc][dm8 // 4][
                                            128 * tb:128 * (tb + 1),
                                            512 * (dm8 % 4):
                                            512 * (dm8 % 4 + 1)],
                                        in_=s_stage[:, 512 * (gtb % 3):
                                                    512 * (gtb % 3 + 1)]), 16)
                            for s_ in (sLORp[0], sLORp[1], sWSPp[0], sWSPp[1],
                                       sWSPp[2]):
                                sync.wait_ge(s_.h, s_.n)
                            phase_end_q(sync)

                        @blk.tensor
                        def _(tensor):
                            phase_start(tensor)
                            last = None
                            for rc in range(8):
                                bwait(tensor, sLORp, rc, 64)
                                for dm8 in range(8):
                                    gi = 8 * rc + dm8
                                    bwait(tensor, sLWOp, gi, 64)
                                    for tb in range(8):
                                        gtb = 8 * gi + tb
                                        if gtb >= 3:
                                            tensor.wait_ge(aSE.h, gtb - 2)
                                        for h in range(4):
                                            last = tensor.matmul(
                                                out=ps_s[:, 512 * (gtb % 3):
                                                         512 * (gtb % 3 + 1)],
                                                lhsT=o_rc[:, 4096 * (rc % 2)
                                                          + 1024 * h + 128 * tb:
                                                          4096 * (rc % 2)
                                                          + 1024 * h
                                                          + 128 * (tb + 1)],
                                                rhs=wo_sb[:, 2048 * (gi % 2)
                                                          + 512 * h:
                                                          2048 * (gi % 2)
                                                          + 512 * (h + 1)],
                                                start=(h == 0), stop=(h == 3),
                                                skip_group_check=True)
                                        tS4.inc(last)
                            phase_end_c(tensor, tS4)

                        @blk.scalar
                        def _(scalar):
                            phase_start(scalar)

                            def wo_load(gi):
                                dm8 = gi % 8
                                if gi >= 2:
                                    scalar.wait_ge(tS4.h, 8 * (gi - 1))
                                for h in range(4):
                                    sLWOp[gi % 2].inc(scalar.dma_start(
                                        out=wo_sb[:, 2048 * (gi % 2) + 512 * h:
                                                  2048 * (gi % 2) + 512 * (h + 1)],
                                        in_=wo_bf[128 * h:128 * (h + 1),
                                                  512 * dm8:512 * (dm8 + 1)]), 16)

                            wo_load(0)
                            wo_load(1)
                            last = None
                            for gtb in range(512):
                                scalar.wait_ge(tS4.h, gtb + 1)
                                if gtb >= 3:
                                    scalar.wait_ge(sWSPp[gtb % 3].h,
                                                   16 * ((gtb - 3) // 3 + 1))
                                last = scalar.activation(
                                    out=s_stage[:, 512 * (gtb % 3):
                                                512 * (gtb % 3 + 1)],
                                    in_=ps_s[:, 512 * (gtb % 3):
                                             512 * (gtb % 3 + 1)],
                                    func=AF.Copy)
                                aSE.inc(last)
                                if gtb % 8 == 7 and gtb // 8 + 2 < 64:
                                    wo_load(gtb // 8 + 2)
                            scalar.wait_ge(sLWOp[0].h, sLWOp[0].n)
                            scalar.wait_ge(sLWOp[1].h, sLWOp[1].n)
                            phase_end_c(scalar, aSE)

                        @blk.vector
                        def _(vector):
                            phase_start(vector)
                            phase_end_q(vector)

                    PH[0] += 1
                    assert sPH.n == 5 * PH[0], sPH.n

        NP5 = 8
        IQ6 = [2, 2, 2, 2, 2, 1]
        IS6 = [0, 2, 4, 6, 8, 10]
        P5_SEQ = [(j, hf, IS6[j] + itl) for j in range(6)
                  for hf in range(2) for itl in range(IQ6[j])]
        assert len(P5_SEQ) == 22

        def p5_cons(gc):
            pc, jc = divmod(gc, 6)
            return 22 * pc + 2 * (IS6[jc] + IQ6[jc])

        with ExitStack() as p5:
            xn_f = sbt(p5, "xn_f", [128, 2 * 32 * 512], BF16, side="left")
            wgu_q = sbt(p5, "wgu_q", [128, 2 * 2 * 32 * 256], BF16, side="left")
            wd_sb = sbt(p5, "wd_sb", [128, 3 * IC * 512], BF16, side="left")
            h_sb = sbt(p5, "h_sb", [128, IC * 1024], BF16, side="left")
            sg_sb = sbt(p5, "sg_sb", [128, 2 * 512], BF16, side="left")
            su_sb = sbt(p5, "su_sb", [128, 2 * 512], BF16, side="left")
            f_stage = sbt(p5, "f_stage", [128, 2 * 512], F32, side="left")
            ps_g = pst(p5, "ps_g", [128, 2 * 512], F32)
            ps_u = pst(p5, "ps_u", [128, 2 * 512], F32)
            ps_f = pst(p5, "ps_f", [128, 2 * 512], F32)

            WQ2 = 256

            def gu_sl(buf, w, g, lo, width, nit):
                base = (2 * buf + w) * 32 * WQ2 + g * (128 * nit) + lo
                return wgu_q[:, base:base + width]

            @blk.sync
            def _(sync):
                phase_start(sync)

                def wg_half(gc):
                    if gc >= 6 * NP5:
                        return
                    j = gc % 6
                    nit = IQ6[j]
                    i0 = IS6[j]
                    if gc >= 2:
                        sync.wait_ge(tGU.h, p5_cons(gc - 2))
                    sLWGp[gc % 2].inc(sync.dma_start(
                        out=gu_sl(gc % 2, 0, 0, 0, 32 * 128 * nit, nit),
                        in_=bass.AP(wg_dq.tensor, 128 * i0,
                                    [[1408, 128], [128 * 1408, 32],
                                     [1, 128 * nit]])), 16)

                def xn_load(p):
                    if p >= NP5:
                        return
                    if p >= 1:
                        sync.wait_ge(tGU.h, 22 * p)
                    for hf in range(2):
                        sync.wait_ge(cAGs[hf].h, 1)
                        sLXF.inc(sync.dma_start(
                            out=xn_f[:, 32 * 512 * hf:32 * 512 * (hf + 1)],
                            in_=bass.AP(xn_g[hf].tensor, DM * p * 512,
                                        [[512, 128], [128 * 512, 32],
                                         [1, 512]])), 16)

                xn_load(0)
                wg_half(0)
                wg_half(1)
                for p in range(NP5):
                    wg_half(6 * p + 2)
                    wg_half(6 * p + 3)
                    wg_half(6 * p + 4)
                    wg_half(6 * p + 5)
                    wg_half(6 * p + 6)
                    wg_half(6 * p + 7)
                    xn_load(p + 1)
                    for ftl in range(64):
                        q8, tb2 = divmod(ftl, 8)
                        ftb = 64 * p + ftl
                        sync.wait_ge(vFE.h, ftb + 1)
                        sWFPp[ftb % 2].inc(sync.dma_start(
                            out=f_part[p][q8 // 4][
                                128 * tb2:128 * (tb2 + 1),
                                512 * (q8 % 4):512 * (q8 % 4 + 1)],
                            in_=f_stage[:, 512 * (ftb % 2):
                                        512 * (ftb % 2 + 1)]), 16)
                for s_ in (sWFPp[0], sWFPp[1], sLWGp[0], sLWGp[1], sLXF):
                    sync.wait_ge(s_.h, s_.n)
                phase_end_q(sync)

            @blk.gpsimd
            def _(gpsimd):
                phase_start(gpsimd)
                for p in range(NP5):
                    for _p, _sw in enumerate(sWSPp):
                        _cnt = sum(1 for _g in range(64 * (p + 1))
                                   if _g % 3 == _p)
                        gpsimd.wait_ge(_sw.h, 16 * _cnt)
                    cS[p].inc(gpsimd.collective_compute(
                        "ReduceScatter", ALU.add, replica_groups=groups,
                        ins=[s_part[p][0].opt()], outs=[s_ownr[p][0].opt()]))
                    if p >= 1:
                        gpsimd.wait_ge(sWFPp[0].h, 16 * 32 * p)
                        gpsimd.wait_ge(sWFPp[1].h, 16 * 32 * p)
                        gpsimd.wait_ge(tGU.h, 22 * p + 6)
                        cF[p - 1].inc(gpsimd.collective_compute(
                            "ReduceScatter", ALU.add, replica_groups=groups,
                            ins=[f_part[p - 1][0].opt()],
                            outs=[f_ownr[p - 1][0].opt()]))
                    gpsimd.wait_ge(tGU.h, 22 * p + 11)
                    cS[p].inc(gpsimd.collective_compute(
                        "ReduceScatter", ALU.add, replica_groups=groups,
                        ins=[s_part[p][1].opt()], outs=[s_ownr[p][1].opt()]))
                    if p >= 1:
                        gpsimd.wait_ge(tGU.h, 22 * p + 16)
                        cF[p - 1].inc(gpsimd.collective_compute(
                            "ReduceScatter", ALU.add, replica_groups=groups,
                            ins=[f_part[p - 1][1].opt()],
                            outs=[f_ownr[p - 1][1].opt()]))
                gpsimd.wait_ge(sWFPp[0].h, 16 * 32 * 8)
                gpsimd.wait_ge(sWFPp[1].h, 16 * 32 * 8)
                for q in range(2):
                    cF[7].inc(gpsimd.collective_compute(
                        "ReduceScatter", ALU.add, replica_groups=groups,
                        ins=[f_part[7][q].opt()], outs=[f_ownr[7][q].opt()]))
                phase_end_q(gpsimd)

            @blk.tensor
            def _(tensor):
                phase_start(tensor)
                last = None
                for p in range(NP5):
                    tensor.wait_ge(sLXF.h, 16 * 2 * (p + 1))
                    gidx = 22 * p
                    for j in range(6):
                        gc = 6 * p + j
                        nit = IQ6[j]
                        i0 = IS6[j]
                        bwait(tensor, sLWGp, gc, 16 * 2)
                        for hf in range(2):
                            for itl in range(nit):
                                if gidx >= 2:
                                    tensor.wait_ge(vH.h, gidx - 1)
                                for g in range(32):
                                    tensor.matmul(
                                        out=ps_g[:, 512 * (gidx % 2):
                                                 512 * (gidx % 2 + 1)],
                                        lhsT=gu_sl(gc % 2, 0, g,
                                                   128 * itl, 128, nit),
                                        rhs=xn_f[:, 32 * 512 * hf + 512 * g:
                                                 32 * 512 * hf + 512 * (g + 1)],
                                        start=(g == 0), stop=(g == 31),
                                        skip_group_check=True)
                                    last = tensor.matmul(
                                        out=ps_u[:, 512 * (gidx % 2):
                                                 512 * (gidx % 2 + 1)],
                                        lhsT=gu_sl(gc % 2, 1, g,
                                                   128 * itl, 128, nit),
                                        rhs=xn_f[:, 32 * 512 * hf + 512 * g:
                                                 32 * 512 * hf + 512 * (g + 1)],
                                        start=(g == 0), stop=(g == 31),
                                        skip_group_check=True)
                                tGU.inc(last)
                                gidx += 1
                    tensor.wait_ge(vH.h, 22 * (p + 1))
                    for q8 in range(8):
                        gq8 = 8 * p + q8
                        tensor.wait_ge(sLWDp[gq8 % 3].h, 16 * (gq8 // 3 + 1))
                        for tb2 in range(8):
                            ftb = 64 * p + 8 * q8 + tb2
                            if ftb >= 2:
                                tensor.wait_ge(vFE.h, ftb - 1)
                            for it in range(IC):
                                last = tensor.matmul(
                                    out=ps_f[:, 512 * (ftb % 2):
                                             512 * (ftb % 2 + 1)],
                                    lhsT=h_sb[:, 1024 * it + 128 * tb2:
                                              1024 * it + 128 * (tb2 + 1)],
                                    rhs=wd_sb[:, IC * 512 * (gq8 % 3)
                                              + 512 * it:
                                              IC * 512 * (gq8 % 3)
                                              + 512 * (it + 1)],
                                    start=(it == 0), stop=(it == IC - 1),
                                    skip_group_check=True)
                            tDN.inc(last)
                phase_end_c(tensor, tDN)

            @blk.scalar
            def _(scalar):
                phase_start(scalar)
                scalar.preload_act_table(AF.Silu)

                def wu_half(gc):
                    if gc >= 6 * NP5:
                        return
                    j = gc % 6
                    nit = IQ6[j]
                    i0 = IS6[j]
                    if gc >= 2:
                        scalar.wait_ge(tGU.h, p5_cons(gc - 2))
                    sLWGp[gc % 2].inc(scalar.dma_start(
                        out=gu_sl(gc % 2, 1, 0, 0, 32 * 128 * nit, nit),
                        in_=bass.AP(wu_dq.tensor, 128 * i0,
                                    [[1408, 128], [128 * 1408, 32],
                                     [1, 128 * nit]])), 16)

                def wd_load(gq8):
                    if gq8 >= 16 * NP5 // 2:
                        return
                    if gq8 >= 3:
                        scalar.wait_ge(tDN.h, 8 * (gq8 - 2))
                    sLWDp[gq8 % 3].inc(scalar.dma_start(
                        out=wd_sb[:, IC * 512 * (gq8 % 3):
                                  IC * 512 * (gq8 % 3 + 1)],
                        in_=bass.AP(wd_dq.tensor, 512 * (gq8 % 8),
                                    [[DM, 128], [128 * DM, IC],
                                     [1, 512]])), 16)

                wu_half(0)
                wu_half(1)
                last = None
                for p in range(NP5):
                    for w_, (j, hf, it) in enumerate(P5_SEQ):
                        gidx = 22 * p + w_
                        scalar.wait_ge(tGU.h, gidx + 1)
                        if gidx >= 2:
                            scalar.wait_ge(vH.h, gidx - 1)
                        scalar.activation(
                            out=sg_sb[:, 512 * (gidx % 2):
                                      512 * (gidx % 2 + 1)],
                            in_=ps_g[:, 512 * (gidx % 2):
                                     512 * (gidx % 2 + 1)],
                            func=AF.Silu)
                        last = scalar.activation(
                            out=su_sb[:, 512 * (gidx % 2):
                                      512 * (gidx % 2 + 1)],
                            in_=ps_u[:, 512 * (gidx % 2):
                                     512 * (gidx % 2 + 1)],
                            func=AF.Copy)
                        aGU.inc(last)
                        if w_ == 0:
                            wd_load(8 * p)
                            wd_load(8 * p + 1)
                        elif w_ == 3:
                            wu_half(6 * p + 2)
                        elif w_ == 7:
                            wu_half(6 * p + 3)
                        elif w_ == 11:
                            wu_half(6 * p + 4)
                        elif w_ == 15:
                            wu_half(6 * p + 5)
                        elif w_ == 19:
                            wu_half(6 * p + 6)
                        elif w_ == 21:
                            wu_half(6 * p + 7)
                            for q8 in range(2, 8):
                                wd_load(8 * p + q8)
                phase_end_c(scalar, aGU)

            @blk.vector
            def _(vector):
                phase_start(vector)
                last = None
                for p in range(NP5):
                    for w_, (j, hf, it) in enumerate(P5_SEQ):
                        gidx = 22 * p + w_
                        vector.wait_ge(aGU.h, gidx + 1)
                        vH.inc(vector.tensor_mul(
                            out=h_sb[:, 1024 * it + 512 * hf:
                                     1024 * it + 512 * (hf + 1)],
                            in0=sg_sb[:, 512 * (gidx % 2):
                                      512 * (gidx % 2 + 1)],
                            in1=su_sb[:, 512 * (gidx % 2):
                                      512 * (gidx % 2 + 1)]))
                    for ftl in range(64):
                        ftb = 64 * p + ftl
                        vector.wait_ge(tDN.h, ftb + 1)
                        if ftb >= 2:
                            bwait(vector, sWFPp, ftb - 2, 16)
                        last = vector.tensor_copy(
                            out=f_stage[:, 512 * (ftb % 2):
                                        512 * (ftb % 2 + 1)],
                            in_=ps_f[:, 512 * (ftb % 2):
                                     512 * (ftb % 2 + 1)])
                        vFE.inc(last)
                phase_end_c(vector, vFE)

        PH[0] += 1
        assert sPH.n == 5 * PH[0], sPH.n

        with ExitStack() as p6:
            s_t6 = sbt(p6, "s_t6", [128, 2 * DM], F32, side="left")
            f_t6 = sbt(p6, "f_t6", [128, 2 * DM], F32, side="left")
            x_t6 = sbt(p6, "x_t6", [128, 2 * DM], F32, side="left")
            g_t6 = sbt(p6, "g_t6", [128, 2 * DM], F32, side="left")
            sg_t6 = sbt(p6, "sg_t6", [128, 2 * DM], F32, side="left")
            out_t6 = sbt(p6, "out_t6", [128, 2 * DM], F32, side="left")

            @blk.sync
            def _(sync):
                phase_start(sync)
                def st6(rc):
                    b = DM * (rc % 2)
                    sync.wait_ge(vOUT.h, rc + 1)
                    sWOUTp[rc % 2].inc(sync.dma_start(
                        out=out_own[128 * rc:128 * (rc + 1), :],
                        in_=out_t6[:, b:b + DM]), 16)

                for rc in range(8):
                    b = DM * (rc % 2)
                    sync.wait_ge(cS[rc].h, 2)
                    sync.wait_ge(cF[rc].h, 2)
                    if rc >= 2:
                        sync.wait_ge(vOUT.h, rc - 1)
                    for q in range(2):
                        sLSFp[rc % 2].inc(sync.dma_start(
                            out=s_t6[:, b + 2048 * q:b + 2048 * (q + 1)],
                            in_=s_ownr[rc][q]), 16)
                        sLSFp[rc % 2].inc(sync.dma_start(
                            out=f_t6[:, b + 2048 * q:b + 2048 * (q + 1)],
                            in_=f_ownr[rc][q]), 16)
                    sLSFp[rc % 2].inc(sync.dma_start(
                        out=x_t6[:, b:b + DM],
                        in_=x_sc[128 * rc:128 * (rc + 1), :]), 16)
                    if rc >= 2:
                        st6(rc - 2)
                st6(6)
                st6(7)
                for name, dst in dbg_outs.items():
                    sWOUTp[0].inc(sync.dma_start(out=dst, in_=dbg_srcs[name]), 16)
                sync.wait_ge(sWOUTp[0].h, sWOUTp[0].n)
                sync.wait_ge(sWOUTp[1].h, sWOUTp[1].n)

            @blk.gpsimd
            def _(gpsimd):
                phase_start(gpsimd)

            @blk.tensor
            def _(tensor):
                phase_start(tensor)

            @blk.scalar
            def _(scalar):
                phase_start(scalar)
                for rc in range(8):
                    b = DM * (rc % 2)
                    scalar.wait_ge(sLSFp[rc % 2].h, 80 * (rc // 2 + 1))
                    if rc >= 2:
                        scalar.wait_ge(vOUT.h, rc - 1)
                    aG6.inc(scalar.activation(out=g_t6[:, b:b + DM],
                                              in_=f_t6[:, b:b + DM],
                                              func=AF.Silu))

            @blk.vector
            def _(vector):
                phase_start(vector)
                for rc in range(8):
                    b = DM * (rc % 2)
                    vector.wait_ge(aG6.h, rc + 1)
                    vSELF.inc(vector.tensor_mul(out=sg_t6[:, b:b + DM],
                                                in0=s_t6[:, b:b + DM],
                                                in1=g_t6[:, b:b + DM]))
                    vector.wait_ge(vSELF.h, vSELF.n)
                    if rc >= 2:
                        vector.wait_ge(sWOUTp[rc % 2].h, 16 * (rc // 2))
                    vOUT.inc(vector.tensor_add(out=out_t6[:, b:b + DM],
                                               in0=sg_t6[:, b:b + DM],
                                               in1=x_t6[:, b:b + DM]))

    return nc


def _alibi_slopes(n):
    cp2 = 2 ** math.floor(math.log2(n))
    base = 2.0 ** (-8.0 / cp2)
    slopes = base ** np.arange(1, cp2 + 1, dtype=np.float32)
    if cp2 != n:
        extra_base = 2.0 ** (-4.0 / cp2)
        extra = extra_base ** np.arange(1, 1 + 2 * (n - cp2), 2, dtype=np.float32)
        slopes = np.concatenate([slopes, extra])
    return slopes.astype(np.float32)


def make_in_maps(x_full, w_qkv, w_o, gqw, gsc, gqz, uqw, usc, uqz, dqw, dsc, dqz):
    x = np.ascontiguousarray(np.asarray(x_full, dtype=np.float32).reshape(S, DM))
    sl = _alibi_slopes(32)
    x4 = x.reshape(8, 8, 128, DM)
    in_maps = []
    for c in range(NCORE):
        g0, gn = G0S[c], GROUPS[c]
        m = {}
        m["x_own"] = np.ascontiguousarray(x[1024 * c:1024 * (c + 1)])
        m["x_sc"] = np.ascontiguousarray(x4[:, c].reshape(1024, DM))
        m["wqkv"] = np.ascontiguousarray(np.concatenate(
            [w_qkv[:, 512 * c:512 * (c + 1)],
             w_qkv[:, 4096 + 128 * c:4096 + 128 * (c + 1)],
             w_qkv[:, 5120 + 128 * c:5120 + 128 * (c + 1)]], axis=1))
        m["wo"] = np.ascontiguousarray(w_o[512 * c:512 * (c + 1)])
        m["slopes"] = np.ascontiguousarray(sl[4 * c:4 * (c + 1)].reshape(1, 4))
        gw = np.zeros((DM, 176), np.int32)
        gw[:, :16 * gn] = gqw[:, 16 * g0:16 * (g0 + gn)]
        m["g_qw"] = gw
        gz = np.zeros((32, 176), np.int32)
        gz[:, :16 * gn] = gqz[:, 16 * g0:16 * (g0 + gn)]
        m["g_qz"] = gz
        gs = np.zeros((32, 1408), np.float32)
        gs[:, :128 * gn] = gsc[:, 128 * g0:128 * (g0 + gn)]
        m["g_sc"] = gs
        uw = np.zeros((DM, 176), np.int32)
        uw[:, :16 * gn] = uqw[:, 16 * g0:16 * (g0 + gn)]
        m["u_qw"] = uw
        uz = np.zeros((32, 176), np.int32)
        uz[:, :16 * gn] = uqz[:, 16 * g0:16 * (g0 + gn)]
        m["u_qz"] = uz
        us = np.zeros((32, 1408), np.float32)
        us[:, :128 * gn] = usc[:, 128 * g0:128 * (g0 + gn)]
        m["u_sc"] = us
        dw = np.zeros((1408, 512), np.int32)
        dw[:128 * gn] = dqw[128 * g0:128 * (g0 + gn)]
        m["d_qw"] = dw
        dz = np.zeros((IC, 512), np.int32)
        dz[:gn] = dqz[g0:g0 + gn]
        m["d_qz"] = dz
        ds = np.zeros((IC, DM), np.float32)
        ds[:gn] = dsc[g0:g0 + gn]
        m["d_sc"] = ds
        in_maps.append(m)
    return in_maps


def kernel(L0_attn_hidden_states, L0_attn_w_qkv, L0_attn_w_o,
           L0_ffn_gate_qweight, L0_ffn_gate_scales, L0_ffn_gate_qzeros,
           L0_ffn_up_qweight, L0_ffn_up_scales, L0_ffn_up_qzeros,
           L0_ffn_down_qweight, L0_ffn_down_scales, L0_ffn_down_qzeros):
    in_maps = make_in_maps(
        np.asarray(L0_attn_hidden_states, dtype=np.float32),
        np.asarray(L0_attn_w_qkv, dtype=np.float32),
        np.asarray(L0_attn_w_o, dtype=np.float32),
        np.asarray(L0_ffn_gate_qweight, dtype=np.int32),
        np.asarray(L0_ffn_gate_scales, dtype=np.float32),
        np.asarray(L0_ffn_gate_qzeros, dtype=np.int32),
        np.asarray(L0_ffn_up_qweight, dtype=np.int32),
        np.asarray(L0_ffn_up_scales, dtype=np.float32),
        np.asarray(L0_ffn_up_qzeros, dtype=np.int32),
        np.asarray(L0_ffn_down_qweight, dtype=np.int32),
        np.asarray(L0_ffn_down_scales, dtype=np.float32),
        np.asarray(L0_ffn_down_qzeros, dtype=np.int32))

    nc = build_program()
    res = run_bass_kernel_spmd(nc, in_maps, core_ids=list(range(NCORE)))

    out = np.empty((8, 8, 128, DM), dtype=np.float32)
    for c in range(NCORE):
        out[:, c] = res.results[c]["out_own"].reshape(8, 128, DM)
    return out.reshape(1, S, DM)
